# version 07
# baseline (speedup 1.0000x reference)
import contextlib
import numpy as np
import ml_dtypes

import concourse.bass as bass
import concourse.mybir as mybir
from concourse.bass_utils import run_bass_kernel_spmd

F32 = mybir.dt.float32
BF16 = mybir.dt.bfloat16
FP8 = mybir.dt.float8e4
AF = mybir.ActivationFunctionType
ALU = mybir.AluOpType
AX = mybir.AxisListType

S, D = 2048, 1024
LN64 = float(np.log(64.0))
EPS = 1e-5
GSC = 4.0 / (448.0 * 448.0)
DEBUG = False
BCUT = 4
B2SUB = 3


class _DummyInstr:
    def then_inc(self, *a, **k):
        return self


class _DummyEngine:
    def __getattr__(self, name):
        def f(*a, **k):
            return _DummyInstr()
        return f


class Ctx:
    def __init__(self):
        self.emit = False
        self.cnt = {}
        self.cp = {}
        self.sems = {}

    def inc(self, instr, sem, n=1):
        self.cnt[sem] = self.cnt.get(sem, 0) + n
        if self.emit:
            instr.then_inc(self.sems[sem], n)
        return instr

    def mark(self, name, sem):
        cur = (sem, self.cnt.get(sem, 0))
        if not self.emit:
            assert name not in self.cp, f"dup checkpoint {name}"
            self.cp[name] = cur
        else:
            assert self.cp.get(name) == cur, (name, self.cp.get(name), cur)

    def wait(self, eng, name):
        if self.emit and not isinstance(eng, _DummyEngine):
            sem, val = self.cp[name]
            if val > 0:
                eng.wait_ge(self.sems[sem], val)


def build_nc(n_phases=4):
    nc = bass.Bass("TRN2", target_bir_lowering=False, debug=False)

    x_in = nc.dram_tensor("x_in", [D, S], F32, kind="ExternalInput")
    wqb = nc.dram_tensor("wqb", [8, 128, 1024], F32, kind="ExternalInput")
    wkb = nc.dram_tensor("wkb", [8, 128, 1024], F32, kind="ExternalInput")
    wvb = nc.dram_tensor("wvb", [8, 128, 1024], F32, kind="ExternalInput")
    wob = nc.dram_tensor("wob", [8, 128, 1024], F32, kind="ExternalInput")
    wfb = nc.dram_tensor("wfb", [8, 128, 1024], F32, kind="ExternalInput")
    knbd = nc.dram_tensor("knbd", [32, 128, 1024], F32, kind="ExternalInput")
    vnbd = nc.dram_tensor("vnbd", [32, 128, 1024], F32, kind="ExternalInput")
    bias_in = nc.dram_tensor("bias_in", [128, 8], F32, kind="ExternalInput")
    mask_in = nc.dram_tensor("mask_in", [128, 384], BF16, kind="ExternalInput")
    id_in = nc.dram_tensor("id_in", [128, 128], BF16, kind="ExternalInput")
    ones_in = nc.dram_tensor("ones_in", [128, 1], BF16, kind="ExternalInput")
    idf_in = nc.dram_tensor("idf_in", [128, 128], F32, kind="ExternalInput")
    onesr_in = nc.dram_tensor("onesr_in", [1, 128], F32, kind="ExternalInput")
    out_t = nc.dram_tensor("out", [D, S], F32, kind="ExternalOutput")
    if DEBUG:
        dbg1 = nc.dram_tensor("dbg1", [1, 16 + 3 * S], F32, kind="ExternalOutput")
        dbgs = nc.dram_tensor("dbgs", [128, 4], F32, kind="ExternalOutput")
        dbga = nc.dram_tensor("dbga", [128, 1], F32, kind="ExternalOutput")
        dbgm = nc.dram_tensor("dbgm", [128, 8], F32, kind="ExternalOutput")
        dbgr = nc.dram_tensor("dbgr", [128, 8], F32, kind="ExternalOutput")

    rb_h = nc.dram_tensor("rbounce", [1, S], F32)
    cb_h = nc.dram_tensor("cbounce", [128, 1], F32)
    sb_h = nc.dram_tensor("sbounce", [1, 4], F32)
    cc_in = [nc.dram_tensor(f"cc_in{i}", [128, 128], F32) for i in range(2)]
    cc_out = [nc.dram_tensor(f"cc_out{i}", [128, 128], F32) for i in range(2)]

    ctx = Ctx()
    es = contextlib.ExitStack()
    sbuf = lambda name, shape, dt: es.enter_context(nc.sbuf_tensor(name, shape, dt))

    with es:
        xt = [sbuf(f"xt{c}", [128, S], F32) for c in range(8)]
        maskt = sbuf("maskt", [128, 384], BF16)
        ident = sbuf("ident", [128, 128], BF16)
        onescol = sbuf("onescol", [128, 1], BF16)
        identf = sbuf("identf", [128, 128], F32)
        onesr = sbuf("onesr", [1, 128], F32)
        bias_sb = sbuf("bias_sb", [128, 8], F32)
        rows = sbuf("rows", [1, 3 * S], F32)
        m_row = rows[0:1, 0:S]
        r_row = rows[0:1, S:2 * S]
        t_row = rows[0:1, 2 * S:3 * S]
        smalls = sbuf("smalls", [1, 16], F32)
        scol = sbuf("scol", [128, 4], F32)
        amax_col = sbuf("amax_col", [128, 1], F32)
        zc = sbuf("zc", [128, 128], F32)
        aux = sbuf("aux", [128, 2 * S], F32)
        mB = aux[:, 0:S]
        rB = aux[:, S:2 * S]
        cB = aux[:, S:S + 512]
        gtf = [aux[:, S + 512 + i * 512:S + 512 + (i + 1) * 512] for i in range(2)]
        sqb2 = sbuf("sqb2", [128, 1024], BF16)
        sqb = [sqb2[:, 0:512], sqb2[:, 512:1024]]
        xbc = sbuf("xbc", [128, 1024], BF16)
        xbcs = [xbc[:, 0:512], xbc[:, 512:1024]]
        xba = sbuf("xba", [128, 4096], BF16)
        xbas = [xba[:, i * 512:(i + 1) * 512] for i in range(8)]
        wld = [sbuf(f"wld{i}", [128, 1024], F32) for i in range(4)]
        wbf = [sbuf(f"wbf{i}", [128, 1024], BF16) for i in range(3)]
        vTt = sbuf("vTt", [128, 512], BF16)
        vaug = [sbuf(f"vaug{t}", [128, 130], BF16) for t in range(16)]
        rcol = sbuf("rcol", [128, 2], F32)

        arenaB = sbuf("arenaB", [128, 12288], BF16)
        qTt = arenaB[:, 0:2048]
        kTt = arenaB[:, 2048:4096]
        ATt = [[arenaB[:, 4096 + (h * 4 + i) * 384: 4096 + (h * 4 + i + 1) * 384]
                for i in range(4)] for h in range(2)]
        aoTs = [arenaB[:, i * 512:(i + 1) * 512] for i in range(8)]
        wobf = [arenaB[:, 4096 + i * 1024:4096 + (i + 1) * 1024] for i in range(8)]
        arenaB8 = arenaB.bitcast(FP8)
        xq8 = [arenaB8[:, i * S:(i + 1) * S] for i in range(8)]
        wq8 = [arenaB8[:, 16384 + i * 1024:16384 + (i + 1) * 1024] for i in range(8)]
        arenaBf = arenaB.bitcast(F32)
        xbt_b = [arenaB[:, i * 512:(i + 1) * 512] for i in range(8)]

        big = sbuf("big", [128, 16384], BF16)
        att_o = [big[:, i * 1024:(i + 1) * 1024] for i in range(16)]
        gt = [big[:, i * 512:(i + 1) * 512] for i in range(32)]
        bigf = big.bitcast(F32)
        xfs = [bigf[:, i * S:(i + 1) * S] for i in range(4)]

        po = [es.enter_context(nc.psum_tensor(f"po{i}", [128, 512], F32)) for i in range(8)]
        poa = [p.ap() for p in po]
        pob = [p.ap().bitcast(BF16) for p in po]

        for name in ["sy", "sy0", "sy1", "sy2", "sy3", "sq", "pe", "ve", "se", "ge", "pl",
                     "c0a", "c0b", "c0c", "c1a", "c1b", "c1c"]:
            ctx.sems[name] = es.enter_context(nc.semaphore(name))

        def mm(eng, out, lhsT, rhs, start, stop):
            return eng.matmul(out, lhsT, rhs, start=start, stop=stop, skip_group_check=True)

        PH = ["attn_a", "ffn1", "attn_b", "ffn2"][:n_phases]
        wj = [0]

        def preload(E, eng):
            if E == "sync":
                for c in range(8):
                    ctx.inc(eng.dma_start(out=xt[c][:, :], in_=x_in.ap()[128 * c:128 * (c + 1), :]), "pl", 16)
                ctx.inc(eng.dma_start(out=maskt[:, :], in_=mask_in.ap()), "pl", 16)
                ctx.inc(eng.dma_start(out=ident[:, :], in_=id_in.ap()), "pl", 16)
                ctx.inc(eng.dma_start(out=onescol[:, :], in_=ones_in.ap()), "pl", 16)
                ctx.inc(eng.dma_start(out=bias_sb[:, :], in_=bias_in.ap()), "pl", 16)
                ctx.inc(eng.dma_start(out=identf[:, :], in_=idf_in.ap()), "pl", 16)
                ctx.inc(eng.dma_start(out=onesr[:, :], in_=onesr_in.ap()), "pl", 16)
                ctx.mark("ld0", "pl")
            if E == "vector":
                ctx.wait(eng, "ld0")
                for t in range(16):
                    eng.memset(vaug[t][:, 64:65], 1.0)
                    eng.memset(vaug[t][:, 129:130], 1.0)
                ctx.inc(eng.memset(zc[:, :], 0.0), "ve")
                ctx.mark("init_ve", "ve")

        def ln_stats_tc(E, eng, tag, tc, ready):
            sl = slice(512 * tc, 512 * (tc + 1))
            sq4 = sqb + [xbc[:, 0:512], xbc[:, 512:1024]]
            if E == "vector":
                ctx.wait(eng, ready)
                if tc > 0:
                    ctx.wait(eng, f"{tag}.smm{(tc - 1) * 8 + 7}")
                for c in range(8):
                    ctx.inc(eng.tensor_copy(out=xbas[c][:, :], in_=xt[c][:, sl]), "ve")
                ctx.mark(f"{tag}.xc{tc}", "ve")
            for c in range(8):
                j = tc * 8 + c
                if E == "scalar":
                    if c == 0:
                        ctx.wait(eng, ready)
                    if j >= 4:
                        ctx.wait(eng, f"{tag}.smm{j - 4}")
                    ctx.inc(eng.activation(out=sq4[c % 4][:, :], in_=xt[c][:, sl], func=AF.Square), "se")
                    ctx.mark(f"{tag}.sq{j}", "se")
                if E == "tensor":
                    if c == 0:
                        ctx.wait(eng, f"{tag}.xc{tc}")
                        if tc > 0:
                            ctx.wait(eng, f"{tag}.sev{tc - 1}")
                    ctx.wait(eng, f"{tag}.sq{j}")
                    mm(eng, poa[4][0:1, :], onescol[:, :], xbas[c][:, :], c == 0, c == 7)
                    ctx.inc(mm(eng, poa[5][0:1, :], onescol[:, :], sq4[c % 4][:, :], c == 0, c == 7), "pe")
                    ctx.mark(f"{tag}.smm{j}", "pe")
            if E == "vector":
                ctx.wait(eng, f"{tag}.smm{tc * 8 + 7}")
                eng.tensor_scalar(out=m_row[:, sl], in0=poa[4][0:1, :], scalar1=1.0 / D, scalar2=None, op0=ALU.mult)
                ctx.inc(eng.tensor_scalar(out=t_row[:, sl], in0=poa[5][0:1, :], scalar1=1.0 / D, scalar2=None, op0=ALU.mult), "ve")
                ctx.mark(f"{tag}.sev{tc}", "ve")

        def ln_stats_fin(E, eng, tag, want_mB, want_rB):
            if E == "vector":
                ctx.wait(eng, f"{tag}.sev3")
                ctx.inc(eng.tensor_tensor(out=r_row[:, :], in0=m_row[:, :], in1=m_row[:, :], op=ALU.mult), "ve")
                ctx.mark(f"{tag}.vmm", "ve")
                ctx.wait(eng, f"{tag}.vmm")
                ctx.inc(eng.scalar_tensor_tensor(out=r_row[:, :], in0=t_row[:, :], scalar=EPS,
                                                 in1=r_row[:, :], op0=ALU.add, op1=ALU.subtract), "ve")
                ctx.mark(f"{tag}.var", "ve")
            if E == "scalar":
                ctx.wait(eng, f"{tag}.var")
                ctx.inc(eng.activation(out=r_row[:, :], in_=r_row[:, :], func=AF.Ln), "se")
                ctx.mark(f"{tag}.rln", "se")
                ctx.wait(eng, f"{tag}.rln")
                ctx.inc(eng.activation(out=r_row[:, :], in_=r_row[:, :], func=AF.Exp, scale=-0.5), "se")
                ctx.mark(f"{tag}.r", "se")
                if want_mB:
                    ctx.wait(eng, f"{tag}.sev3")
                    ctx.inc(eng.dma_start(out=rb_h.ap(), in_=m_row[:, :]), "sq", 16)
                    ctx.mark(f"{tag}.mbo", "sq")
                    ctx.wait(eng, f"{tag}.mbo")
                    ctx.inc(eng.dma_start(out=mB[:, :], in_=bass.AP(rb_h, 0, [[0, 128], [1, S]])), "sq", 16)
                    ctx.mark(f"{tag}.mB", "sq")
                if want_rB:
                    ctx.wait(eng, f"{tag}.r")
                    if want_mB:
                        ctx.wait(eng, f"{tag}.mB")
                    ctx.inc(eng.dma_start(out=rb_h.ap(), in_=r_row[:, :]), "sq", 16)
                    ctx.mark(f"{tag}.rbo", "sq")
                    ctx.wait(eng, f"{tag}.rbo")
                    ctx.inc(eng.dma_start(out=rB[:, :], in_=bass.AP(rb_h, 0, [[0, 128], [1, S]])), "sq", 16)
                    ctx.mark(f"{tag}.rB", "sq")

        def ln_stats(E, eng, tag, want_mbf, want_mB, want_rB, pe_pre=None, se_pre=None):
            for tc in range(4):
                ln_stats_tc(E, eng, tag, tc, se_pre or "ld0")
            ln_stats_fin(E, eng, tag, want_mB, want_rB)

        def attn_a(E, eng):
            tag = "a1"
            ln_stats(E, eng, tag, want_mbf=False, want_mB=True, want_rB=True)
            def wcasts(eng, hh):
                for wi in range(3):
                    j = wj[0] + hh * 3 + wi
                    ctx.wait(eng, f"w{j}.ld")
                    if hh > 0:
                        ctx.wait(eng, f"{tag}.pfold{hh - 1}_3")
                    ctx.inc(eng.tensor_copy(out=wbf[wi][:, :], in_=wld[j % 4][:, :]), "ve")
                    ctx.mark(f"w{j}.cast", "ve")
                    ctx.mark(f"w{j}.free", "ve")

            def xba0(eng, hh):
                if hh == 0:
                    ctx.wait(eng, f"{tag}.mB")
                else:
                    ctx.wait(eng, f"{tag}.pfold{hh - 1}_3")
                for c in range(8):
                    ctx.inc(eng.tensor_tensor(out=xbas[c][:, :], in0=xt[c][:, 0:512], in1=mB[:, 0:512], op=ALU.subtract), "ve")
                ctx.mark(f"{tag}.pc{hh}_0", "ve")

            for hh in range(8):
                for wi, wsrc in enumerate((wqb, wkb, wvb)):
                    j = wj[0] + hh * 3 + wi
                    if E == "sync":
                        if j >= 4:
                            ctx.wait(eng, f"w{j - 4}.free")
                        ctx.inc(eng.dma_start(out=wld[j % 4][:, :], in_=wsrc.ap()[hh]), f"sy{j % 4}", 16)
                        ctx.mark(f"w{j}.ld", f"sy{j % 4}")
                if E == "vector" and hh == 0:
                    wcasts(eng, 0)
                    xba0(eng, 0)

                for tc in range(4):
                    sl = slice(512 * tc, 512 * (tc + 1))
                    if E == "vector" and tc > 0:
                        ctx.wait(eng, f"{tag}.pfold{hh}_{tc - 1}")
                        for c in range(8):
                            ctx.inc(eng.tensor_tensor(out=xbas[c][:, :], in0=xt[c][:, sl], in1=mB[:, sl], op=ALU.subtract), "ve")
                        ctx.mark(f"{tag}.pc{hh}_{tc}", "ve")
                    pb = 3 * (tc % 2)
                    if E == "tensor":
                        ctx.wait(eng, f"{tag}.pc{hh}_{tc}")
                        if tc >= 2:
                            ctx.wait(eng, f"{tag}.pev{hh}_{tc - 2}")
                        elif hh > 0:
                            ctx.wait(eng, f"{tag}.pev{hh - 1}_{tc + 2}")
                            if tc == 1:
                                ctx.wait(eng, f"{tag}.avev{hh - 1}_1_15")
                        if tc == 0:
                            for wi in range(3):
                                ctx.wait(eng, f"w{wj[0] + hh * 3 + wi}.cast")
                        for c in range(8):
                            for wi in range(3):
                                last = mm(eng, poa[pb + wi][:, :], wbf[wi][:, 128 * c:128 * (c + 1)], xbas[c][:, :],
                                          c == 0, c == 7)
                        ctx.inc(last, "pe")
                        ctx.mark(f"{tag}.pfold{hh}_{tc}", "pe")
                    if E == "vector":
                        ctx.wait(eng, f"{tag}.pfold{hh}_{tc}")
                        if hh == 0 and tc == 0:
                            ctx.wait(eng, f"{tag}.rB")
                        if hh > 0 and tc == 0:
                            ctx.wait(eng, f"{tag}.qk{hh - 1}_1_15")
                        if tc > 0:
                            ctx.wait(eng, f"{tag}.vtr{hh}_{tc * 4 - 1}")
                        eng.tensor_tensor(out=qTt[:, sl], in0=poa[pb + 0][:, :], in1=rB[:, sl], op=ALU.mult)
                        eng.tensor_tensor(out=kTt[:, sl], in0=poa[pb + 1][:, :], in1=rB[:, sl], op=ALU.mult)
                        ctx.inc(eng.tensor_tensor(out=vTt[:, :], in0=poa[pb + 2][:, :], in1=rB[:, sl], op=ALU.mult), "ve")
                        ctx.mark(f"{tag}.pev{hh}_{tc}", "ve")
                    for b in range(4):
                        t = tc * 4 + b
                        if E == "tensor":
                            ctx.wait(eng, f"{tag}.pev{hh}_{tc}")
                            if t == 0 and hh > 0:
                                ctx.wait(eng, f"{tag}.avev{hh - 1}_1_15")
                            if t >= 2:
                                ctx.wait(eng, f"{tag}.vev{hh}_{t - 2}")
                            ctx.inc(eng.transpose(pob[6 + (b % 2)][:, 0:128], vTt[:, 128 * b:128 * (b + 1)], ident[:, :]), "pe")
                            ctx.mark(f"{tag}.vtr{hh}_{t}", "pe")
                        if E == "vector":
                            ctx.wait(eng, f"{tag}.vtr{hh}_{t}")
                            if t == 0 and hh > 0:
                                ctx.wait(eng, f"{tag}.avden{hh - 1}_1_15")
                            eng.tensor_copy(out=vaug[t][:, 0:64], in_=pob[6 + (b % 2)][:, 0:64])
                            ctx.inc(eng.tensor_copy(out=vaug[t][:, 65:129], in_=pob[6 + (b % 2)][:, 64:128]), "ve")
                            ctx.mark(f"{tag}.vev{hh}_{t}", "ve")
                if E == "vector" and hh < 7:
                    wcasts(eng, hh + 1)
                    xba0(eng, hh + 1)
                for kb in range(17):
                    qb = kb - 1
                    w = min(384, S - 128 * kb) if kb <= 15 else 0
                    for h in range(2):
                        hp = 64 * h
                        if kb <= 15:
                            if E == "tensor":
                                if kb >= 2:
                                    ctx.wait(eng, f"{tag}.exp{hh}_{h}_{kb - 2}")
                                elif hh > 0:
                                    ctx.wait(eng, f"{tag}.exp{hh - 1}_{h}_{14 + kb}")
                                if kb == 0 and h == 0:
                                    ctx.wait(eng, f"{tag}.pev{hh}_3")
                                    ctx.wait(eng, f"{tag}.vev{hh}_15")
                                ctx.inc(mm(eng, poa[h * 2 + kb % 2][:, 0:w],
                                           kTt[hp:hp + 64, 128 * kb:128 * (kb + 1)],
                                           qTt[hp:hp + 64, 128 * kb:128 * kb + w],
                                           True, True), "pe")
                                ctx.mark(f"{tag}.qk{hh}_{h}_{kb}", "pe")
                            if E == "scalar":
                                ctx.wait(eng, f"{tag}.qk{hh}_{h}_{kb}")
                                if kb >= 4:
                                    ctx.wait(eng, f"{tag}.avden{hh}_{h}_{kb - 2}")
                                elif hh > 0:
                                    ctx.wait(eng, f"{tag}.avden{hh - 1}_{h}_{min(15, kb + 14)}")
                                ctx.inc(eng.activation(out=ATt[h][kb % 4][:, 0:w], in_=poa[h * 2 + kb % 2][:, 0:w],
                                                       func=AF.Exp, scale=0.125), "se")
                                ctx.mark(f"{tag}.exp{hh}_{h}_{kb}", "se")
                            if E == "gpsimd":
                                ctx.wait(eng, f"{tag}.exp{hh}_{h}_{kb}")
                                eng.tensor_mul(out=ATt[h][kb % 4][:, 0:128], in0=ATt[h][kb % 4][:, 0:128], in1=maskt[:, 0:128])
                                if w == 384:
                                    eng.tensor_mul(out=ATt[h][kb % 4][:, 256:384], in0=ATt[h][kb % 4][:, 256:384], in1=maskt[:, 256:384])
                                ctx.inc(eng.engine_nop(), "ge")
                                ctx.mark(f"{tag}.msk{hh}_{h}_{kb}", "ge")
                        if 0 <= qb <= 15:
                            slot = 4 + h if qb % 2 == 0 else 6 + h
                            if E == "tensor":
                                ctx.wait(eng, f"{tag}.msk{hh}_{h}_{qb}")
                                if qb >= 2:
                                    ctx.wait(eng, f"{tag}.avev{hh}_{h}_{qb - 2}")
                                elif hh > 0:
                                    ctx.wait(eng, f"{tag}.avev{hh - 1}_{h}_{qb + 14}")
                                k0 = max(0, qb - 2)
                                for kc in range(k0, qb + 1):
                                    last = mm(eng, poa[slot][:, 0:65],
                                              ATt[h][kc % 4][:, 128 * (qb - kc):128 * (qb - kc + 1)],
                                              vaug[kc][:, 65 * h:65 * (h + 1)],
                                              kc == k0, kc == qb)
                                ctx.inc(last, "pe")
                                ctx.mark(f"{tag}.avden{hh}_{h}_{qb}", "pe")
                            if E == "vector":
                                ctx.wait(eng, f"{tag}.avden{hh}_{h}_{qb}")
                                if qb >= 1:
                                    ctx.wait(eng, f"{tag}.avev{hh}_{h}_{qb - 1}")
                                elif hh > 0:
                                    ctx.wait(eng, f"{tag}.avev{hh - 1}_{h}_15")
                                ctx.inc(eng.reciprocal(out=rcol[:, h:h + 1], in_=poa[slot][:, 64:65]), "ve")
                                ctx.mark(f"{tag}.rc{hh}_{h}_{qb}", "ve")
                            if E == "scalar":
                                ctx.wait(eng, f"{tag}.rc{hh}_{h}_{qb}")
                                hidx = 2 * hh + h
                                ctx.inc(eng.activation(out=att_o[qb][:, 64 * hidx:64 * (hidx + 1)],
                                                       in_=poa[slot][:, 0:64], func=AF.Copy,
                                                       scale=rcol[:, h:h + 1]), "se")
                                ctx.mark(f"{tag}.avev{hh}_{h}_{qb}", "se")
            wj[0] += 24
            for ec in range(8):
                j = wj[0] + ec
                if E == "sync":
                    ctx.wait(eng, f"w{j - 4}.free")
                    ctx.inc(eng.dma_start(out=wld[j % 4][:, :], in_=wob.ap()[ec]), f"sy{j % 4}", 16)
                    ctx.mark(f"w{j}.ld", f"sy{j % 4}")
                if E == "vector":
                    ctx.wait(eng, f"w{j}.ld")
                    if ec == 0:
                        ctx.wait(eng, f"{tag}.avden7_1_15")
                    ctx.inc(eng.tensor_copy(out=wobf[ec][:, :], in_=wld[j % 4][:, :]), "ve")
                    ctx.mark(f"w{j}.cast", "ve")
                    ctx.mark(f"w{j}.free", "ve")
            for tg in range(4):
                gsl = slice(512 * tg, 512 * (tg + 1))
                for bq in range(4):
                    qb = tg * 4 + bq
                    for ec in range(8):
                        t = qb * 8 + ec
                        tl = bq * 8 + ec
                        if E == "tensor":
                            if t == 0:
                                ctx.wait(eng, f"{tag}.avev7_1_15")
                            if t >= 2:
                                ctx.wait(eng, f"{tag}.wtev{t - 2}")
                            if tl < 2 and tg > 0:
                                ctx.wait(eng, f"{tag}.woev{tg - 1}_7")
                            ctx.inc(eng.transpose(pob[2 + t % 2][:, 0:128], att_o[qb][:, 128 * ec:128 * (ec + 1)], ident[:, :]), "pe")
                            ctx.mark(f"{tag}.wtr{t}", "pe")
                        if E == "vector":
                            ctx.wait(eng, f"{tag}.wtr{t}")
                            if t == 0:
                                ctx.wait(eng, f"{tag}.pev7_3")
                            ctx.inc(eng.tensor_copy(out=aoTs[ec][:, 128 * bq:128 * (bq + 1)], in_=pob[2 + t % 2][:, 0:128]), "ve")
                            ctx.mark(f"{tag}.wtev{t}", "ve")
                for ec in range(8):
                    if E == "tensor":
                        ctx.wait(eng, f"{tag}.wtev{(tg * 4 + 3) * 8 + 7}")
                        ctx.wait(eng, f"w{wj[0] + ec}.cast")
                        u = tg * 8 + ec
                        if u >= 2:
                            ctx.wait(eng, f"{tag}.woev{(u - 2) // 8}_{(u - 2) % 8}")
                        for c in range(8):
                            last = mm(eng, poa[ec % 2][:, :], wobf[ec][:, 128 * c:128 * (c + 1)], aoTs[c][:, :],
                                      c == 0, c == 7)
                        ctx.inc(last, "pe")
                        ctx.mark(f"{tag}.womm{tg}_{ec}", "pe")
                    if E == "vector":
                        ctx.wait(eng, f"{tag}.womm{tg}_{ec}")
                        ctx.inc(eng.tensor_tensor(out=xt[ec][:, gsl], in0=poa[ec % 2][:, :], in1=xt[ec][:, gsl], op=ALU.add), "ve")
                        ctx.mark(f"{tag}.woev{tg}_{ec}", "ve")
                if "ffn1" in PH:
                    ln_stats_tc(E, eng, "f0", tg, f"{tag}.woev{tg}_7")
            wj[0] += 8

        def ffn(E, eng, fi):
            tag = f"f{fi}"
            ln_stats_fin(E, eng, tag, want_mB=True, want_rB=True)
            if fi == 0:
                for o in range(8):
                    j = wj[0] + o
                    if E == "sync":
                        ctx.wait(eng, f"w{j - 4}.free")
                        ctx.inc(eng.dma_start(out=wld[j % 4][:, :], in_=wfb.ap()[o]), f"sy{j % 4}", 16)
                        ctx.mark(f"w{j}.ld", f"sy{j % 4}")
                    if E == "vector":
                        ctx.wait(eng, f"w{j}.ld")
                        if o == 0:
                            ctx.inc(eng.tensor_reduce(out=amax_col[:, :], in_=wld[j % 4][:, :], axis=AX.X, op=ALU.max, apply_absolute_value=True), "ve")
                            ctx.mark(f"w{j}.free", "ve")
                            ctx.wait(eng, f"w{j}.free")
                        else:
                            ctx.inc(eng.tensor_reduce(out=scol[:, 3:4], in_=wld[j % 4][:, :], axis=AX.X, op=ALU.max, apply_absolute_value=True), "ve")
                            ctx.mark(f"{tag}.wred{o}", "ve")
                            ctx.wait(eng, f"{tag}.wred{o}")
                            ctx.inc(eng.tensor_tensor(out=amax_col[:, :], in0=amax_col[:, :], in1=scol[:, 3:4], op=ALU.max), "ve")
                            ctx.mark(f"w{j}.free", "ve")
                            ctx.wait(eng, f"w{j}.free")
                if E == "tensor":
                    ctx.wait(eng, f"w{wj[0] + 7}.free")
                    ctx.inc(eng.transpose(poa[4][0:1, 0:128], amax_col[:, :], identf[:, :]), "pe")
                    ctx.mark(f"{tag}.wtr", "pe")
                if E == "vector":
                    ctx.wait(eng, f"{tag}.wtr")
                    ctx.inc(eng.tensor_reduce(out=smalls[0:1, 2:3], in_=poa[4][0:1, 0:128], axis=AX.X, op=ALU.max), "ve")
                    ctx.mark(f"{tag}.gw", "ve")
                    ctx.wait(eng, f"{tag}.gw")
                wj[0] += 8
            for c in range(8):
                half = "vector" if c < 4 else "gpsimd"
                sem = "ve" if c < 4 else "ge"
                xfc = xfs[c % 2 + (0 if c < 4 else 2)]
                if E == half:
                    if c in (0, 4):
                        ctx.wait(eng, f"{tag}.mB")
                        ctx.wait(eng, f"{tag}.rB")
                        if fi == 0 and c == 0:
                            ctx.wait(eng, f"{tag}.wtr")
                    if c >= 6:
                        ctx.wait(eng, f"{tag}.axr{c - 2}")
                    ctx.inc(eng.tensor_tensor(out=xfc[:, :], in0=xt[c][:, :], in1=mB[:, :], op=ALU.subtract), sem)
                    ctx.mark(f"{tag}.axs{c}", sem)
                    ctx.wait(eng, f"{tag}.axs{c}")
                    ctx.inc(eng.tensor_tensor(out=xfc[:, :], in0=xfc[:, :], in1=rB[:, :], op=ALU.mult), sem)
                    ctx.mark(f"{tag}.axm{c}", sem)
                    ctx.wait(eng, f"{tag}.axm{c}")
                if E == "vector":
                    ctx.wait(eng, f"{tag}.axm{c}")
                    ctx.inc(eng.tensor_reduce(out=scol[:, 3:4], in_=xfc[:, :], axis=AX.X, op=ALU.max, apply_absolute_value=True), "ve")
                    ctx.mark(f"{tag}.axr{c}", "ve")
                    ctx.wait(eng, f"{tag}.axr{c}")
                    if c == 0:
                        ctx.inc(eng.tensor_copy(out=amax_col[:, :], in_=scol[:, 3:4]), "ve")
                    else:
                        ctx.inc(eng.tensor_tensor(out=amax_col[:, :], in0=amax_col[:, :], in1=scol[:, 3:4], op=ALU.max), "ve")
                    ctx.mark(f"{tag}.ax{c}", "ve")
                    ctx.wait(eng, f"{tag}.ax{c}")
            if E == "vector":
                ctx.mark(f"{tag}.axall", "ve")
            if E == "tensor":
                ctx.wait(eng, f"{tag}.axall")
                ctx.inc(eng.transpose(poa[4][0:1, 0:128], amax_col[:, :], identf[:, :]), "pe")
                ctx.mark(f"{tag}.axtr", "pe")
            if E == "vector":
                ctx.wait(eng, f"{tag}.axtr")
                ctx.inc(eng.tensor_reduce(out=smalls[0:1, 0:1], in_=poa[4][0:1, 0:128], axis=AX.X, op=ALU.max), "ve")
                ctx.mark(f"{tag}.lred", "ve")
                ctx.wait(eng, f"{tag}.lred")
                ctx.wait(eng, "init_ve")
                ctx.inc(eng.tensor_copy(out=zc[0:1, 0:1], in_=smalls[0:1, 0:1]), "ve")
                ctx.mark(f"{tag}.lmax", "ve")
            ca, cb, cc2 = f"c{fi}a", f"c{fi}b", f"c{fi}c"
            if E == "gpsimd":
                ctx.wait(eng, f"{tag}.lmax")
                ctx.inc(eng.dma_start(out=cc_in[fi].ap(), in_=zc[:, :]), ca, 16)
                ctx.mark(f"{tag}.cci", ca)
                ctx.wait(eng, f"{tag}.cci")
                ctx.inc(eng.collective_compute("AllReduce", ALU.max, replica_groups=[list(range(8))],
                                               ins=[cc_in[fi].ap().opt()], outs=[cc_out[fi].ap().opt()]), cc2, 1)
                ctx.mark(f"{tag}.ccc", cc2)
                ctx.wait(eng, f"{tag}.ccc")
                ctx.inc(eng.dma_start(out=smalls[0:1, 1:2], in_=cc_out[fi].ap()[0:1, 0:1]), cb, 16)
                ctx.mark(f"{tag}.gmo", cb)
                ctx.wait(eng, f"{tag}.gmo")
                ctx.inc(eng.engine_nop(), "ge")
                ctx.mark(f"{tag}.gmax", "ge")
            for c in range(4):
                xfc = xfs[c]
                if E == "vector":
                    ctx.inc(eng.tensor_tensor(out=xfc[:, :], in0=xt[c][:, :], in1=mB[:, :], op=ALU.subtract), "ve")
                    ctx.mark(f"{tag}.xs{c}", "ve")
                    ctx.wait(eng, f"{tag}.xs{c}")
                    ctx.inc(eng.tensor_tensor(out=xfc[:, :], in0=xfc[:, :], in1=rB[:, :], op=ALU.mult), "ve")
                    ctx.mark(f"{tag}.xn{c}", "ve")
            if E == "vector":
                ctx.wait(eng, f"{tag}.gmax")
                if fi == 0:
                    ctx.wait(eng, f"{tag}.gw")
                steps = [
                    lambda: eng.reciprocal(out=smalls[0:1, 4:5], in_=smalls[0:1, 1:2]),
                    lambda: eng.tensor_scalar(out=smalls[0:1, 5:6], in0=smalls[0:1, 4:5], scalar1=224.0, scalar2=None, op0=ALU.mult),
                    lambda: eng.tensor_tensor(out=smalls[0:1, 3:4], in0=smalls[0:1, 1:2], in1=smalls[0:1, 2:3], op=ALU.mult),
                    lambda: eng.tensor_scalar(out=smalls[0:1, 3:4], in0=smalls[0:1, 3:4], scalar1=GSC, scalar2=None, op0=ALU.mult),
                    lambda: eng.reciprocal(out=smalls[0:1, 6:7], in_=smalls[0:1, 2:3]),
                    lambda: eng.tensor_scalar(out=smalls[0:1, 7:8], in0=smalls[0:1, 6:7], scalar1=224.0, scalar2=None, op0=ALU.mult),
                ]
                for si, f in enumerate(steps):
                    ctx.inc(f(), "ve")
                    ctx.mark(f"{tag}.scl{si}", "ve")
                    ctx.wait(eng, f"{tag}.scl{si}")
                ctx.mark(f"{tag}.scales", "ve")
            for si, (sidx, col) in enumerate([(5, 0), (3, 1), (7, 2)]):
                if E == "vector":
                    if si == 0:
                        ctx.wait(eng, f"{tag}.scales")
                    ctx.inc(eng.tensor_scalar(out=t_row[0:1, 0:128], in0=onesr[:, :],
                                               scalar1=smalls[0:1, sidx:sidx + 1], scalar2=None, op0=ALU.mult), "ve")
                    ctx.mark(f"{tag}.bro{si}", "ve")
                if E == "tensor":
                    ctx.wait(eng, f"{tag}.bro{si}")
                    ctx.inc(eng.transpose(poa[4][:, 0:1], t_row[0:1, 0:128], identf[0:1, 0:1]), "pe")
                    ctx.mark(f"{tag}.brt{si}", "pe")
                if E == "vector":
                    ctx.wait(eng, f"{tag}.brt{si}")
                    ctx.inc(eng.tensor_copy(out=scol[:, col:col + 1], in_=poa[4][:, 0:1]), "ve")
                    ctx.mark(f"{tag}.brc{si}", "ve")
                if E == "tensor":
                    ctx.wait(eng, f"{tag}.brc{si}")
                    eng.nop()
            if E == "vector":
                ctx.mark(f"{tag}.sc1", "ve")
                ctx.mark(f"{tag}.sc2", "ve")
            for c in range(8):
                xfc = xfs[c % 4]
                if E == "vector" and c >= 4:
                    ctx.wait(eng, f"{tag}.xq{c - 4}")
                    ctx.inc(eng.tensor_tensor(out=xfc[:, :], in0=xt[c][:, :], in1=mB[:, :], op=ALU.subtract), "ve")
                    ctx.mark(f"{tag}.xs{c}", "ve")
                    ctx.wait(eng, f"{tag}.xs{c}")
                    ctx.inc(eng.tensor_tensor(out=xfc[:, :], in0=xfc[:, :], in1=rB[:, :], op=ALU.mult), "ve")
                    ctx.mark(f"{tag}.xn{c}", "ve")
                ceng = "scalar" if c % 2 == 0 else "vector"
                csem = "se" if c % 2 == 0 else "ve"
                if E == ceng:
                    ctx.wait(eng, f"{tag}.xn{c}")
                    if c in (0, 1):
                        ctx.wait(eng, f"{tag}.sc1")
                    if ceng == "scalar":
                        ctx.inc(eng.activation(out=xq8[c][:, :], in_=xfc[:, :], func=AF.Copy, scale=scol[:, 0:1]), csem)
                    else:
                        ctx.inc(eng.tensor_scalar(out=xq8[c][:, :], in0=xfc[:, :], scalar1=scol[:, 0:1], scalar2=None, op0=ALU.mult), csem)
                    ctx.mark(f"{tag}.xq{c}", csem)
            for o in range(8):
                j = wj[0] + o
                if E == "sync":
                    ctx.wait(eng, f"w{j - 4}.free")
                    ctx.inc(eng.dma_start(out=wld[j % 4][:, :], in_=wfb.ap()[o]), f"sy{j % 4}", 16)
                    ctx.mark(f"w{j}.ld", f"sy{j % 4}")
                ceng = "scalar" if o % 2 == 0 else "vector"
                csem = "se" if o % 2 == 0 else "ve"
                if E == ceng:
                    ctx.wait(eng, f"w{j}.ld")
                    if o in (0, 1):
                        ctx.wait(eng, f"{tag}.sc1")
                        if fi == 0:
                            ctx.wait(eng, "a1.woevall")
                    if ceng == "scalar":
                        ctx.inc(eng.activation(out=wq8[o][:, :], in_=wld[j % 4][:, :], func=AF.Copy, scale=scol[:, 2:3]), csem)
                    else:
                        ctx.inc(eng.tensor_scalar(out=wq8[o][:, :], in0=wld[j % 4][:, :], scalar1=scol[:, 2:3], scalar2=None, op0=ALU.mult), csem)
                    ctx.mark(f"w{j}.free", csem)
                    ctx.mark(f"{tag}.wq{o}", csem)
            wj[0] += 8
            for tc in range(4):
                for o in range(8):
                    u = tc * 8 + o
                    sl = slice(512 * tc, 512 * (tc + 1))
                    if E == "tensor":
                        if u == 0:
                            ctx.wait(eng, f"{tag}.wq7")
                            if fi == 1:
                                ctx.wait(eng, "b.avevall")
                            else:
                                ctx.wait(eng, "a1.woevall")
                        if u >= 2:
                            ctx.wait(eng, f"{tag}.fev{u - 2}")
                        for c in range(8):
                            last = mm(eng, poa[o % 2][:, :], wq8[o][:, 128 * c:128 * (c + 1)], xq8[c][:, sl],
                                      c == 0, c == 7)
                        ctx.inc(last, "pe")
                        ctx.mark(f"{tag}.fmm{u}", "pe")
                    if E == "scalar":
                        ctx.wait(eng, f"{tag}.fmm{u}")
                        if u == 0:
                            ctx.wait(eng, f"{tag}.sc2")
                        if u >= 2:
                            ctx.wait(eng, f"{tag}.fev{u - 2}")
                        ctx.inc(eng.activation(out=gtf[o % 2][:, :], in_=poa[o % 2][:, :], func=AF.Copy,
                                               scale=scol[:, 1:2]), "se")
                        ctx.mark(f"{tag}.fep{u}", "se")
                    if E == "vector":
                        ctx.wait(eng, f"{tag}.fep{u}")
                        ctx.inc(eng.scalar_tensor_tensor(out=xt[o][:, sl], in0=gtf[o % 2][:, :],
                                                         scalar=bias_sb[:, o:o + 1], in1=xt[o][:, sl],
                                                         op0=ALU.add, op1=ALU.add), "ve")
                        ctx.mark(f"{tag}.fev{u}", "ve")
                if fi == 0 and "attn_b" in PH:
                    ln_stats_tc(E, eng, "b", tc, f"{tag}.fev{tc * 8 + 7}")
            if E == "vector":
                ctx.mark(f"{tag}.fevall", "ve")

        def attn_b(E, eng):
            tag = "b"
            ln_stats_fin(E, eng, tag, want_mB=True, want_rB=False)
            for tt in range(4):
                ttsl = slice(512 * tt, 512 * (tt + 1))
                for c in range(8):
                    if E == "vector":
                        if c == 0:
                            if tt == 0:
                                ctx.wait(eng, f"{tag}.mB")
                            else:
                                ctx.wait(eng, f"{tag}.smmdone{tt - 1}")
                        ctx.inc(eng.tensor_tensor(out=xbt_b[c][:, :], in0=xt[c][:, ttsl], in1=mB[:, ttsl], op=ALU.subtract), "ve")
                        ctx.mark(f"{tag}.xb{tt}_{c}", "ve")
                stride = 64 if BCUT >= 4 else 32
                kbase = wj[0] + tt * stride
                vbase = kbase + 32
                for n in range(32):
                    j = kbase + n
                    if E == "sync":
                        ctx.wait(eng, f"w{j - 4}.free")
                        ctx.inc(eng.dma_start(out=wld[j % 4][:, :], in_=knbd.ap()[n]), f"sy{j % 4}", 16)
                        ctx.mark(f"w{j}.ld", f"sy{j % 4}")
                    if E == "vector":
                        ctx.wait(eng, f"w{j}.ld")
                        if n >= 2:
                            ctx.wait(eng, f"{tag}.smm{tt}_{n - 2}")
                        elif tt > 0:
                            ctx.wait(eng, f"{tag}.avmm{tt - 1}_{30 + n}" if BCUT >= 4 else f"{tag}.smm{tt - 1}_{30 + n}")
                        ctx.inc(eng.tensor_copy(out=wbf[n % 2][:, :], in_=wld[j % 4][:, :]), "ve")
                        ctx.mark(f"w{j}.cast", "ve")
                        ctx.mark(f"w{j}.free", "ve")
                    if E == "tensor":
                        if n == 0:
                            for c in range(8):
                                ctx.wait(eng, f"{tag}.xb{tt}_{c}")
                            if tt == 0:
                                ctx.wait(eng, "f0.fevall")
                            elif BCUT >= 4:
                                ctx.wait(eng, f"{tag}.avevall{tt - 1}")
                        ctx.wait(eng, f"w{j}.cast")
                        if n >= 2:
                            ctx.wait(eng, f"{tag}.scp{tt}_{n - 2}")
                            if BCUT >= 2:
                                ctx.wait(eng, f"{tag}.ssq{tt}_{n - 2}")
                        for c in range(8):
                            last = mm(eng, poa[n % 2][:, :], wbf[n % 2][:, 128 * c:128 * (c + 1)], xbt_b[c][:, :],
                                      c == 0, c == 7)
                        ctx.inc(last, "pe")
                        ctx.mark(f"{tag}.smm{tt}_{n}", "pe")
                    if E == "scalar" and BCUT >= 2:
                        ctx.wait(eng, f"{tag}.scp{tt}_{n}")
                        if n >= 2 and B2SUB >= 1:
                            ctx.wait(eng, f"{tag}.n2mm{tt}_{n - 2}")
                        ctx.inc(eng.activation(out=sqb[n % 2][:, :], in_=gt[n][:, :], func=AF.Square), "se")
                        ctx.mark(f"{tag}.ssq{tt}_{n}", "se")
                    if E == "vector":
                        ctx.wait(eng, f"{tag}.smm{tt}_{n}")
                        ctx.inc(eng.tensor_copy(out=gt[n][:, :], in_=poa[n % 2][:, :]), "ve")
                        ctx.mark(f"{tag}.scp{tt}_{n}", "ve")
                    if E == "tensor" and BCUT >= 2 and B2SUB >= 1:
                        ctx.wait(eng, f"{tag}.ssq{tt}_{n}")
                        ctx.inc(mm(eng, poa[6][0:1, :], onescol[:, :], sqb[n % 2][:, :],
                                   n == 0, n == 31), "pe")
                        ctx.mark(f"{tag}.n2mm{tt}_{n}", "pe")
                if E == "tensor":
                    ctx.mark(f"{tag}.smmdone{tt}", "pe")
                if E == "scalar" and BCUT >= 2 and B2SUB >= 2:
                    ctx.wait(eng, f"{tag}.n2mm{tt}_31")
                    ctx.inc(eng.activation(out=r_row[:, 0:512], in_=poa[6][0:1, :], func=AF.Ln), "se")
                    ctx.mark(f"{tag}.cl{tt}", "se")
                    ctx.wait(eng, f"{tag}.cl{tt}")
                    ctx.inc(eng.activation(out=r_row[:, 0:512], in_=r_row[:, 0:512], func=AF.Exp, scale=-0.5), "se")
                    ctx.mark(f"{tag}.ce{tt}", "se")
                    ctx.wait(eng, f"{tag}.ce{tt}")
                    ctx.inc(eng.mul(out=r_row[:, 0:512], in_=r_row[:, 0:512], mul=64.0), "se")
                    ctx.mark(f"{tag}.c{tt}", "se")
                    ctx.wait(eng, f"{tag}.c{tt}")
                    if B2SUB >= 3:
                        if tt > 0 and BCUT >= 3:
                            ctx.wait(eng, f"{tag}.gmulall{tt - 1}")
                        ctx.inc(eng.dma_start(out=rb_h.ap()[0:1, 0:512], in_=r_row[:, 0:512]), "sq", 16)
                        ctx.mark(f"{tag}.cbo{tt}", "sq")
                        ctx.wait(eng, f"{tag}.cbo{tt}")
                        ctx.inc(eng.dma_start(out=cB[:, :], in_=bass.AP(rb_h, 0, [[0, 128], [1, 512]])), "sq", 16)
                        ctx.mark(f"{tag}.cB{tt}", "sq")
                for n in range(32):
                    if E == "vector" and BCUT >= 3:
                        if n == 0:
                            ctx.wait(eng, f"{tag}.cB{tt}")
                        if n >= 2:
                            ctx.wait(eng, f"{tag}.gel{tt}_{n - 2}")
                        ctx.inc(eng.tensor_tensor(out=gtf[n % 2][:, :], in0=gt[n][:, :], in1=cB[:, :], op=ALU.mult), "ve")
                        ctx.mark(f"{tag}.gmul{tt}_{n}", "ve")
                    if E == "scalar" and BCUT >= 3:
                        ctx.wait(eng, f"{tag}.gmul{tt}_{n}")
                        ctx.inc(eng.activation(out=gt[n][:, :], in_=gtf[n % 2][:, :], func=AF.Gelu), "se")
                        ctx.mark(f"{tag}.gel{tt}_{n}", "se")
                    if BCUT >= 4:
                        j = vbase + n
                        if E == "sync":
                            ctx.wait(eng, f"w{j - 4}.free")
                            ctx.inc(eng.dma_start(out=wld[j % 4][:, :], in_=vnbd.ap()[n]), f"sy{j % 4}", 16)
                            ctx.mark(f"w{j}.ld", f"sy{j % 4}")
                        if E == "scalar":
                            ctx.wait(eng, f"w{j}.ld")
                            if n >= 2:
                                ctx.wait(eng, f"{tag}.avmm{tt}_{n - 2}")
                            else:
                                ctx.wait(eng, f"{tag}.smm{tt}_{30 + n}")
                            ctx.inc(eng.activation(out=wbf[n % 2][:, :], in_=wld[j % 4][:, :], func=AF.Copy), "se")
                            ctx.mark(f"w{j}.cast", "se")
                            ctx.mark(f"w{j}.free", "se")
                        if E == "tensor":
                            ctx.wait(eng, f"w{j}.cast")
                            ctx.wait(eng, f"{tag}.gel{tt}_{n}")
                            for dc in range(8):
                                last = mm(eng, poa[dc][:, :], wbf[n % 2][:, 128 * dc:128 * (dc + 1)], gt[n][:, :],
                                          n == 0, n == 31)
                            ctx.inc(last, "pe")
                            ctx.mark(f"{tag}.avmm{tt}_{n}", "pe")
                if E == "vector" and BCUT >= 3:
                    ctx.mark(f"{tag}.gmulall{tt}", "ve")
                if E == "tensor" and tt == 3 and BCUT >= 4:
                    ctx.mark(f"{tag}.avdone", "pe")
                for dc in range(8 if BCUT >= 4 else 0):
                    if E == "vector":
                        ctx.wait(eng, f"{tag}.avmm{tt}_31")
                        ctx.inc(eng.tensor_tensor(out=xt[dc][:, ttsl], in0=poa[dc][:, :], in1=xt[dc][:, ttsl], op=ALU.add), "ve")
                        ctx.mark(f"{tag}.avev{tt}_{dc}", "ve")
                if E == "vector" and BCUT >= 4:
                    ctx.mark(f"{tag}.avevall{tt}", "ve")
                if BCUT >= 4 and "ffn2" in PH:
                    ln_stats_tc(E, eng, "f1", tt, f"{tag}.avev{tt}_7")
            wj[0] += 256 if BCUT >= 4 else 128
            if E == "vector" and BCUT >= 4:
                ctx.mark(f"{tag}.avevall", "ve")

        def store(E, eng, final_mark):
            if E == "sync":
                ctx.wait(eng, final_mark)
                for c in range(8):
                    ctx.inc(eng.dma_start(out=out_t.ap()[128 * c:128 * (c + 1), :], in_=xt[c][:, :]), "sy", 16)
                if DEBUG:
                    ctx.inc(eng.dma_start(out=dbg1.ap()[0:1, 0:16], in_=smalls[:, :]), "sy", 16)
                    ctx.inc(eng.dma_start(out=dbg1.ap()[0:1, 16:16 + S], in_=m_row), "sy", 16)
                    ctx.inc(eng.dma_start(out=dbg1.ap()[0:1, 16 + S:16 + 2 * S], in_=r_row), "sy", 16)
                    ctx.inc(eng.dma_start(out=dbg1.ap()[0:1, 16 + 2 * S:16 + 3 * S], in_=t_row), "sy", 16)
                    ctx.inc(eng.dma_start(out=dbgs.ap(), in_=scol[:, :]), "sy", 16)
                    ctx.inc(eng.dma_start(out=dbga.ap(), in_=amax_col[:, :]), "sy", 16)
                    ctx.inc(eng.dma_start(out=dbgm.ap(), in_=mB[:, 0:8]), "sy", 16)
                    ctx.inc(eng.dma_start(out=dbgr.ap(), in_=rB[:, 0:8]), "sy", 16)
                ctx.mark("st_done", "sy")
                ctx.wait(eng, "st_done")
            if E == "gpsimd":
                ctx.wait(eng, "st_done")

        def program(E, eng):
            wj[0] = 0
            preload(E, eng)
            final = None
            if "attn_a" in PH:
                attn_a(E, eng)
                if E == "vector":
                    ctx.mark("a1.woevall", "ve")
                if E == "tensor":
                    ctx.mark("a1.woevall_pe", "pe")
                final = "a1.woevall"
            if "ffn1" in PH:
                ffn(E, eng, 0)
                if E == "tensor":
                    ctx.mark("f0.fevall_pe", "pe")
                final = "f0.fevall"
            if "attn_b" in PH:
                attn_b(E, eng)
                final = "b.avevall" if BCUT >= 4 else "f0.fevall"
            if "ffn2" in PH:
                ffn(E, eng, 1)
                final = "f1.fevall"
            store(E, eng, final)

        ctx.emit = False
        for E in ["sync", "tensor", "vector", "scalar", "gpsimd"]:
            program(E, _DummyEngine())
        counted = dict(ctx.cnt)

        ctx.emit = True
        with nc.Block() as block:
            for ename, deco in (("sync", block.sync), ("tensor", block.tensor),
                                ("vector", block.vector), ("scalar", block.scalar),
                                ("gpsimd", block.gpsimd)):
                def run_engine(eng, _ename=ename):
                    ctx.cnt = {}
                    program(_ename, eng)
                deco(run_engine)
        ctx.cnt = counted
    return nc



def _hhblock(wT):
    blk = wT.reshape(8, 128, 8, 128).transpose(2, 1, 0, 3).reshape(8, 128, 1024)
    return np.ascontiguousarray(blk)


def _consts():
    i = np.arange(128)
    causal = (i[:, None] <= i[None, :]).astype(ml_dtypes.bfloat16)
    anti = (i[:, None] >= i[None, :] + 1).astype(ml_dtypes.bfloat16)
    mask = np.concatenate([causal, np.ones((128, 128), ml_dtypes.bfloat16), anti], axis=1)
    ident = np.eye(128, dtype=ml_dtypes.bfloat16)
    ones = np.ones((128, 1), ml_dtypes.bfloat16)
    return mask, ident, ones


def make_inputs(L0_attn_a_x, L0_attn_a_Wq, L0_attn_a_Wk, L0_attn_a_Wv, L0_attn_a_Wo,
                L0_attn_b_key_param_tokens, L0_attn_b_value_param_tokens,
                L0_ffn_weight, L0_ffn_bias):
    x = np.asarray(L0_attn_a_x, dtype=np.float32)
    wq = _hhblock(np.ascontiguousarray(np.asarray(L0_attn_a_Wq, np.float32).T))
    wk = _hhblock(np.ascontiguousarray(np.asarray(L0_attn_a_Wk, np.float32).T))
    wv = _hhblock(np.ascontiguousarray(np.asarray(L0_attn_a_Wv, np.float32).T))
    wo = _hhblock(np.ascontiguousarray(np.asarray(L0_attn_a_Wo, np.float32).T))
    wf = _hhblock(np.ascontiguousarray(np.asarray(L0_ffn_weight, np.float32).T))
    kT = np.ascontiguousarray(np.asarray(L0_attn_b_key_param_tokens, np.float32).T)
    knb = np.ascontiguousarray(kT.reshape(8, 128, 32, 128).transpose(2, 1, 0, 3).reshape(32, 128, 1024))
    vnb = np.ascontiguousarray(np.asarray(L0_attn_b_value_param_tokens, np.float32).reshape(32, 128, 1024))
    bias = np.ascontiguousarray(np.asarray(L0_ffn_bias, np.float32).reshape(8, 128).T)
    mask, ident, ones = _consts()
    identf = np.eye(128, dtype=np.float32)
    onesr = np.ones((1, 128), np.float32)
    in_maps = []
    for b in range(8):
        in_maps.append({
            "x_in": np.ascontiguousarray(x[b].T),
            "wqb": wq, "wkb": wk, "wvb": wv, "wob": wo, "wfb": wf,
            "knbd": knb, "vnbd": vnb, "bias_in": bias,
            "mask_in": mask, "id_in": ident, "ones_in": ones,
            "idf_in": identf, "onesr_in": onesr,
        })
    return in_maps


def kernel(**inputs):
    in_maps = make_inputs(**inputs)
    nc = build_nc(4)
    res = run_bass_kernel_spmd(nc, in_maps, core_ids=list(range(8)))
    out = np.stack([np.asarray(res.results[b]["out"]).T for b in range(8)])
    return np.ascontiguousarray(out)


# version 08
# speedup vs baseline: 1.0916x; 1.0916x over previous
import contextlib
import numpy as np
import ml_dtypes

import concourse.bass as bass
import concourse.mybir as mybir
from concourse.bass_utils import run_bass_kernel_spmd

F32 = mybir.dt.float32
BF16 = mybir.dt.bfloat16
FP8 = mybir.dt.float8e4
AF = mybir.ActivationFunctionType
ALU = mybir.AluOpType
AX = mybir.AxisListType

S, D = 2048, 1024
LN64 = float(np.log(64.0))
EPS = 1e-5
GSC = 4.0 / (448.0 * 448.0)
DEBUG = False
BCUT = 4
B2SUB = 3


class _DummyInstr:
    def then_inc(self, *a, **k):
        return self


class _DummyEngine:
    def __getattr__(self, name):
        def f(*a, **k):
            return _DummyInstr()
        return f


class Ctx:
    def __init__(self):
        self.emit = False
        self.cnt = {}
        self.cp = {}
        self.sems = {}

    def inc(self, instr, sem, n=1):
        self.cnt[sem] = self.cnt.get(sem, 0) + n
        if self.emit:
            instr.then_inc(self.sems[sem], n)
        return instr

    def mark(self, name, sem):
        cur = (sem, self.cnt.get(sem, 0))
        if not self.emit:
            assert name not in self.cp, f"dup checkpoint {name}"
            self.cp[name] = cur
        else:
            assert self.cp.get(name) == cur, (name, self.cp.get(name), cur)

    def wait(self, eng, name):
        if self.emit and not isinstance(eng, _DummyEngine):
            sem, val = self.cp[name]
            if val > 0:
                eng.wait_ge(self.sems[sem], val)


def build_nc(n_phases=4):
    nc = bass.Bass("TRN2", target_bir_lowering=False, debug=False)

    x_in = nc.dram_tensor("x_in", [D, S], F32, kind="ExternalInput")
    wqb = nc.dram_tensor("wqb", [8, 128, 1024], F32, kind="ExternalInput")
    wkb = nc.dram_tensor("wkb", [8, 128, 1024], F32, kind="ExternalInput")
    wvb = nc.dram_tensor("wvb", [8, 128, 1024], F32, kind="ExternalInput")
    wob = nc.dram_tensor("wob", [8, 128, 1024], F32, kind="ExternalInput")
    wfb = nc.dram_tensor("wfb", [8, 128, 1024], F32, kind="ExternalInput")
    knbd = nc.dram_tensor("knbd", [32, 128, 1024], F32, kind="ExternalInput")
    vnbd = nc.dram_tensor("vnbd", [32, 128, 1024], F32, kind="ExternalInput")
    bias_in = nc.dram_tensor("bias_in", [128, 8], F32, kind="ExternalInput")
    mask_in = nc.dram_tensor("mask_in", [128, 384], BF16, kind="ExternalInput")
    id_in = nc.dram_tensor("id_in", [128, 128], BF16, kind="ExternalInput")
    ones_in = nc.dram_tensor("ones_in", [128, 1], BF16, kind="ExternalInput")
    idf_in = nc.dram_tensor("idf_in", [128, 128], F32, kind="ExternalInput")
    onesr_in = nc.dram_tensor("onesr_in", [1, 128], F32, kind="ExternalInput")
    out_t = nc.dram_tensor("out", [D, S], F32, kind="ExternalOutput")
    if DEBUG:
        dbg1 = nc.dram_tensor("dbg1", [1, 16 + 3 * S], F32, kind="ExternalOutput")
        dbgs = nc.dram_tensor("dbgs", [128, 4], F32, kind="ExternalOutput")
        dbga = nc.dram_tensor("dbga", [128, 1], F32, kind="ExternalOutput")
        dbgm = nc.dram_tensor("dbgm", [128, 8], F32, kind="ExternalOutput")
        dbgr = nc.dram_tensor("dbgr", [128, 8], F32, kind="ExternalOutput")

    rb_h = nc.dram_tensor("rbounce", [1, S], F32)
    cb_h = nc.dram_tensor("cbounce", [128, 1], F32)
    sb_h = nc.dram_tensor("sbounce", [1, 4], F32)
    cc_in = [nc.dram_tensor(f"cc_in{i}", [128, 128], F32) for i in range(2)]
    cc_out = [nc.dram_tensor(f"cc_out{i}", [128, 128], F32) for i in range(2)]

    ctx = Ctx()
    es = contextlib.ExitStack()
    sbuf = lambda name, shape, dt: es.enter_context(nc.sbuf_tensor(name, shape, dt))

    with es:
        xt = [sbuf(f"xt{c}", [128, S], F32) for c in range(8)]
        maskt = sbuf("maskt", [128, 384], BF16)
        ident = sbuf("ident", [128, 128], BF16)
        onescol = sbuf("onescol", [128, 1], BF16)
        identf = sbuf("identf", [128, 128], F32)
        onesr = sbuf("onesr", [1, 128], F32)
        bias_sb = sbuf("bias_sb", [128, 8], F32)
        rows = sbuf("rows", [1, 3 * S], F32)
        m_row = rows[0:1, 0:S]
        r_row = rows[0:1, S:2 * S]
        t_row = rows[0:1, 2 * S:3 * S]
        smalls = sbuf("smalls", [1, 16], F32)
        scol = sbuf("scol", [128, 4], F32)
        amax_col = sbuf("amax_col", [128, 1], F32)
        zc = sbuf("zc", [128, 128], F32)
        aux = sbuf("aux", [128, 2 * S], F32)
        mB = aux[:, 0:S]
        rB = aux[:, S:2 * S]
        cB = aux[:, S:S + 512]
        gtf = [aux[:, S + 512 + i * 512:S + 512 + (i + 1) * 512] for i in range(2)]
        sqb2 = sbuf("sqb2", [128, 1024], BF16)
        sqb = [sqb2[:, 0:512], sqb2[:, 512:1024]]
        sq4b = None
        xbc = sbuf("xbc", [128, 1024], BF16)
        xbcs = [xbc[:, 0:512], xbc[:, 512:1024]]
        sq4b = sqb + xbcs
        xba = sbuf("xba", [128, 4096], BF16)
        xbas = [xba[:, i * 512:(i + 1) * 512] for i in range(8)]
        wld = [sbuf(f"wld{i}", [128, 1024], F32) for i in range(4)]
        wbf = [sbuf(f"wbf{i}", [128, 1024], BF16) for i in range(3)]
        vTt = sbuf("vTt", [128, 512], BF16)
        vaug = [sbuf(f"vaug{t}", [128, 130], BF16) for t in range(16)]
        rcol = sbuf("rcol", [128, 2], F32)

        arenaB = sbuf("arenaB", [128, 12288], BF16)
        qTt = arenaB[:, 0:2048]
        kTt = arenaB[:, 2048:4096]
        ATt = [[arenaB[:, 4096 + (h * 4 + i) * 384: 4096 + (h * 4 + i + 1) * 384]
                for i in range(4)] for h in range(2)]
        aoTs = [arenaB[:, i * 512:(i + 1) * 512] for i in range(8)]
        wobf = [arenaB[:, 4096 + i * 1024:4096 + (i + 1) * 1024] for i in range(8)]
        arenaB8 = arenaB.bitcast(FP8)
        xq8 = [arenaB8[:, i * S:(i + 1) * S] for i in range(8)]
        wq8 = [arenaB8[:, 16384 + i * 1024:16384 + (i + 1) * 1024] for i in range(8)]
        arenaBf = arenaB.bitcast(F32)
        xbt_b = [arenaB[:, i * 512:(i + 1) * 512] for i in range(8)]

        big = sbuf("big", [128, 16384], BF16)
        att_o = [big[:, i * 1024:(i + 1) * 1024] for i in range(16)]
        gt = [big[:, i * 512:(i + 1) * 512] for i in range(32)]
        bigf = big.bitcast(F32)
        xfs = [bigf[:, i * S:(i + 1) * S] for i in range(4)]

        po = [es.enter_context(nc.psum_tensor(f"po{i}", [128, 512], F32)) for i in range(8)]
        poa = [p.ap() for p in po]
        pob = [p.ap().bitcast(BF16) for p in po]

        for name in ["sy", "sy0", "sy1", "sy2", "sy3", "sq", "pe", "ve", "se", "ge", "pl",
                     "c0a", "c0b", "c0c", "c1a", "c1b", "c1c"]:
            ctx.sems[name] = es.enter_context(nc.semaphore(name))

        def mm(eng, out, lhsT, rhs, start, stop):
            return eng.matmul(out, lhsT, rhs, start=start, stop=stop, skip_group_check=True)

        PH = ["attn_a", "ffn1", "attn_b", "ffn2"][:n_phases]
        wj = [0]

        def preload(E, eng):
            if E == "sync":
                for c in range(8):
                    ctx.inc(eng.dma_start(out=xt[c][:, :], in_=x_in.ap()[128 * c:128 * (c + 1), :]), "pl", 16)
                ctx.inc(eng.dma_start(out=maskt[:, :], in_=mask_in.ap()), "pl", 16)
                ctx.inc(eng.dma_start(out=ident[:, :], in_=id_in.ap()), "pl", 16)
                ctx.inc(eng.dma_start(out=onescol[:, :], in_=ones_in.ap()), "pl", 16)
                ctx.inc(eng.dma_start(out=bias_sb[:, :], in_=bias_in.ap()), "pl", 16)
                ctx.inc(eng.dma_start(out=identf[:, :], in_=idf_in.ap()), "pl", 16)
                ctx.inc(eng.dma_start(out=onesr[:, :], in_=onesr_in.ap()), "pl", 16)
                ctx.mark("ld0", "pl")
            if E == "vector":
                ctx.wait(eng, "ld0")
                for t in range(16):
                    eng.memset(vaug[t][:, 64:65], 1.0)
                    eng.memset(vaug[t][:, 129:130], 1.0)
                ctx.inc(eng.memset(zc[:, :], 0.0), "ve")
                ctx.mark("init_ve", "ve")

        def ln_stats_tc(E, eng, tag, tc, ready):
            sl = slice(512 * tc, 512 * (tc + 1))
            sq4 = sqb + [xbc[:, 0:512], xbc[:, 512:1024]]
            if E == "vector":
                ctx.wait(eng, ready)
                if tc > 0:
                    ctx.wait(eng, f"{tag}.smm{(tc - 1) * 8 + 7}")
                for c in range(8):
                    ctx.inc(eng.tensor_copy(out=xbas[c][:, :], in_=xt[c][:, sl]), "ve")
                ctx.mark(f"{tag}.xc{tc}", "ve")
            for c in range(8):
                j = tc * 8 + c
                if E == "scalar":
                    if c == 0:
                        ctx.wait(eng, ready)
                    if j >= 4:
                        ctx.wait(eng, f"{tag}.smm{j - 4}")
                    ctx.inc(eng.activation(out=sq4[c % 4][:, :], in_=xt[c][:, sl], func=AF.Square), "se")
                    ctx.mark(f"{tag}.sq{j}", "se")
                if E == "tensor":
                    if c == 0:
                        ctx.wait(eng, f"{tag}.xc{tc}")
                        if tc > 0:
                            ctx.wait(eng, f"{tag}.sev{tc - 1}")
                    ctx.wait(eng, f"{tag}.sq{j}")
                    mm(eng, poa[4][0:1, :], onescol[:, :], xbas[c][:, :], c == 0, c == 7)
                    ctx.inc(mm(eng, poa[5][0:1, :], onescol[:, :], sq4[c % 4][:, :], c == 0, c == 7), "pe")
                    ctx.mark(f"{tag}.smm{j}", "pe")
            if E == "vector":
                ctx.wait(eng, f"{tag}.smm{tc * 8 + 7}")
                eng.tensor_scalar(out=m_row[:, sl], in0=poa[4][0:1, :], scalar1=1.0 / D, scalar2=None, op0=ALU.mult)
                ctx.inc(eng.tensor_scalar(out=t_row[:, sl], in0=poa[5][0:1, :], scalar1=1.0 / D, scalar2=None, op0=ALU.mult), "ve")
                ctx.mark(f"{tag}.sev{tc}", "ve")

        def ln_stats_fin(E, eng, tag, want_mB, want_rB):
            if E == "vector":
                ctx.wait(eng, f"{tag}.sev3")
                ctx.inc(eng.tensor_tensor(out=r_row[:, :], in0=m_row[:, :], in1=m_row[:, :], op=ALU.mult), "ve")
                ctx.mark(f"{tag}.vmm", "ve")
                ctx.wait(eng, f"{tag}.vmm")
                ctx.inc(eng.scalar_tensor_tensor(out=r_row[:, :], in0=t_row[:, :], scalar=EPS,
                                                 in1=r_row[:, :], op0=ALU.add, op1=ALU.subtract), "ve")
                ctx.mark(f"{tag}.var", "ve")
            if E == "scalar":
                ctx.wait(eng, f"{tag}.var")
                ctx.inc(eng.activation(out=r_row[:, :], in_=r_row[:, :], func=AF.Ln), "se")
                ctx.mark(f"{tag}.rln", "se")
                ctx.wait(eng, f"{tag}.rln")
                ctx.inc(eng.activation(out=r_row[:, :], in_=r_row[:, :], func=AF.Exp, scale=-0.5), "se")
                ctx.mark(f"{tag}.r", "se")
                if want_mB:
                    ctx.wait(eng, f"{tag}.sev3")
                    ctx.inc(eng.dma_start(out=rb_h.ap(), in_=m_row[:, :]), "sq", 16)
                    ctx.mark(f"{tag}.mbo", "sq")
                    ctx.wait(eng, f"{tag}.mbo")
                    ctx.inc(eng.dma_start(out=mB[:, :], in_=bass.AP(rb_h, 0, [[0, 128], [1, S]])), "sq", 16)
                    ctx.mark(f"{tag}.mB", "sq")
                if want_rB:
                    ctx.wait(eng, f"{tag}.r")
                    if want_mB:
                        ctx.wait(eng, f"{tag}.mB")
                    ctx.inc(eng.dma_start(out=rb_h.ap(), in_=r_row[:, :]), "sq", 16)
                    ctx.mark(f"{tag}.rbo", "sq")
                    ctx.wait(eng, f"{tag}.rbo")
                    ctx.inc(eng.dma_start(out=rB[:, :], in_=bass.AP(rb_h, 0, [[0, 128], [1, S]])), "sq", 16)
                    ctx.mark(f"{tag}.rB", "sq")

        def ln_stats(E, eng, tag, want_mbf, want_mB, want_rB, pe_pre=None, se_pre=None):
            for tc in range(4):
                ln_stats_tc(E, eng, tag, tc, se_pre or "ld0")
            ln_stats_fin(E, eng, tag, want_mB, want_rB)

        def attn_a(E, eng):
            tag = "a1"
            ln_stats(E, eng, tag, want_mbf=False, want_mB=True, want_rB=True)
            def wcasts(eng, hh):
                for wi in range(3):
                    j = wj[0] + hh * 3 + wi
                    ctx.wait(eng, f"w{j}.ld")
                    if hh > 0:
                        ctx.wait(eng, f"{tag}.pfold{hh - 1}_3")
                    ctx.inc(eng.tensor_copy(out=wbf[wi][:, :], in_=wld[j % 4][:, :]), "ve")
                    ctx.mark(f"w{j}.cast", "ve")
                    ctx.mark(f"w{j}.free", "ve")

            def xba0(eng, hh):
                if hh == 0:
                    ctx.wait(eng, f"{tag}.mB")
                else:
                    ctx.wait(eng, f"{tag}.pfold{hh - 1}_3")
                for c in range(8):
                    ctx.inc(eng.tensor_tensor(out=xbas[c][:, :], in0=xt[c][:, 0:512], in1=mB[:, 0:512], op=ALU.subtract), "ve")
                ctx.mark(f"{tag}.pc{hh}_0", "ve")

            for hh in range(8):
                for wi, wsrc in enumerate((wqb, wkb, wvb)):
                    j = wj[0] + hh * 3 + wi
                    if E == "sync":
                        if j >= 4:
                            ctx.wait(eng, f"w{j - 4}.free")
                        ctx.inc(eng.dma_start(out=wld[j % 4][:, :], in_=wsrc.ap()[hh]), f"sy{j % 4}", 16)
                        ctx.mark(f"w{j}.ld", f"sy{j % 4}")
                if E == "vector" and hh == 0:
                    wcasts(eng, 0)
                    xba0(eng, 0)

                for tc in range(4):
                    sl = slice(512 * tc, 512 * (tc + 1))
                    if E == "vector" and tc > 0:
                        ctx.wait(eng, f"{tag}.pfold{hh}_{tc - 1}")
                        for c in range(8):
                            ctx.inc(eng.tensor_tensor(out=xbas[c][:, :], in0=xt[c][:, sl], in1=mB[:, sl], op=ALU.subtract), "ve")
                        ctx.mark(f"{tag}.pc{hh}_{tc}", "ve")
                    pb = 3 * (tc % 2)
                    if E == "tensor":
                        ctx.wait(eng, f"{tag}.pc{hh}_{tc}")
                        if tc >= 2:
                            ctx.wait(eng, f"{tag}.pev{hh}_{tc - 2}")
                        elif hh > 0:
                            ctx.wait(eng, f"{tag}.pev{hh - 1}_{tc + 2}")
                            if tc == 1:
                                ctx.wait(eng, f"{tag}.avev{hh - 1}_1_15")
                        if tc == 0:
                            for wi in range(3):
                                ctx.wait(eng, f"w{wj[0] + hh * 3 + wi}.cast")
                        for c in range(8):
                            for wi in range(3):
                                last = mm(eng, poa[pb + wi][:, :], wbf[wi][:, 128 * c:128 * (c + 1)], xbas[c][:, :],
                                          c == 0, c == 7)
                        ctx.inc(last, "pe")
                        ctx.mark(f"{tag}.pfold{hh}_{tc}", "pe")
                    if E == "vector":
                        ctx.wait(eng, f"{tag}.pfold{hh}_{tc}")
                        if hh == 0 and tc == 0:
                            ctx.wait(eng, f"{tag}.rB")
                        if hh > 0 and tc == 0:
                            ctx.wait(eng, f"{tag}.qk{hh - 1}_1_15")
                        if tc > 0:
                            ctx.wait(eng, f"{tag}.vtr{hh}_{tc * 4 - 1}")
                        eng.tensor_tensor(out=qTt[:, sl], in0=poa[pb + 0][:, :], in1=rB[:, sl], op=ALU.mult)
                        eng.tensor_tensor(out=kTt[:, sl], in0=poa[pb + 1][:, :], in1=rB[:, sl], op=ALU.mult)
                        ctx.inc(eng.tensor_tensor(out=vTt[:, :], in0=poa[pb + 2][:, :], in1=rB[:, sl], op=ALU.mult), "ve")
                        ctx.mark(f"{tag}.pev{hh}_{tc}", "ve")
                    for b in range(4):
                        t = tc * 4 + b
                        if E == "tensor":
                            ctx.wait(eng, f"{tag}.pev{hh}_{tc}")
                            if t == 0 and hh > 0:
                                ctx.wait(eng, f"{tag}.avev{hh - 1}_1_15")
                            if t >= 2:
                                ctx.wait(eng, f"{tag}.vev{hh}_{t - 2}")
                            ctx.inc(eng.transpose(pob[6 + (b % 2)][:, 0:128], vTt[:, 128 * b:128 * (b + 1)], ident[:, :]), "pe")
                            ctx.mark(f"{tag}.vtr{hh}_{t}", "pe")
                        if E == "vector":
                            ctx.wait(eng, f"{tag}.vtr{hh}_{t}")
                            if t == 0 and hh > 0:
                                ctx.wait(eng, f"{tag}.avden{hh - 1}_1_15")
                            eng.tensor_copy(out=vaug[t][:, 0:64], in_=pob[6 + (b % 2)][:, 0:64])
                            ctx.inc(eng.tensor_copy(out=vaug[t][:, 65:129], in_=pob[6 + (b % 2)][:, 64:128]), "ve")
                            ctx.mark(f"{tag}.vev{hh}_{t}", "ve")
                if E == "vector" and hh < 7:
                    wcasts(eng, hh + 1)
                    xba0(eng, hh + 1)
                for kb in range(18):
                    qb = kb - 2
                    w = min(384, S - 128 * kb) if kb <= 15 else 0
                    for h in range(2):
                        hp = 64 * h
                        if kb <= 15:
                            if E == "tensor":
                                if kb >= 2:
                                    ctx.wait(eng, f"{tag}.exp{hh}_{h}_{kb - 2}")
                                elif hh > 0:
                                    ctx.wait(eng, f"{tag}.exp{hh - 1}_{h}_{14 + kb}")
                                if kb == 0 and h == 0:
                                    ctx.wait(eng, f"{tag}.pev{hh}_3")
                                    ctx.wait(eng, f"{tag}.vev{hh}_15")
                                ctx.inc(mm(eng, poa[h * 2 + kb % 2][:, 0:w],
                                           kTt[hp:hp + 64, 128 * kb:128 * (kb + 1)],
                                           qTt[hp:hp + 64, 128 * kb:128 * kb + w],
                                           True, True), "pe")
                                ctx.mark(f"{tag}.qk{hh}_{h}_{kb}", "pe")
                            if E == "scalar":
                                ctx.wait(eng, f"{tag}.qk{hh}_{h}_{kb}")
                                if kb >= 4:
                                    ctx.wait(eng, f"{tag}.avden{hh}_{h}_{kb - 2}")
                                elif hh > 0:
                                    ctx.wait(eng, f"{tag}.avden{hh - 1}_{h}_{min(15, kb + 14)}")
                                ctx.inc(eng.activation(out=ATt[h][kb % 4][:, 0:w], in_=poa[h * 2 + kb % 2][:, 0:w],
                                                       func=AF.Exp, scale=0.125), "se")
                                ctx.mark(f"{tag}.exp{hh}_{h}_{kb}", "se")
                            if E == "gpsimd":
                                ctx.wait(eng, f"{tag}.exp{hh}_{h}_{kb}")
                                eng.tensor_mul(out=ATt[h][kb % 4][:, 0:128], in0=ATt[h][kb % 4][:, 0:128], in1=maskt[:, 0:128])
                                if w == 384:
                                    eng.tensor_mul(out=ATt[h][kb % 4][:, 256:384], in0=ATt[h][kb % 4][:, 256:384], in1=maskt[:, 256:384])
                                ctx.inc(eng.engine_nop(), "ge")
                                ctx.mark(f"{tag}.msk{hh}_{h}_{kb}", "ge")
                        if 0 <= qb <= 15:
                            slot = 4 + h if qb % 2 == 0 else 6 + h
                            if E == "tensor":
                                ctx.wait(eng, f"{tag}.msk{hh}_{h}_{qb}")
                                if qb >= 2:
                                    ctx.wait(eng, f"{tag}.avev{hh}_{h}_{qb - 2}")
                                elif hh > 0:
                                    ctx.wait(eng, f"{tag}.avev{hh - 1}_{h}_{qb + 14}")
                                k0 = max(0, qb - 2)
                                for kc in range(k0, qb + 1):
                                    last = mm(eng, poa[slot][:, 0:65],
                                              ATt[h][kc % 4][:, 128 * (qb - kc):128 * (qb - kc + 1)],
                                              vaug[kc][:, 65 * h:65 * (h + 1)],
                                              kc == k0, kc == qb)
                                ctx.inc(last, "pe")
                                ctx.mark(f"{tag}.avden{hh}_{h}_{qb}", "pe")
                            if E == "vector":
                                ctx.wait(eng, f"{tag}.avden{hh}_{h}_{qb}")
                                if qb >= 1:
                                    ctx.wait(eng, f"{tag}.avev{hh}_{h}_{qb - 1}")
                                elif hh > 0:
                                    ctx.wait(eng, f"{tag}.avev{hh - 1}_{h}_15")
                                ctx.inc(eng.reciprocal(out=rcol[:, h:h + 1], in_=poa[slot][:, 64:65]), "ve")
                                ctx.mark(f"{tag}.rc{hh}_{h}_{qb}", "ve")
                            if E == "scalar":
                                ctx.wait(eng, f"{tag}.rc{hh}_{h}_{qb}")
                                hidx = 2 * hh + h
                                ctx.inc(eng.activation(out=att_o[qb][:, 64 * hidx:64 * (hidx + 1)],
                                                       in_=poa[slot][:, 0:64], func=AF.Copy,
                                                       scale=rcol[:, h:h + 1]), "se")
                                ctx.mark(f"{tag}.avev{hh}_{h}_{qb}", "se")
            wj[0] += 24
            for ec in range(8):
                j = wj[0] + ec
                if E == "sync":
                    ctx.wait(eng, f"w{j - 4}.free")
                    ctx.inc(eng.dma_start(out=wld[j % 4][:, :], in_=wob.ap()[ec]), f"sy{j % 4}", 16)
                    ctx.mark(f"w{j}.ld", f"sy{j % 4}")
                if E == "vector":
                    ctx.wait(eng, f"w{j}.ld")
                    if ec == 0:
                        ctx.wait(eng, f"{tag}.avden7_1_15")
                    ctx.inc(eng.tensor_copy(out=wobf[ec][:, :], in_=wld[j % 4][:, :]), "ve")
                    ctx.mark(f"w{j}.cast", "ve")
                    ctx.mark(f"w{j}.free", "ve")
            for tg in range(4):
                gsl = slice(512 * tg, 512 * (tg + 1))
                for bq in range(4):
                    qb = tg * 4 + bq
                    for ec in range(8):
                        t = qb * 8 + ec
                        tl = bq * 8 + ec
                        if E == "tensor":
                            if t == 0:
                                ctx.wait(eng, f"{tag}.avev7_1_15")
                            if t >= 2:
                                ctx.wait(eng, f"{tag}.wtev{t - 2}")
                            if tl < 2 and tg > 0:
                                ctx.wait(eng, f"{tag}.woev{tg - 1}_7")
                            ctx.inc(eng.transpose(pob[2 + t % 2][:, 0:128], att_o[qb][:, 128 * ec:128 * (ec + 1)], ident[:, :]), "pe")
                            ctx.mark(f"{tag}.wtr{t}", "pe")
                        if E == "vector":
                            ctx.wait(eng, f"{tag}.wtr{t}")
                            if t == 0:
                                ctx.wait(eng, f"{tag}.pev7_3")
                            ctx.inc(eng.tensor_copy(out=aoTs[ec][:, 128 * bq:128 * (bq + 1)], in_=pob[2 + t % 2][:, 0:128]), "ve")
                            ctx.mark(f"{tag}.wtev{t}", "ve")
                for ec in range(8):
                    if E == "tensor":
                        ctx.wait(eng, f"{tag}.wtev{(tg * 4 + 3) * 8 + 7}")
                        ctx.wait(eng, f"w{wj[0] + ec}.cast")
                        u = tg * 8 + ec
                        if u >= 2:
                            ctx.wait(eng, f"{tag}.woev{(u - 2) // 8}_{(u - 2) % 8}")
                        for c in range(8):
                            last = mm(eng, poa[ec % 2][:, :], wobf[ec][:, 128 * c:128 * (c + 1)], aoTs[c][:, :],
                                      c == 0, c == 7)
                        ctx.inc(last, "pe")
                        ctx.mark(f"{tag}.womm{tg}_{ec}", "pe")
                    if E == "vector":
                        ctx.wait(eng, f"{tag}.womm{tg}_{ec}")
                        ctx.inc(eng.tensor_tensor(out=xt[ec][:, gsl], in0=poa[ec % 2][:, :], in1=xt[ec][:, gsl], op=ALU.add), "ve")
                        ctx.mark(f"{tag}.woev{tg}_{ec}", "ve")
                if "ffn1" in PH:
                    ln_stats_tc(E, eng, "f0", tg, f"{tag}.woev{tg}_7")
            wj[0] += 8

        def ffn(E, eng, fi):
            tag = f"f{fi}"
            ln_stats_fin(E, eng, tag, want_mB=True, want_rB=True)
            if fi == 0:
                for o in range(8):
                    j = wj[0] + o
                    if E == "sync":
                        ctx.wait(eng, f"w{j - 4}.free")
                        ctx.inc(eng.dma_start(out=wld[j % 4][:, :], in_=wfb.ap()[o]), f"sy{j % 4}", 16)
                        ctx.mark(f"w{j}.ld", f"sy{j % 4}")
                    if E == "vector":
                        ctx.wait(eng, f"w{j}.ld")
                        if o == 0:
                            ctx.inc(eng.tensor_reduce(out=amax_col[:, :], in_=wld[j % 4][:, :], axis=AX.X, op=ALU.max, apply_absolute_value=True), "ve")
                            ctx.mark(f"w{j}.free", "ve")
                            ctx.wait(eng, f"w{j}.free")
                        else:
                            ctx.inc(eng.tensor_reduce(out=scol[:, 3:4], in_=wld[j % 4][:, :], axis=AX.X, op=ALU.max, apply_absolute_value=True), "ve")
                            ctx.mark(f"{tag}.wred{o}", "ve")
                            ctx.wait(eng, f"{tag}.wred{o}")
                            ctx.inc(eng.tensor_tensor(out=amax_col[:, :], in0=amax_col[:, :], in1=scol[:, 3:4], op=ALU.max), "ve")
                            ctx.mark(f"w{j}.free", "ve")
                            ctx.wait(eng, f"w{j}.free")
                if E == "tensor":
                    ctx.wait(eng, f"w{wj[0] + 7}.free")
                    ctx.inc(eng.transpose(poa[4][0:1, 0:128], amax_col[:, :], identf[:, :]), "pe")
                    ctx.mark(f"{tag}.wtr", "pe")
                if E == "vector":
                    ctx.wait(eng, f"{tag}.wtr")
                    ctx.inc(eng.tensor_reduce(out=smalls[0:1, 2:3], in_=poa[4][0:1, 0:128], axis=AX.X, op=ALU.max), "ve")
                    ctx.mark(f"{tag}.gw", "ve")
                    ctx.wait(eng, f"{tag}.gw")
                wj[0] += 8
            for c in range(8):
                half = "vector" if c < 4 else "gpsimd"
                sem = "ve" if c < 4 else "ge"
                xfc = xfs[c % 2 + (0 if c < 4 else 2)]
                if E == half:
                    if c in (0, 4):
                        ctx.wait(eng, f"{tag}.mB")
                        ctx.wait(eng, f"{tag}.rB")
                        if fi == 0 and c == 0:
                            ctx.wait(eng, f"{tag}.wtr")
                    if c >= 6:
                        ctx.wait(eng, f"{tag}.axr{c - 2}")
                    ctx.inc(eng.tensor_tensor(out=xfc[:, :], in0=xt[c][:, :], in1=mB[:, :], op=ALU.subtract), sem)
                    ctx.mark(f"{tag}.axs{c}", sem)
                    ctx.wait(eng, f"{tag}.axs{c}")
                    ctx.inc(eng.tensor_tensor(out=xfc[:, :], in0=xfc[:, :], in1=rB[:, :], op=ALU.mult), sem)
                    ctx.mark(f"{tag}.axm{c}", sem)
                    ctx.wait(eng, f"{tag}.axm{c}")
                if E == "vector":
                    ctx.wait(eng, f"{tag}.axm{c}")
                    ctx.inc(eng.tensor_reduce(out=scol[:, 3:4], in_=xfc[:, :], axis=AX.X, op=ALU.max, apply_absolute_value=True), "ve")
                    ctx.mark(f"{tag}.axr{c}", "ve")
                    ctx.wait(eng, f"{tag}.axr{c}")
                    if c == 0:
                        ctx.inc(eng.tensor_copy(out=amax_col[:, :], in_=scol[:, 3:4]), "ve")
                    else:
                        ctx.inc(eng.tensor_tensor(out=amax_col[:, :], in0=amax_col[:, :], in1=scol[:, 3:4], op=ALU.max), "ve")
                    ctx.mark(f"{tag}.ax{c}", "ve")
                    ctx.wait(eng, f"{tag}.ax{c}")
            if E == "vector":
                ctx.mark(f"{tag}.axall", "ve")
            if E == "tensor":
                ctx.wait(eng, f"{tag}.axall")
                ctx.inc(eng.transpose(poa[4][0:1, 0:128], amax_col[:, :], identf[:, :]), "pe")
                ctx.mark(f"{tag}.axtr", "pe")
            if E == "vector":
                ctx.wait(eng, f"{tag}.axtr")
                ctx.inc(eng.tensor_reduce(out=smalls[0:1, 0:1], in_=poa[4][0:1, 0:128], axis=AX.X, op=ALU.max), "ve")
                ctx.mark(f"{tag}.lred", "ve")
                ctx.wait(eng, f"{tag}.lred")
                ctx.wait(eng, "init_ve")
                ctx.inc(eng.tensor_copy(out=zc[0:1, 0:1], in_=smalls[0:1, 0:1]), "ve")
                ctx.mark(f"{tag}.lmax", "ve")
            ca, cb, cc2 = f"c{fi}a", f"c{fi}b", f"c{fi}c"
            if E == "gpsimd":
                ctx.wait(eng, f"{tag}.lmax")
                ctx.inc(eng.dma_start(out=cc_in[fi].ap(), in_=zc[:, :]), ca, 16)
                ctx.mark(f"{tag}.cci", ca)
                ctx.wait(eng, f"{tag}.cci")
                ctx.inc(eng.collective_compute("AllReduce", ALU.max, replica_groups=[list(range(8))],
                                               ins=[cc_in[fi].ap().opt()], outs=[cc_out[fi].ap().opt()]), cc2, 1)
                ctx.mark(f"{tag}.ccc", cc2)
                ctx.wait(eng, f"{tag}.ccc")
                ctx.inc(eng.dma_start(out=smalls[0:1, 1:2], in_=cc_out[fi].ap()[0:1, 0:1]), cb, 16)
                ctx.mark(f"{tag}.gmo", cb)
                ctx.wait(eng, f"{tag}.gmo")
                ctx.inc(eng.engine_nop(), "ge")
                ctx.mark(f"{tag}.gmax", "ge")
            for c in range(4):
                xfc = xfs[c]
                if E == "vector":
                    ctx.inc(eng.tensor_tensor(out=xfc[:, :], in0=xt[c][:, :], in1=mB[:, :], op=ALU.subtract), "ve")
                    ctx.mark(f"{tag}.xs{c}", "ve")
                    ctx.wait(eng, f"{tag}.xs{c}")
                    ctx.inc(eng.tensor_tensor(out=xfc[:, :], in0=xfc[:, :], in1=rB[:, :], op=ALU.mult), "ve")
                    ctx.mark(f"{tag}.xn{c}", "ve")
            if E == "vector":
                ctx.wait(eng, f"{tag}.gmax")
                if fi == 0:
                    ctx.wait(eng, f"{tag}.gw")
                steps = [
                    lambda: eng.reciprocal(out=smalls[0:1, 4:5], in_=smalls[0:1, 1:2]),
                    lambda: eng.tensor_scalar(out=smalls[0:1, 5:6], in0=smalls[0:1, 4:5], scalar1=224.0, scalar2=None, op0=ALU.mult),
                    lambda: eng.tensor_tensor(out=smalls[0:1, 3:4], in0=smalls[0:1, 1:2], in1=smalls[0:1, 2:3], op=ALU.mult),
                    lambda: eng.tensor_scalar(out=smalls[0:1, 3:4], in0=smalls[0:1, 3:4], scalar1=GSC, scalar2=None, op0=ALU.mult),
                    lambda: eng.reciprocal(out=smalls[0:1, 6:7], in_=smalls[0:1, 2:3]),
                    lambda: eng.tensor_scalar(out=smalls[0:1, 7:8], in0=smalls[0:1, 6:7], scalar1=224.0, scalar2=None, op0=ALU.mult),
                ]
                for si, f in enumerate(steps):
                    ctx.inc(f(), "ve")
                    ctx.mark(f"{tag}.scl{si}", "ve")
                    ctx.wait(eng, f"{tag}.scl{si}")
                ctx.mark(f"{tag}.scales", "ve")
            for si, (sidx, col) in enumerate([(5, 0), (3, 1), (7, 2)]):
                if E == "vector":
                    if si == 0:
                        ctx.wait(eng, f"{tag}.scales")
                    ctx.inc(eng.tensor_scalar(out=t_row[0:1, 0:128], in0=onesr[:, :],
                                               scalar1=smalls[0:1, sidx:sidx + 1], scalar2=None, op0=ALU.mult), "ve")
                    ctx.mark(f"{tag}.bro{si}", "ve")
                if E == "tensor":
                    ctx.wait(eng, f"{tag}.bro{si}")
                    ctx.inc(eng.transpose(poa[4][:, 0:1], t_row[0:1, 0:128], identf[0:1, 0:1]), "pe")
                    ctx.mark(f"{tag}.brt{si}", "pe")
                if E == "vector":
                    ctx.wait(eng, f"{tag}.brt{si}")
                    ctx.inc(eng.tensor_copy(out=scol[:, col:col + 1], in_=poa[4][:, 0:1]), "ve")
                    ctx.mark(f"{tag}.brc{si}", "ve")
                if E == "tensor":
                    ctx.wait(eng, f"{tag}.brc{si}")
                    eng.nop()
            if E == "vector":
                ctx.mark(f"{tag}.sc1", "ve")
                ctx.mark(f"{tag}.sc2", "ve")
            for c in range(8):
                xfc = xfs[c % 4]
                if E == "vector" and c >= 4:
                    ctx.wait(eng, f"{tag}.xq{c - 4}")
                    ctx.inc(eng.tensor_tensor(out=xfc[:, :], in0=xt[c][:, :], in1=mB[:, :], op=ALU.subtract), "ve")
                    ctx.mark(f"{tag}.xs{c}", "ve")
                    ctx.wait(eng, f"{tag}.xs{c}")
                    ctx.inc(eng.tensor_tensor(out=xfc[:, :], in0=xfc[:, :], in1=rB[:, :], op=ALU.mult), "ve")
                    ctx.mark(f"{tag}.xn{c}", "ve")
                ceng = "scalar" if c % 2 == 0 else "vector"
                csem = "se" if c % 2 == 0 else "ve"
                if E == ceng:
                    ctx.wait(eng, f"{tag}.xn{c}")
                    if c in (0, 1):
                        ctx.wait(eng, f"{tag}.sc1")
                    if ceng == "scalar":
                        ctx.inc(eng.activation(out=xq8[c][:, :], in_=xfc[:, :], func=AF.Copy, scale=scol[:, 0:1]), csem)
                    else:
                        ctx.inc(eng.tensor_scalar(out=xq8[c][:, :], in0=xfc[:, :], scalar1=scol[:, 0:1], scalar2=None, op0=ALU.mult), csem)
                    ctx.mark(f"{tag}.xq{c}", csem)
            for o in range(8):
                j = wj[0] + o
                if E == "sync":
                    ctx.wait(eng, f"w{j - 4}.free")
                    ctx.inc(eng.dma_start(out=wld[j % 4][:, :], in_=wfb.ap()[o]), f"sy{j % 4}", 16)
                    ctx.mark(f"w{j}.ld", f"sy{j % 4}")
                ceng = "scalar" if o % 2 == 0 else "vector"
                csem = "se" if o % 2 == 0 else "ve"
                if E == ceng:
                    ctx.wait(eng, f"w{j}.ld")
                    if o in (0, 1):
                        ctx.wait(eng, f"{tag}.sc1")
                        if fi == 0:
                            ctx.wait(eng, "a1.woevall")
                    if ceng == "scalar":
                        ctx.inc(eng.activation(out=wq8[o][:, :], in_=wld[j % 4][:, :], func=AF.Copy, scale=scol[:, 2:3]), csem)
                    else:
                        ctx.inc(eng.tensor_scalar(out=wq8[o][:, :], in0=wld[j % 4][:, :], scalar1=scol[:, 2:3], scalar2=None, op0=ALU.mult), csem)
                    ctx.mark(f"w{j}.free", csem)
                    ctx.mark(f"{tag}.wq{o}", csem)
            wj[0] += 8
            for tc in range(4):
                for o in range(8):
                    u = tc * 8 + o
                    sl = slice(512 * tc, 512 * (tc + 1))
                    if E == "tensor":
                        if u == 0:
                            ctx.wait(eng, f"{tag}.wq7")
                            if fi == 1:
                                ctx.wait(eng, "b.avevall")
                            else:
                                ctx.wait(eng, "a1.woevall")
                        if u >= 2:
                            ctx.wait(eng, f"{tag}.fev{u - 2}")
                        for c in range(8):
                            last = mm(eng, poa[o % 2][:, :], wq8[o][:, 128 * c:128 * (c + 1)], xq8[c][:, sl],
                                      c == 0, c == 7)
                        ctx.inc(last, "pe")
                        ctx.mark(f"{tag}.fmm{u}", "pe")
                    if E == "scalar":
                        ctx.wait(eng, f"{tag}.fmm{u}")
                        if u == 0:
                            ctx.wait(eng, f"{tag}.sc2")
                        if u >= 2:
                            ctx.wait(eng, f"{tag}.fev{u - 2}")
                        ctx.inc(eng.activation(out=gtf[o % 2][:, :], in_=poa[o % 2][:, :], func=AF.Copy,
                                               scale=scol[:, 1:2]), "se")
                        ctx.mark(f"{tag}.fep{u}", "se")
                    if E == "vector":
                        ctx.wait(eng, f"{tag}.fep{u}")
                        ctx.inc(eng.scalar_tensor_tensor(out=xt[o][:, sl], in0=gtf[o % 2][:, :],
                                                         scalar=bias_sb[:, o:o + 1], in1=xt[o][:, sl],
                                                         op0=ALU.add, op1=ALU.add), "ve")
                        ctx.mark(f"{tag}.fev{u}", "ve")
                if fi == 0 and "attn_b" in PH:
                    ln_stats_tc(E, eng, "b", tc, f"{tag}.fev{tc * 8 + 7}")
            if E == "vector":
                ctx.mark(f"{tag}.fevall", "ve")

        def attn_b(E, eng):
            tag = "b"
            ln_stats_fin(E, eng, tag, want_mB=True, want_rB=False)
            for tt in range(4):
                ttsl = slice(512 * tt, 512 * (tt + 1))
                for c in range(8):
                    if E == "vector":
                        if c == 0:
                            if tt == 0:
                                ctx.wait(eng, f"{tag}.mB")
                            else:
                                ctx.wait(eng, f"{tag}.smmdone{tt - 1}")
                        ctx.inc(eng.tensor_tensor(out=xbt_b[c][:, :], in0=xt[c][:, ttsl], in1=mB[:, ttsl], op=ALU.subtract), "ve")
                        ctx.mark(f"{tag}.xb{tt}_{c}", "ve")
                stride = 64 if BCUT >= 4 else 32
                kbase = wj[0] + tt * stride
                vbase = kbase + 32
                for n in range(32):
                    j = kbase + n
                    if E == "sync":
                        ctx.wait(eng, f"w{j - 4}.free")
                        ctx.inc(eng.dma_start(out=wld[j % 4][:, :], in_=knbd.ap()[n]), f"sy{j % 4}", 16)
                        ctx.mark(f"w{j}.ld", f"sy{j % 4}")
                    if E == "vector":
                        ctx.wait(eng, f"w{j}.ld")
                        if n >= 2:
                            ctx.wait(eng, f"{tag}.smm{tt}_{n - 2}")
                        elif tt > 0:
                            ctx.wait(eng, f"{tag}.avmm{tt - 1}_{30 + n}" if BCUT >= 4 else f"{tag}.smm{tt - 1}_{30 + n}")
                        ctx.inc(eng.tensor_copy(out=wbf[n % 2][:, :], in_=wld[j % 4][:, :]), "ve")
                        ctx.mark(f"w{j}.cast", "ve")
                        ctx.mark(f"w{j}.free", "ve")
                    if E == "tensor":
                        if n == 0:
                            for c in range(8):
                                ctx.wait(eng, f"{tag}.xb{tt}_{c}")
                            if tt == 0:
                                ctx.wait(eng, "f0.fevall")
                            elif BCUT >= 4:
                                ctx.wait(eng, f"{tag}.avevall{tt - 1}")
                        ctx.wait(eng, f"w{j}.cast")
                        if n >= 2:
                            ctx.wait(eng, f"{tag}.scp{tt}_{n - 2}")
                            if BCUT >= 2:
                                ctx.wait(eng, f"{tag}.ssq{tt}_{n - 2}")
                        for c in range(8):
                            last = mm(eng, poa[n % 2][:, :], wbf[n % 2][:, 128 * c:128 * (c + 1)], xbt_b[c][:, :],
                                      c == 0, c == 7)
                        ctx.inc(last, "pe")
                        ctx.mark(f"{tag}.smm{tt}_{n}", "pe")
                    if E == "scalar" and BCUT >= 2:
                        ctx.wait(eng, f"{tag}.scp{tt}_{n}")
                        if n >= 4 and B2SUB >= 1:
                            ctx.wait(eng, f"{tag}.n2mm{tt}_{n - 4}")
                        ctx.inc(eng.activation(out=sq4b[n % 4][:, :], in_=gt[n][:, :], func=AF.Square), "se")
                        ctx.mark(f"{tag}.ssq{tt}_{n}", "se")
                    if E == "vector":
                        ctx.wait(eng, f"{tag}.smm{tt}_{n}")
                        ctx.inc(eng.tensor_copy(out=gt[n][:, :], in_=poa[n % 2][:, :]), "ve")
                        ctx.mark(f"{tag}.scp{tt}_{n}", "ve")
                    if E == "tensor" and BCUT >= 2 and B2SUB >= 1:
                        for nd in ([n - 3] if n >= 3 else []) + ([29, 30, 31] if n == 31 else []):
                            ctx.wait(eng, f"{tag}.ssq{tt}_{nd}")
                            ctx.inc(mm(eng, poa[6][0:1, :], onescol[:, :], sq4b[nd % 4][:, :],
                                       nd == 0, nd == 31), "pe")
                            ctx.mark(f"{tag}.n2mm{tt}_{nd}", "pe")
                if E == "tensor":
                    ctx.mark(f"{tag}.smmdone{tt}", "pe")
                if E == "scalar" and BCUT >= 2 and B2SUB >= 2:
                    ctx.wait(eng, f"{tag}.n2mm{tt}_31")
                    ctx.inc(eng.activation(out=r_row[:, 0:512], in_=poa[6][0:1, :], func=AF.Ln), "se")
                    ctx.mark(f"{tag}.cl{tt}", "se")
                    ctx.wait(eng, f"{tag}.cl{tt}")
                    ctx.inc(eng.activation(out=r_row[:, 0:512], in_=r_row[:, 0:512], func=AF.Exp, scale=-0.5), "se")
                    ctx.mark(f"{tag}.ce{tt}", "se")
                    ctx.wait(eng, f"{tag}.ce{tt}")
                    ctx.inc(eng.mul(out=r_row[:, 0:512], in_=r_row[:, 0:512], mul=64.0), "se")
                    ctx.mark(f"{tag}.c{tt}", "se")
                    ctx.wait(eng, f"{tag}.c{tt}")
                    if B2SUB >= 3:
                        if tt > 0 and BCUT >= 3:
                            ctx.wait(eng, f"{tag}.gmulall{tt - 1}")
                        ctx.inc(eng.dma_start(out=rb_h.ap()[0:1, 0:512], in_=r_row[:, 0:512]), "sq", 16)
                        ctx.mark(f"{tag}.cbo{tt}", "sq")
                        ctx.wait(eng, f"{tag}.cbo{tt}")
                        ctx.inc(eng.dma_start(out=cB[:, :], in_=bass.AP(rb_h, 0, [[0, 128], [1, 512]])), "sq", 16)
                        ctx.mark(f"{tag}.cB{tt}", "sq")
                for n in range(32):
                    if E == "vector" and BCUT >= 3:
                        if n == 0:
                            ctx.wait(eng, f"{tag}.cB{tt}")
                        if n >= 2:
                            ctx.wait(eng, f"{tag}.gel{tt}_{n - 2}")
                        ctx.inc(eng.tensor_tensor(out=gtf[n % 2][:, :], in0=gt[n][:, :], in1=cB[:, :], op=ALU.mult), "ve")
                        ctx.mark(f"{tag}.gmul{tt}_{n}", "ve")
                    if E == "scalar" and BCUT >= 3:
                        ctx.wait(eng, f"{tag}.gmul{tt}_{n}")
                        ctx.inc(eng.activation(out=gt[n][:, :], in_=gtf[n % 2][:, :], func=AF.Gelu), "se")
                        ctx.mark(f"{tag}.gel{tt}_{n}", "se")
                    if BCUT >= 4:
                        j = vbase + n
                        if E == "sync":
                            ctx.wait(eng, f"w{j - 4}.free")
                            ctx.inc(eng.dma_start(out=wld[j % 4][:, :], in_=vnbd.ap()[n]), f"sy{j % 4}", 16)
                            ctx.mark(f"w{j}.ld", f"sy{j % 4}")
                        if E == "scalar":
                            ctx.wait(eng, f"w{j}.ld")
                            if n >= 2:
                                ctx.wait(eng, f"{tag}.avmm{tt}_{n - 2}")
                            else:
                                ctx.wait(eng, f"{tag}.smm{tt}_{30 + n}")
                            ctx.inc(eng.activation(out=wbf[n % 2][:, :], in_=wld[j % 4][:, :], func=AF.Copy), "se")
                            ctx.mark(f"w{j}.cast", "se")
                            ctx.mark(f"w{j}.free", "se")
                        if E == "tensor":
                            ctx.wait(eng, f"w{j}.cast")
                            ctx.wait(eng, f"{tag}.gel{tt}_{n}")
                            for dc in range(8):
                                last = mm(eng, poa[dc][:, :], wbf[n % 2][:, 128 * dc:128 * (dc + 1)], gt[n][:, :],
                                          n == 0, n == 31)
                            ctx.inc(last, "pe")
                            ctx.mark(f"{tag}.avmm{tt}_{n}", "pe")
                if E == "vector" and BCUT >= 3:
                    ctx.mark(f"{tag}.gmulall{tt}", "ve")
                if E == "tensor" and tt == 3 and BCUT >= 4:
                    ctx.mark(f"{tag}.avdone", "pe")
                for dc in range(8 if BCUT >= 4 else 0):
                    if E == "vector":
                        ctx.wait(eng, f"{tag}.avmm{tt}_31")
                        ctx.inc(eng.tensor_tensor(out=xt[dc][:, ttsl], in0=poa[dc][:, :], in1=xt[dc][:, ttsl], op=ALU.add), "ve")
                        ctx.mark(f"{tag}.avev{tt}_{dc}", "ve")
                if E == "vector" and BCUT >= 4:
                    ctx.mark(f"{tag}.avevall{tt}", "ve")
                if BCUT >= 4 and "ffn2" in PH:
                    ln_stats_tc(E, eng, "f1", tt, f"{tag}.avev{tt}_7")
            wj[0] += 256 if BCUT >= 4 else 128
            if E == "vector" and BCUT >= 4:
                ctx.mark(f"{tag}.avevall", "ve")

        def store(E, eng, final_mark):
            if E == "sync":
                ctx.wait(eng, final_mark)
                for c in range(8):
                    ctx.inc(eng.dma_start(out=out_t.ap()[128 * c:128 * (c + 1), :], in_=xt[c][:, :]), "sy", 16)
                if DEBUG:
                    ctx.inc(eng.dma_start(out=dbg1.ap()[0:1, 0:16], in_=smalls[:, :]), "sy", 16)
                    ctx.inc(eng.dma_start(out=dbg1.ap()[0:1, 16:16 + S], in_=m_row), "sy", 16)
                    ctx.inc(eng.dma_start(out=dbg1.ap()[0:1, 16 + S:16 + 2 * S], in_=r_row), "sy", 16)
                    ctx.inc(eng.dma_start(out=dbg1.ap()[0:1, 16 + 2 * S:16 + 3 * S], in_=t_row), "sy", 16)
                    ctx.inc(eng.dma_start(out=dbgs.ap(), in_=scol[:, :]), "sy", 16)
                    ctx.inc(eng.dma_start(out=dbga.ap(), in_=amax_col[:, :]), "sy", 16)
                    ctx.inc(eng.dma_start(out=dbgm.ap(), in_=mB[:, 0:8]), "sy", 16)
                    ctx.inc(eng.dma_start(out=dbgr.ap(), in_=rB[:, 0:8]), "sy", 16)
                ctx.mark("st_done", "sy")
                ctx.wait(eng, "st_done")
            if E == "gpsimd":
                ctx.wait(eng, "st_done")

        def program(E, eng):
            wj[0] = 0
            preload(E, eng)
            final = None
            if "attn_a" in PH:
                attn_a(E, eng)
                if E == "vector":
                    ctx.mark("a1.woevall", "ve")
                if E == "tensor":
                    ctx.mark("a1.woevall_pe", "pe")
                final = "a1.woevall"
            if "ffn1" in PH:
                ffn(E, eng, 0)
                if E == "tensor":
                    ctx.mark("f0.fevall_pe", "pe")
                final = "f0.fevall"
            if "attn_b" in PH:
                attn_b(E, eng)
                final = "b.avevall" if BCUT >= 4 else "f0.fevall"
            if "ffn2" in PH:
                ffn(E, eng, 1)
                final = "f1.fevall"
            store(E, eng, final)

        ctx.emit = False
        for E in ["sync", "tensor", "vector", "scalar", "gpsimd"]:
            program(E, _DummyEngine())
        counted = dict(ctx.cnt)

        ctx.emit = True
        with nc.Block() as block:
            for ename, deco in (("sync", block.sync), ("tensor", block.tensor),
                                ("vector", block.vector), ("scalar", block.scalar),
                                ("gpsimd", block.gpsimd)):
                def run_engine(eng, _ename=ename):
                    ctx.cnt = {}
                    program(_ename, eng)
                deco(run_engine)
        ctx.cnt = counted
    return nc



def _hhblock(wT):
    blk = wT.reshape(8, 128, 8, 128).transpose(2, 1, 0, 3).reshape(8, 128, 1024)
    return np.ascontiguousarray(blk)


def _consts():
    i = np.arange(128)
    causal = (i[:, None] <= i[None, :]).astype(ml_dtypes.bfloat16)
    anti = (i[:, None] >= i[None, :] + 1).astype(ml_dtypes.bfloat16)
    mask = np.concatenate([causal, np.ones((128, 128), ml_dtypes.bfloat16), anti], axis=1)
    ident = np.eye(128, dtype=ml_dtypes.bfloat16)
    ones = np.ones((128, 1), ml_dtypes.bfloat16)
    return mask, ident, ones


def make_inputs(L0_attn_a_x, L0_attn_a_Wq, L0_attn_a_Wk, L0_attn_a_Wv, L0_attn_a_Wo,
                L0_attn_b_key_param_tokens, L0_attn_b_value_param_tokens,
                L0_ffn_weight, L0_ffn_bias):
    x = np.asarray(L0_attn_a_x, dtype=np.float32)
    wq = _hhblock(np.ascontiguousarray(np.asarray(L0_attn_a_Wq, np.float32).T))
    wk = _hhblock(np.ascontiguousarray(np.asarray(L0_attn_a_Wk, np.float32).T))
    wv = _hhblock(np.ascontiguousarray(np.asarray(L0_attn_a_Wv, np.float32).T))
    wo = _hhblock(np.ascontiguousarray(np.asarray(L0_attn_a_Wo, np.float32).T))
    wf = _hhblock(np.ascontiguousarray(np.asarray(L0_ffn_weight, np.float32).T))
    kT = np.ascontiguousarray(np.asarray(L0_attn_b_key_param_tokens, np.float32).T)
    knb = np.ascontiguousarray(kT.reshape(8, 128, 32, 128).transpose(2, 1, 0, 3).reshape(32, 128, 1024))
    vnb = np.ascontiguousarray(np.asarray(L0_attn_b_value_param_tokens, np.float32).reshape(32, 128, 1024))
    bias = np.ascontiguousarray(np.asarray(L0_ffn_bias, np.float32).reshape(8, 128).T)
    mask, ident, ones = _consts()
    identf = np.eye(128, dtype=np.float32)
    onesr = np.ones((1, 128), np.float32)
    in_maps = []
    for b in range(8):
        in_maps.append({
            "x_in": np.ascontiguousarray(x[b].T),
            "wqb": wq, "wkb": wk, "wvb": wv, "wob": wo, "wfb": wf,
            "knbd": knb, "vnbd": vnb, "bias_in": bias,
            "mask_in": mask, "id_in": ident, "ones_in": ones,
            "idf_in": identf, "onesr_in": onesr,
        })
    return in_maps


def kernel(**inputs):
    in_maps = make_inputs(**inputs)
    nc = build_nc(4)
    res = run_bass_kernel_spmd(nc, in_maps, core_ids=list(range(8)))
    out = np.stack([np.asarray(res.results[b]["out"]).T for b in range(8)])
    return np.ascontiguousarray(out)


# version 09
# speedup vs baseline: 1.1804x; 1.0813x over previous
import contextlib
import numpy as np
import ml_dtypes

import concourse.bass as bass
import concourse.mybir as mybir
from concourse.bass_utils import run_bass_kernel_spmd

F32 = mybir.dt.float32
BF16 = mybir.dt.bfloat16
FP8 = mybir.dt.float8e4
AF = mybir.ActivationFunctionType
ALU = mybir.AluOpType
AX = mybir.AxisListType

S, D = 2048, 1024
LN64 = float(np.log(64.0))
EPS = 1e-5
GSC = 4.0 / (448.0 * 448.0)
DEBUG = False
BCUT = 4
B2SUB = 3


class _DummyInstr:
    def then_inc(self, *a, **k):
        return self


class _DummyEngine:
    def __getattr__(self, name):
        def f(*a, **k):
            return _DummyInstr()
        return f


class Ctx:
    def __init__(self):
        self.emit = False
        self.cnt = {}
        self.cp = {}
        self.sems = {}

    def inc(self, instr, sem, n=1):
        self.cnt[sem] = self.cnt.get(sem, 0) + n
        if self.emit:
            instr.then_inc(self.sems[sem], n)
        return instr

    def mark(self, name, sem):
        cur = (sem, self.cnt.get(sem, 0))
        if not self.emit:
            assert name not in self.cp, f"dup checkpoint {name}"
            self.cp[name] = cur
        else:
            assert self.cp.get(name) == cur, (name, self.cp.get(name), cur)

    def wait(self, eng, name):
        if self.emit and not isinstance(eng, _DummyEngine):
            sem, val = self.cp[name]
            if val > 0:
                eng.wait_ge(self.sems[sem], val)


def build_nc(n_phases=4):
    nc = bass.Bass("TRN2", target_bir_lowering=False, debug=False)

    x_in = nc.dram_tensor("x_in", [D, S], F32, kind="ExternalInput")
    wqb = nc.dram_tensor("wqb", [8, 128, 1024], F32, kind="ExternalInput")
    wkb = nc.dram_tensor("wkb", [8, 128, 1024], F32, kind="ExternalInput")
    wvb = nc.dram_tensor("wvb", [8, 128, 1024], F32, kind="ExternalInput")
    wob = nc.dram_tensor("wob", [8, 128, 1024], F32, kind="ExternalInput")
    wfb = nc.dram_tensor("wfb", [8, 128, 1024], F32, kind="ExternalInput")
    knbd = nc.dram_tensor("knbd", [32, 128, 1024], F32, kind="ExternalInput")
    vnbd = nc.dram_tensor("vnbd", [32, 128, 1024], F32, kind="ExternalInput")
    bias_in = nc.dram_tensor("bias_in", [128, 8], F32, kind="ExternalInput")
    mask_in = nc.dram_tensor("mask_in", [128, 384], BF16, kind="ExternalInput")
    id_in = nc.dram_tensor("id_in", [128, 128], BF16, kind="ExternalInput")
    ones_in = nc.dram_tensor("ones_in", [128, 1], BF16, kind="ExternalInput")
    idf_in = nc.dram_tensor("idf_in", [128, 128], F32, kind="ExternalInput")
    onesr_in = nc.dram_tensor("onesr_in", [1, 128], F32, kind="ExternalInput")
    out_t = nc.dram_tensor("out", [D, S], F32, kind="ExternalOutput")
    if DEBUG:
        dbg1 = nc.dram_tensor("dbg1", [1, 16 + 3 * S], F32, kind="ExternalOutput")
        dbgs = nc.dram_tensor("dbgs", [128, 4], F32, kind="ExternalOutput")
        dbga = nc.dram_tensor("dbga", [128, 1], F32, kind="ExternalOutput")
        dbgm = nc.dram_tensor("dbgm", [128, 8], F32, kind="ExternalOutput")
        dbgr = nc.dram_tensor("dbgr", [128, 8], F32, kind="ExternalOutput")

    rb_h = nc.dram_tensor("rbounce", [1, S], F32)
    cb_h = nc.dram_tensor("cbounce", [128, 1], F32)
    sb_h = nc.dram_tensor("sbounce", [1, 4], F32)
    cc_in = [nc.dram_tensor(f"cc_in{i}", [128, 128], F32) for i in range(2)]
    cc_out = [nc.dram_tensor(f"cc_out{i}", [128, 128], F32) for i in range(2)]

    ctx = Ctx()
    es = contextlib.ExitStack()
    sbuf = lambda name, shape, dt: es.enter_context(nc.sbuf_tensor(name, shape, dt))

    with es:
        xt = [sbuf(f"xt{c}", [128, S], F32) for c in range(8)]
        maskt = sbuf("maskt", [128, 384], BF16)
        ident = sbuf("ident", [128, 128], BF16)
        onescol = sbuf("onescol", [128, 1], BF16)
        identf = sbuf("identf", [128, 128], F32)
        onesr = sbuf("onesr", [1, 128], F32)
        bias_sb = sbuf("bias_sb", [128, 8], F32)
        rows = sbuf("rows", [1, 3 * S], F32)
        m_row = rows[0:1, 0:S]
        r_row = rows[0:1, S:2 * S]
        t_row = rows[0:1, 2 * S:3 * S]
        smalls = sbuf("smalls", [1, 16], F32)
        scol = sbuf("scol", [128, 4], F32)
        amax_col = sbuf("amax_col", [128, 1], F32)
        zc = sbuf("zc", [128, 128], F32)
        aux = sbuf("aux", [128, 2 * S], F32)
        mB = aux[:, 0:S]
        rB = aux[:, S:2 * S]
        cB = aux[:, S:S + 512]
        gtf = [aux[:, S + 512 + i * 512:S + 512 + (i + 1) * 512] for i in range(2)]
        sqb2 = sbuf("sqb2", [128, 1024], BF16)
        sqb = [sqb2[:, 0:512], sqb2[:, 512:1024]]
        sq4b = None
        xbc = sbuf("xbc", [128, 1024], BF16)
        xbcs = [xbc[:, 0:512], xbc[:, 512:1024]]
        sq4b = sqb + xbcs
        xba = sbuf("xba", [128, 4096], BF16)
        xbas = [xba[:, i * 512:(i + 1) * 512] for i in range(8)]
        wld = [sbuf(f"wld{i}", [128, 1024], F32) for i in range(4)]
        wbf = [sbuf(f"wbf{i}", [128, 1024], BF16) for i in range(3)]
        vTt = sbuf("vTt", [128, 512], BF16)
        vaug = [sbuf(f"vaug{t}", [128, 130], BF16) for t in range(16)]
        rcol = sbuf("rcol", [128, 2], F32)

        arenaB = sbuf("arenaB", [128, 12288], BF16)
        qTt = arenaB[:, 0:2048]
        kTt = arenaB[:, 2048:4096]
        ATt = [[arenaB[:, 4096 + (h * 4 + i) * 384: 4096 + (h * 4 + i + 1) * 384]
                for i in range(4)] for h in range(2)]
        aoTs = [arenaB[:, i * 512:(i + 1) * 512] for i in range(8)]
        wobf = [arenaB[:, 4096 + i * 1024:4096 + (i + 1) * 1024] for i in range(8)]
        arenaB8 = arenaB.bitcast(FP8)
        xq8 = [arenaB8[:, i * S:(i + 1) * S] for i in range(8)]
        wq8 = [arenaB8[:, 16384 + i * 1024:16384 + (i + 1) * 1024] for i in range(8)]
        arenaBf = arenaB.bitcast(F32)
        xbt_b = [arenaB[:, i * 512:(i + 1) * 512] for i in range(8)]

        big = sbuf("big", [128, 16384], BF16)
        att_o = [big[:, i * 1024:(i + 1) * 1024] for i in range(16)]
        gt = [big[:, i * 512:(i + 1) * 512] for i in range(32)]
        bigf = big.bitcast(F32)
        xfs = [bigf[:, i * S:(i + 1) * S] for i in range(4)]

        po = [es.enter_context(nc.psum_tensor(f"po{i}", [128, 512], F32)) for i in range(8)]
        poa = [p.ap() for p in po]
        pob = [p.ap().bitcast(BF16) for p in po]

        for name in ["sy", "sy0", "sy1", "sy2", "sy3", "sq", "pe", "ve", "se", "ge", "pl",
                     "c0a", "c0b", "c0c", "c1a", "c1b", "c1c"]:
            ctx.sems[name] = es.enter_context(nc.semaphore(name))

        def mm(eng, out, lhsT, rhs, start, stop):
            return eng.matmul(out, lhsT, rhs, start=start, stop=stop, skip_group_check=True)

        PH = ["attn_a", "ffn1", "attn_b", "ffn2"][:n_phases]
        wj = [0]

        def preload(E, eng):
            if E == "sync":
                for c in range(8):
                    ctx.inc(eng.dma_start(out=xt[c][:, :], in_=x_in.ap()[128 * c:128 * (c + 1), :]), "pl", 16)
                ctx.inc(eng.dma_start(out=maskt[:, :], in_=mask_in.ap()), "pl", 16)
                ctx.inc(eng.dma_start(out=ident[:, :], in_=id_in.ap()), "pl", 16)
                ctx.inc(eng.dma_start(out=onescol[:, :], in_=ones_in.ap()), "pl", 16)
                ctx.inc(eng.dma_start(out=bias_sb[:, :], in_=bias_in.ap()), "pl", 16)
                ctx.inc(eng.dma_start(out=identf[:, :], in_=idf_in.ap()), "pl", 16)
                ctx.inc(eng.dma_start(out=onesr[:, :], in_=onesr_in.ap()), "pl", 16)
                ctx.mark("ld0", "pl")
            if E == "vector":
                ctx.wait(eng, "ld0")
                for t in range(16):
                    eng.memset(vaug[t][:, 64:65], 1.0)
                    eng.memset(vaug[t][:, 129:130], 1.0)
                ctx.inc(eng.memset(zc[:, :], 0.0), "ve")
                ctx.mark("init_ve", "ve")

        def ln_stats_tc(E, eng, tag, tc, ready):
            sl = slice(512 * tc, 512 * (tc + 1))
            sq4 = sqb + [xbc[:, 0:512], xbc[:, 512:1024]]
            if E == "vector":
                ctx.wait(eng, ready)
                if tc > 0:
                    ctx.wait(eng, f"{tag}.smm{(tc - 1) * 8 + 7}")
                for c in range(8):
                    ctx.inc(eng.tensor_copy(out=xbas[c][:, :], in_=xt[c][:, sl]), "ve")
                ctx.mark(f"{tag}.xc{tc}", "ve")
            for c in range(8):
                j = tc * 8 + c
                if E == "scalar":
                    if c == 0:
                        ctx.wait(eng, ready)
                    if j >= 4:
                        ctx.wait(eng, f"{tag}.smm{j - 4}")
                    ctx.inc(eng.activation(out=sq4[c % 4][:, :], in_=xt[c][:, sl], func=AF.Square), "se")
                    ctx.mark(f"{tag}.sq{j}", "se")
                if E == "tensor":
                    if c == 0:
                        ctx.wait(eng, f"{tag}.xc{tc}")
                        if tc > 0:
                            ctx.wait(eng, f"{tag}.sev{tc - 1}")
                    ctx.wait(eng, f"{tag}.sq{j}")
                    mm(eng, poa[4][0:1, :], onescol[:, :], xbas[c][:, :], c == 0, c == 7)
                    ctx.inc(mm(eng, poa[5][0:1, :], onescol[:, :], sq4[c % 4][:, :], c == 0, c == 7), "pe")
                    ctx.mark(f"{tag}.smm{j}", "pe")
            if E == "vector":
                ctx.wait(eng, f"{tag}.smm{tc * 8 + 7}")
                eng.tensor_scalar(out=m_row[:, sl], in0=poa[4][0:1, :], scalar1=1.0 / D, scalar2=None, op0=ALU.mult)
                ctx.inc(eng.tensor_scalar(out=t_row[:, sl], in0=poa[5][0:1, :], scalar1=1.0 / D, scalar2=None, op0=ALU.mult), "ve")
                ctx.mark(f"{tag}.sev{tc}", "ve")

        def ln_stats_fin(E, eng, tag, want_mB, want_rB):
            if E == "vector":
                ctx.wait(eng, f"{tag}.sev3")
                ctx.inc(eng.tensor_tensor(out=r_row[:, :], in0=m_row[:, :], in1=m_row[:, :], op=ALU.mult), "ve")
                ctx.mark(f"{tag}.vmm", "ve")
                ctx.wait(eng, f"{tag}.vmm")
                ctx.inc(eng.scalar_tensor_tensor(out=r_row[:, :], in0=t_row[:, :], scalar=EPS,
                                                 in1=r_row[:, :], op0=ALU.add, op1=ALU.subtract), "ve")
                ctx.mark(f"{tag}.var", "ve")
            if E == "scalar":
                ctx.wait(eng, f"{tag}.var")
                ctx.inc(eng.activation(out=r_row[:, :], in_=r_row[:, :], func=AF.Ln), "se")
                ctx.mark(f"{tag}.rln", "se")
                ctx.wait(eng, f"{tag}.rln")
                ctx.inc(eng.activation(out=r_row[:, :], in_=r_row[:, :], func=AF.Exp, scale=-0.5), "se")
                ctx.mark(f"{tag}.r", "se")
                if want_mB:
                    ctx.wait(eng, f"{tag}.sev3")
                    ctx.inc(eng.dma_start(out=rb_h.ap(), in_=m_row[:, :]), "sq", 16)
                    ctx.mark(f"{tag}.mbo", "sq")
                    ctx.wait(eng, f"{tag}.mbo")
                    ctx.inc(eng.dma_start(out=mB[:, :], in_=bass.AP(rb_h, 0, [[0, 128], [1, S]])), "sq", 16)
                    ctx.mark(f"{tag}.mB", "sq")
                if want_rB:
                    ctx.wait(eng, f"{tag}.r")
                    if want_mB:
                        ctx.wait(eng, f"{tag}.mB")
                    ctx.inc(eng.dma_start(out=rb_h.ap(), in_=r_row[:, :]), "sq", 16)
                    ctx.mark(f"{tag}.rbo", "sq")
                    ctx.wait(eng, f"{tag}.rbo")
                    ctx.inc(eng.dma_start(out=rB[:, :], in_=bass.AP(rb_h, 0, [[0, 128], [1, S]])), "sq", 16)
                    ctx.mark(f"{tag}.rB", "sq")

        def ln_stats(E, eng, tag, want_mbf, want_mB, want_rB, pe_pre=None, se_pre=None):
            for tc in range(4):
                ln_stats_tc(E, eng, tag, tc, se_pre or "ld0")
            ln_stats_fin(E, eng, tag, want_mB, want_rB)

        def attn_a(E, eng):
            tag = "a1"
            ln_stats(E, eng, tag, want_mbf=False, want_mB=True, want_rB=True)
            def wcasts(eng, hh):
                for wi in range(3):
                    j = wj[0] + hh * 3 + wi
                    ctx.wait(eng, f"w{j}.ld")
                    if hh > 0:
                        ctx.wait(eng, f"{tag}.pfold{hh - 1}_3")
                    ctx.inc(eng.tensor_copy(out=wbf[wi][:, :], in_=wld[j % 4][:, :]), "ve")
                    ctx.mark(f"w{j}.cast", "ve")
                    ctx.mark(f"w{j}.free", "ve")

            def xba0(eng, hh):
                if hh == 0:
                    ctx.wait(eng, f"{tag}.mB")
                else:
                    ctx.wait(eng, f"{tag}.pfold{hh - 1}_3")
                for c in range(8):
                    ctx.inc(eng.tensor_tensor(out=xbas[c][:, :], in0=xt[c][:, 0:512], in1=mB[:, 0:512], op=ALU.subtract), "ve")
                ctx.mark(f"{tag}.pc{hh}_0", "ve")

            for hh in range(8):
                for wi, wsrc in enumerate((wqb, wkb, wvb)):
                    j = wj[0] + hh * 3 + wi
                    if E == "sync":
                        if j >= 4:
                            ctx.wait(eng, f"w{j - 4}.free")
                        ctx.inc(eng.dma_start(out=wld[j % 4][:, :], in_=wsrc.ap()[hh]), f"sy{j % 4}", 16)
                        ctx.mark(f"w{j}.ld", f"sy{j % 4}")
                if E == "vector" and hh == 0:
                    wcasts(eng, 0)
                    xba0(eng, 0)

                for tc in range(4):
                    sl = slice(512 * tc, 512 * (tc + 1))
                    if E == "vector" and tc > 0:
                        ctx.wait(eng, f"{tag}.pfold{hh}_{tc - 1}")
                        for c in range(8):
                            ctx.inc(eng.tensor_tensor(out=xbas[c][:, :], in0=xt[c][:, sl], in1=mB[:, sl], op=ALU.subtract), "ve")
                        ctx.mark(f"{tag}.pc{hh}_{tc}", "ve")
                    pb = 3 * (tc % 2)
                    if E == "tensor":
                        ctx.wait(eng, f"{tag}.pc{hh}_{tc}")
                        if tc >= 2:
                            ctx.wait(eng, f"{tag}.pev{hh}_{tc - 2}")
                        elif hh > 0:
                            ctx.wait(eng, f"{tag}.pev{hh - 1}_{tc + 2}")
                            if tc == 1:
                                ctx.wait(eng, f"{tag}.avev{hh - 1}_1_15")
                        if tc == 0:
                            for wi in range(3):
                                ctx.wait(eng, f"w{wj[0] + hh * 3 + wi}.cast")
                        for c in range(8):
                            for wi in range(3):
                                last = mm(eng, poa[pb + wi][:, :], wbf[wi][:, 128 * c:128 * (c + 1)], xbas[c][:, :],
                                          c == 0, c == 7)
                        ctx.inc(last, "pe")
                        ctx.mark(f"{tag}.pfold{hh}_{tc}", "pe")
                    if E == "vector":
                        ctx.wait(eng, f"{tag}.pfold{hh}_{tc}")
                        if hh == 0 and tc == 0:
                            ctx.wait(eng, f"{tag}.rB")
                        if hh > 0 and tc == 0:
                            ctx.wait(eng, f"{tag}.qk{hh - 1}_1_15")
                        if tc > 0:
                            ctx.wait(eng, f"{tag}.vtr{hh}_{tc * 4 - 1}")
                        eng.tensor_tensor(out=qTt[:, sl], in0=poa[pb + 0][:, :], in1=rB[:, sl], op=ALU.mult)
                        eng.tensor_tensor(out=kTt[:, sl], in0=poa[pb + 1][:, :], in1=rB[:, sl], op=ALU.mult)
                        ctx.inc(eng.tensor_tensor(out=vTt[:, :], in0=poa[pb + 2][:, :], in1=rB[:, sl], op=ALU.mult), "ve")
                        ctx.mark(f"{tag}.pev{hh}_{tc}", "ve")
                    for b in range(4):
                        t = tc * 4 + b
                        if E == "tensor":
                            ctx.wait(eng, f"{tag}.pev{hh}_{tc}")
                            if t == 0 and hh > 0:
                                ctx.wait(eng, f"{tag}.avev{hh - 1}_1_15")
                            if t >= 2:
                                ctx.wait(eng, f"{tag}.vev{hh}_{t - 2}")
                            ctx.inc(eng.transpose(pob[6 + (b % 2)][:, 0:128], vTt[:, 128 * b:128 * (b + 1)], ident[:, :]), "pe")
                            ctx.mark(f"{tag}.vtr{hh}_{t}", "pe")
                        if E == "vector":
                            ctx.wait(eng, f"{tag}.vtr{hh}_{t}")
                            if t == 0 and hh > 0:
                                ctx.wait(eng, f"{tag}.avden{hh - 1}_1_15")
                            eng.tensor_copy(out=vaug[t][:, 0:64], in_=pob[6 + (b % 2)][:, 0:64])
                            ctx.inc(eng.tensor_copy(out=vaug[t][:, 65:129], in_=pob[6 + (b % 2)][:, 64:128]), "ve")
                            ctx.mark(f"{tag}.vev{hh}_{t}", "ve")
                if E == "vector" and hh < 7:
                    wcasts(eng, hh + 1)
                    xba0(eng, hh + 1)
                for kb in range(18):
                    qb = kb - 2
                    w = min(384, S - 128 * kb) if kb <= 15 else 0
                    for h in range(2):
                        hp = 64 * h
                        if kb <= 15:
                            if E == "tensor":
                                if kb >= 2:
                                    ctx.wait(eng, f"{tag}.exp{hh}_{h}_{kb - 2}")
                                elif hh > 0:
                                    ctx.wait(eng, f"{tag}.exp{hh - 1}_{h}_{14 + kb}")
                                if kb == 0 and h == 0:
                                    ctx.wait(eng, f"{tag}.pev{hh}_3")
                                    ctx.wait(eng, f"{tag}.vev{hh}_15")
                                ctx.inc(mm(eng, poa[h * 2 + kb % 2][:, 0:w],
                                           kTt[hp:hp + 64, 128 * kb:128 * (kb + 1)],
                                           qTt[hp:hp + 64, 128 * kb:128 * kb + w],
                                           True, True), "pe")
                                ctx.mark(f"{tag}.qk{hh}_{h}_{kb}", "pe")
                            if E == "scalar":
                                ctx.wait(eng, f"{tag}.qk{hh}_{h}_{kb}")
                                if kb >= 4:
                                    ctx.wait(eng, f"{tag}.avden{hh}_{h}_{kb - 2}")
                                elif hh > 0:
                                    ctx.wait(eng, f"{tag}.avden{hh - 1}_{h}_{min(15, kb + 14)}")
                                ctx.inc(eng.activation(out=ATt[h][kb % 4][:, 0:w], in_=poa[h * 2 + kb % 2][:, 0:w],
                                                       func=AF.Exp, scale=0.125), "se")
                                ctx.mark(f"{tag}.exp{hh}_{h}_{kb}", "se")
                            if E == "gpsimd":
                                ctx.wait(eng, f"{tag}.exp{hh}_{h}_{kb}")
                                eng.tensor_mul(out=ATt[h][kb % 4][:, 0:128], in0=ATt[h][kb % 4][:, 0:128], in1=maskt[:, 0:128])
                                if w == 384:
                                    eng.tensor_mul(out=ATt[h][kb % 4][:, 256:384], in0=ATt[h][kb % 4][:, 256:384], in1=maskt[:, 256:384])
                                ctx.inc(eng.engine_nop(), "ge")
                                ctx.mark(f"{tag}.msk{hh}_{h}_{kb}", "ge")
                        if 0 <= qb <= 15:
                            slot = 4 + h if qb % 2 == 0 else 6 + h
                            if E == "tensor":
                                ctx.wait(eng, f"{tag}.msk{hh}_{h}_{qb}")
                                if qb >= 2:
                                    ctx.wait(eng, f"{tag}.avev{hh}_{h}_{qb - 2}")
                                elif hh > 0:
                                    ctx.wait(eng, f"{tag}.avev{hh - 1}_{h}_{qb + 14}")
                                k0 = max(0, qb - 2)
                                for kc in range(k0, qb + 1):
                                    last = mm(eng, poa[slot][:, 0:65],
                                              ATt[h][kc % 4][:, 128 * (qb - kc):128 * (qb - kc + 1)],
                                              vaug[kc][:, 65 * h:65 * (h + 1)],
                                              kc == k0, kc == qb)
                                ctx.inc(last, "pe")
                                ctx.mark(f"{tag}.avden{hh}_{h}_{qb}", "pe")
                            if E == "vector":
                                ctx.wait(eng, f"{tag}.avden{hh}_{h}_{qb}")
                                if qb >= 1:
                                    ctx.wait(eng, f"{tag}.avev{hh}_{h}_{qb - 1}")
                                elif hh > 0:
                                    ctx.wait(eng, f"{tag}.avev{hh - 1}_{h}_15")
                                ctx.inc(eng.reciprocal(out=rcol[:, h:h + 1], in_=poa[slot][:, 64:65]), "ve")
                                ctx.mark(f"{tag}.rc{hh}_{h}_{qb}", "ve")
                            if E == "scalar":
                                ctx.wait(eng, f"{tag}.rc{hh}_{h}_{qb}")
                                hidx = 2 * hh + h
                                ctx.inc(eng.activation(out=att_o[qb][:, 64 * hidx:64 * (hidx + 1)],
                                                       in_=poa[slot][:, 0:64], func=AF.Copy,
                                                       scale=rcol[:, h:h + 1]), "se")
                                ctx.mark(f"{tag}.avev{hh}_{h}_{qb}", "se")
            wj[0] += 24
            for ec in range(8):
                j = wj[0] + ec
                if E == "sync":
                    ctx.wait(eng, f"w{j - 4}.free")
                    ctx.inc(eng.dma_start(out=wld[j % 4][:, :], in_=wob.ap()[ec]), f"sy{j % 4}", 16)
                    ctx.mark(f"w{j}.ld", f"sy{j % 4}")
                if E == "vector":
                    ctx.wait(eng, f"w{j}.ld")
                    if ec == 0:
                        ctx.wait(eng, f"{tag}.avden7_1_15")
                    ctx.inc(eng.tensor_copy(out=wobf[ec][:, :], in_=wld[j % 4][:, :]), "ve")
                    ctx.mark(f"w{j}.cast", "ve")
                    ctx.mark(f"w{j}.free", "ve")
            for tg in range(4):
                gsl = slice(512 * tg, 512 * (tg + 1))
                for bq in range(4):
                    qb = tg * 4 + bq
                    for ec in range(8):
                        t = qb * 8 + ec
                        tl = bq * 8 + ec
                        if E == "tensor":
                            if t == 0:
                                ctx.wait(eng, f"{tag}.avev7_1_15")
                            if t >= 2:
                                ctx.wait(eng, f"{tag}.wtev{t - 2}")
                            if tl < 2 and tg > 0:
                                ctx.wait(eng, f"{tag}.woev{tg - 1}_7")
                            ctx.inc(eng.transpose(pob[2 + t % 2][:, 0:128], att_o[qb][:, 128 * ec:128 * (ec + 1)], ident[:, :]), "pe")
                            ctx.mark(f"{tag}.wtr{t}", "pe")
                        if E == "vector":
                            ctx.wait(eng, f"{tag}.wtr{t}")
                            if t == 0:
                                ctx.wait(eng, f"{tag}.pev7_3")
                            ctx.inc(eng.tensor_copy(out=aoTs[ec][:, 128 * bq:128 * (bq + 1)], in_=pob[2 + t % 2][:, 0:128]), "ve")
                            ctx.mark(f"{tag}.wtev{t}", "ve")
                for ec in range(8):
                    if E == "tensor":
                        ctx.wait(eng, f"{tag}.wtev{(tg * 4 + 3) * 8 + 7}")
                        ctx.wait(eng, f"w{wj[0] + ec}.cast")
                        u = tg * 8 + ec
                        if u >= 2:
                            ctx.wait(eng, f"{tag}.woev{(u - 2) // 8}_{(u - 2) % 8}")
                        for c in range(8):
                            last = mm(eng, poa[ec % 2][:, :], wobf[ec][:, 128 * c:128 * (c + 1)], aoTs[c][:, :],
                                      c == 0, c == 7)
                        ctx.inc(last, "pe")
                        ctx.mark(f"{tag}.womm{tg}_{ec}", "pe")
                    if E == "vector":
                        ctx.wait(eng, f"{tag}.womm{tg}_{ec}")
                        ctx.inc(eng.tensor_tensor(out=xt[ec][:, gsl], in0=poa[ec % 2][:, :], in1=xt[ec][:, gsl], op=ALU.add), "ve")
                        ctx.mark(f"{tag}.woev{tg}_{ec}", "ve")
                if "ffn1" in PH:
                    ln_stats_tc(E, eng, "f0", tg, f"{tag}.woev{tg}_7")
            wj[0] += 8

        def ffn(E, eng, fi):
            tag = f"f{fi}"
            ln_stats_fin(E, eng, tag, want_mB=True, want_rB=True)
            if fi == 0:
                for o in range(8):
                    j = wj[0] + o
                    if E == "sync":
                        ctx.wait(eng, f"w{j - 4}.free")
                        ctx.inc(eng.dma_start(out=wld[j % 4][:, :], in_=wfb.ap()[o]), f"sy{j % 4}", 16)
                        ctx.mark(f"w{j}.ld", f"sy{j % 4}")
                    if E == "vector":
                        ctx.wait(eng, f"w{j}.ld")
                        if o == 0:
                            ctx.inc(eng.tensor_reduce(out=amax_col[:, :], in_=wld[j % 4][:, :], axis=AX.X, op=ALU.max, apply_absolute_value=True), "ve")
                            ctx.mark(f"w{j}.free", "ve")
                            ctx.wait(eng, f"w{j}.free")
                        else:
                            ctx.inc(eng.tensor_reduce(out=scol[:, 3:4], in_=wld[j % 4][:, :], axis=AX.X, op=ALU.max, apply_absolute_value=True), "ve")
                            ctx.mark(f"{tag}.wred{o}", "ve")
                            ctx.wait(eng, f"{tag}.wred{o}")
                            ctx.inc(eng.tensor_tensor(out=amax_col[:, :], in0=amax_col[:, :], in1=scol[:, 3:4], op=ALU.max), "ve")
                            ctx.mark(f"w{j}.free", "ve")
                            ctx.wait(eng, f"w{j}.free")
                if E == "tensor":
                    ctx.wait(eng, f"w{wj[0] + 7}.free")
                    ctx.inc(eng.transpose(poa[4][0:1, 0:128], amax_col[:, :], identf[:, :]), "pe")
                    ctx.mark(f"{tag}.wtr", "pe")
                if E == "vector":
                    ctx.wait(eng, f"{tag}.wtr")
                    ctx.inc(eng.tensor_reduce(out=smalls[0:1, 2:3], in_=poa[4][0:1, 0:128], axis=AX.X, op=ALU.max), "ve")
                    ctx.mark(f"{tag}.gw", "ve")
                    ctx.wait(eng, f"{tag}.gw")
                wj[0] += 8
            for c in range(8):
                xfc = xfs[c % 4]
                if E == "gpsimd":
                    if c == 0:
                        ctx.wait(eng, f"{tag}.mB")
                        ctx.wait(eng, f"{tag}.rB")
                        if fi == 0:
                            ctx.wait(eng, f"{tag}.wtr")
                    if c >= 4:
                        ctx.wait(eng, f"{tag}.axr{c - 4}")
                    ctx.inc(eng.tensor_tensor(out=xfc[:, :], in0=xt[c][:, :], in1=mB[:, :], op=ALU.subtract), "ge")
                    ctx.mark(f"{tag}.axs{c}", "ge")
                    ctx.wait(eng, f"{tag}.axs{c}")
                    ctx.inc(eng.tensor_tensor(out=xfc[:, :], in0=xfc[:, :], in1=rB[:, :], op=ALU.mult), "ge")
                    ctx.mark(f"{tag}.axm{c}", "ge")
                    ctx.wait(eng, f"{tag}.axm{c}")
                if E == "vector":
                    ctx.wait(eng, f"{tag}.axm{c}")
                    ctx.inc(eng.tensor_reduce(out=scol[:, 3:4], in_=xfc[:, :], axis=AX.X, op=ALU.max, apply_absolute_value=True), "ve")
                    ctx.mark(f"{tag}.axr{c}", "ve")
                    ctx.wait(eng, f"{tag}.axr{c}")
                    if c == 0:
                        ctx.inc(eng.tensor_copy(out=amax_col[:, :], in_=scol[:, 3:4]), "ve")
                    else:
                        ctx.inc(eng.tensor_tensor(out=amax_col[:, :], in0=amax_col[:, :], in1=scol[:, 3:4], op=ALU.max), "ve")
                    ctx.mark(f"{tag}.ax{c}", "ve")
                    ctx.wait(eng, f"{tag}.ax{c}")
            if E == "vector":
                ctx.mark(f"{tag}.axall", "ve")
            if E == "tensor":
                ctx.wait(eng, f"{tag}.axall")
                ctx.inc(eng.transpose(poa[4][0:1, 0:128], amax_col[:, :], identf[:, :]), "pe")
                ctx.mark(f"{tag}.axtr", "pe")
            if E == "vector":
                ctx.wait(eng, f"{tag}.axtr")
                ctx.inc(eng.tensor_reduce(out=smalls[0:1, 0:1], in_=poa[4][0:1, 0:128], axis=AX.X, op=ALU.max), "ve")
                ctx.mark(f"{tag}.lred", "ve")
                ctx.wait(eng, f"{tag}.lred")
                ctx.wait(eng, "init_ve")
                ctx.inc(eng.tensor_copy(out=zc[0:1, 0:1], in_=smalls[0:1, 0:1]), "ve")
                ctx.mark(f"{tag}.lmax", "ve")
            ca, cb, cc2 = f"c{fi}a", f"c{fi}b", f"c{fi}c"
            if E == "gpsimd":
                ctx.wait(eng, f"{tag}.lmax")
                ctx.inc(eng.dma_start(out=cc_in[fi].ap(), in_=zc[:, :]), ca, 16)
                ctx.mark(f"{tag}.cci", ca)
                ctx.wait(eng, f"{tag}.cci")
                ctx.inc(eng.collective_compute("AllReduce", ALU.max, replica_groups=[list(range(8))],
                                               ins=[cc_in[fi].ap().opt()], outs=[cc_out[fi].ap().opt()]), cc2, 1)
                ctx.mark(f"{tag}.ccc", cc2)
                ctx.wait(eng, f"{tag}.ccc")
                ctx.inc(eng.dma_start(out=smalls[0:1, 1:2], in_=cc_out[fi].ap()[0:1, 0:1]), cb, 16)
                ctx.mark(f"{tag}.gmo", cb)
                ctx.wait(eng, f"{tag}.gmo")
                ctx.inc(eng.engine_nop(), "ge")
                ctx.mark(f"{tag}.gmax", "ge")
            for c in range(4):
                xfc = xfs[c]
                if E == "vector":
                    ctx.inc(eng.tensor_tensor(out=xfc[:, :], in0=xt[c][:, :], in1=mB[:, :], op=ALU.subtract), "ve")
                    ctx.mark(f"{tag}.xs{c}", "ve")
                    ctx.wait(eng, f"{tag}.xs{c}")
                    ctx.inc(eng.tensor_tensor(out=xfc[:, :], in0=xfc[:, :], in1=rB[:, :], op=ALU.mult), "ve")
                    ctx.mark(f"{tag}.xn{c}", "ve")
            if E == "vector":
                ctx.wait(eng, f"{tag}.gmax")
                if fi == 0:
                    ctx.wait(eng, f"{tag}.gw")
                steps = [
                    lambda: eng.reciprocal(out=smalls[0:1, 4:5], in_=smalls[0:1, 1:2]),
                    lambda: eng.tensor_scalar(out=smalls[0:1, 5:6], in0=smalls[0:1, 4:5], scalar1=224.0, scalar2=None, op0=ALU.mult),
                    lambda: eng.tensor_tensor(out=smalls[0:1, 3:4], in0=smalls[0:1, 1:2], in1=smalls[0:1, 2:3], op=ALU.mult),
                    lambda: eng.tensor_scalar(out=smalls[0:1, 3:4], in0=smalls[0:1, 3:4], scalar1=GSC, scalar2=None, op0=ALU.mult),
                    lambda: eng.reciprocal(out=smalls[0:1, 6:7], in_=smalls[0:1, 2:3]),
                    lambda: eng.tensor_scalar(out=smalls[0:1, 7:8], in0=smalls[0:1, 6:7], scalar1=224.0, scalar2=None, op0=ALU.mult),
                ]
                for si, f in enumerate(steps):
                    ctx.inc(f(), "ve")
                    ctx.mark(f"{tag}.scl{si}", "ve")
                    ctx.wait(eng, f"{tag}.scl{si}")
                ctx.mark(f"{tag}.scales", "ve")
            for si, (sidx, col) in enumerate([(5, 0), (3, 1), (7, 2)]):
                if E == "vector":
                    if si == 0:
                        ctx.wait(eng, f"{tag}.scales")
                    ctx.inc(eng.tensor_scalar(out=t_row[0:1, 0:128], in0=onesr[:, :],
                                               scalar1=smalls[0:1, sidx:sidx + 1], scalar2=None, op0=ALU.mult), "ve")
                    ctx.mark(f"{tag}.bro{si}", "ve")
                if E == "tensor":
                    ctx.wait(eng, f"{tag}.bro{si}")
                    ctx.inc(eng.transpose(poa[4][:, 0:1], t_row[0:1, 0:128], identf[0:1, 0:1]), "pe")
                    ctx.mark(f"{tag}.brt{si}", "pe")
                if E == "vector":
                    ctx.wait(eng, f"{tag}.brt{si}")
                    ctx.inc(eng.tensor_copy(out=scol[:, col:col + 1], in_=poa[4][:, 0:1]), "ve")
                    ctx.mark(f"{tag}.brc{si}", "ve")
                if E == "tensor":
                    ctx.wait(eng, f"{tag}.brc{si}")
                    eng.nop()
            if E == "vector":
                ctx.mark(f"{tag}.sc1", "ve")
                ctx.mark(f"{tag}.sc2", "ve")
            for c in range(8):
                xfc = xfs[c % 4]
                if E == "vector" and c >= 4:
                    ctx.wait(eng, f"{tag}.xq{c - 4}")
                    ctx.inc(eng.tensor_tensor(out=xfc[:, :], in0=xt[c][:, :], in1=mB[:, :], op=ALU.subtract), "ve")
                    ctx.mark(f"{tag}.xs{c}", "ve")
                    ctx.wait(eng, f"{tag}.xs{c}")
                    ctx.inc(eng.tensor_tensor(out=xfc[:, :], in0=xfc[:, :], in1=rB[:, :], op=ALU.mult), "ve")
                    ctx.mark(f"{tag}.xn{c}", "ve")
                ceng = "scalar" if c % 2 == 0 else "vector"
                csem = "se" if c % 2 == 0 else "ve"
                if E == ceng:
                    ctx.wait(eng, f"{tag}.xn{c}")
                    if c in (0, 1):
                        ctx.wait(eng, f"{tag}.sc1")
                    if ceng == "scalar":
                        ctx.inc(eng.activation(out=xq8[c][:, :], in_=xfc[:, :], func=AF.Copy, scale=scol[:, 0:1]), csem)
                    else:
                        ctx.inc(eng.tensor_scalar(out=xq8[c][:, :], in0=xfc[:, :], scalar1=scol[:, 0:1], scalar2=None, op0=ALU.mult), csem)
                    ctx.mark(f"{tag}.xq{c}", csem)
            for o in range(8):
                j = wj[0] + o
                if E == "sync":
                    ctx.wait(eng, f"w{j - 4}.free")
                    ctx.inc(eng.dma_start(out=wld[j % 4][:, :], in_=wfb.ap()[o]), f"sy{j % 4}", 16)
                    ctx.mark(f"w{j}.ld", f"sy{j % 4}")
                ceng = "scalar" if o % 2 == 0 else "vector"
                csem = "se" if o % 2 == 0 else "ve"
                if E == ceng:
                    ctx.wait(eng, f"w{j}.ld")
                    if o in (0, 1):
                        ctx.wait(eng, f"{tag}.sc1")
                        if fi == 0:
                            ctx.wait(eng, "a1.woevall")
                    if ceng == "scalar":
                        ctx.inc(eng.activation(out=wq8[o][:, :], in_=wld[j % 4][:, :], func=AF.Copy, scale=scol[:, 2:3]), csem)
                    else:
                        ctx.inc(eng.tensor_scalar(out=wq8[o][:, :], in0=wld[j % 4][:, :], scalar1=scol[:, 2:3], scalar2=None, op0=ALU.mult), csem)
                    ctx.mark(f"w{j}.free", csem)
                    ctx.mark(f"{tag}.wq{o}", csem)
            wj[0] += 8
            for tc in range(4):
                for o in range(8):
                    u = tc * 8 + o
                    sl = slice(512 * tc, 512 * (tc + 1))
                    if E == "tensor":
                        if u == 0:
                            ctx.wait(eng, f"{tag}.wq7")
                            if fi == 1:
                                ctx.wait(eng, "b.avevall")
                            else:
                                ctx.wait(eng, "a1.woevall")
                        if u >= 2:
                            ctx.wait(eng, f"{tag}.fev{u - 2}")
                        for c in range(8):
                            last = mm(eng, poa[o % 2][:, :], wq8[o][:, 128 * c:128 * (c + 1)], xq8[c][:, sl],
                                      c == 0, c == 7)
                        ctx.inc(last, "pe")
                        ctx.mark(f"{tag}.fmm{u}", "pe")
                    if E == "scalar":
                        ctx.wait(eng, f"{tag}.fmm{u}")
                        if u == 0:
                            ctx.wait(eng, f"{tag}.sc2")
                        if u >= 2:
                            ctx.wait(eng, f"{tag}.fev{u - 2}")
                        ctx.inc(eng.activation(out=gtf[o % 2][:, :], in_=poa[o % 2][:, :], func=AF.Copy,
                                               scale=scol[:, 1:2]), "se")
                        ctx.mark(f"{tag}.fep{u}", "se")
                    if E == "vector":
                        ctx.wait(eng, f"{tag}.fep{u}")
                        ctx.inc(eng.scalar_tensor_tensor(out=xt[o][:, sl], in0=gtf[o % 2][:, :],
                                                         scalar=bias_sb[:, o:o + 1], in1=xt[o][:, sl],
                                                         op0=ALU.add, op1=ALU.add), "ve")
                        ctx.mark(f"{tag}.fev{u}", "ve")
                if fi == 0 and "attn_b" in PH:
                    ln_stats_tc(E, eng, "b", tc, f"{tag}.fev{tc * 8 + 7}")
            if E == "vector":
                ctx.mark(f"{tag}.fevall", "ve")

        def attn_b(E, eng):
            tag = "b"
            ln_stats_fin(E, eng, tag, want_mB=True, want_rB=False)
            for tt in range(4):
                ttsl = slice(512 * tt, 512 * (tt + 1))
                for c in range(8):
                    if E == "vector":
                        if c == 0:
                            if tt == 0:
                                ctx.wait(eng, f"{tag}.mB")
                            else:
                                ctx.wait(eng, f"{tag}.smmdone{tt - 1}")
                        ctx.inc(eng.tensor_tensor(out=xbt_b[c][:, :], in0=xt[c][:, ttsl], in1=mB[:, ttsl], op=ALU.subtract), "ve")
                        ctx.mark(f"{tag}.xb{tt}_{c}", "ve")
                stride = 64 if BCUT >= 4 else 32
                kbase = wj[0] + tt * stride
                vbase = kbase + 32
                def kcast(eng, n):
                    j = kbase + n
                    ctx.wait(eng, f"w{j}.ld")
                    if n >= 3:
                        ctx.wait(eng, f"{tag}.smm{tt}_{n - 3}")
                    elif tt > 0:
                        ctx.wait(eng, f"{tag}.avmm{tt - 1}_{29 + n}" if BCUT >= 4 else f"{tag}.smm{tt - 1}_{29 + n}")
                    ctx.inc(eng.tensor_copy(out=wbf[n % 3][:, :], in_=wld[j % 4][:, :]), "ve")
                    ctx.mark(f"w{j}.cast", "ve")
                    ctx.mark(f"w{j}.free", "ve")

                for n in range(32):
                    j = kbase + n
                    if E == "sync":
                        ctx.wait(eng, f"w{j - 4}.free")
                        ctx.inc(eng.dma_start(out=wld[j % 4][:, :], in_=knbd.ap()[n]), f"sy{j % 4}", 16)
                        ctx.mark(f"w{j}.ld", f"sy{j % 4}")
                    if E == "vector":
                        if n == 0:
                            kcast(eng, 0)
                            kcast(eng, 1)
                        if n + 2 < 32:
                            kcast(eng, n + 2)
                    if E == "tensor":
                        if n == 0:
                            for c in range(8):
                                ctx.wait(eng, f"{tag}.xb{tt}_{c}")
                            if tt == 0:
                                ctx.wait(eng, "f0.fevall")
                            elif BCUT >= 4:
                                ctx.wait(eng, f"{tag}.avevall{tt - 1}")
                        ctx.wait(eng, f"w{j}.cast")
                        if n >= 2:
                            ctx.wait(eng, f"{tag}.scp{tt}_{n - 2}")
                            if BCUT >= 2:
                                ctx.wait(eng, f"{tag}.ssq{tt}_{n - 2}")
                        for c in range(8):
                            last = mm(eng, poa[n % 2][:, :], wbf[n % 3][:, 128 * c:128 * (c + 1)], xbt_b[c][:, :],
                                      c == 0, c == 7)
                        ctx.inc(last, "pe")
                        ctx.mark(f"{tag}.smm{tt}_{n}", "pe")
                    if E == "scalar" and BCUT >= 2:
                        ctx.wait(eng, f"{tag}.scp{tt}_{n}")
                        if n >= 4 and B2SUB >= 1:
                            ctx.wait(eng, f"{tag}.n2mm{tt}_{n - 4}")
                        ctx.inc(eng.activation(out=sq4b[n % 4][:, :], in_=gt[n][:, :], func=AF.Square), "se")
                        ctx.mark(f"{tag}.ssq{tt}_{n}", "se")
                    if E == "vector":
                        ctx.wait(eng, f"{tag}.smm{tt}_{n}")
                        ctx.inc(eng.tensor_copy(out=gt[n][:, :], in_=poa[n % 2][:, :]), "ve")
                        ctx.mark(f"{tag}.scp{tt}_{n}", "ve")
                    if E == "tensor" and BCUT >= 2 and B2SUB >= 1:
                        for nd in ([n - 3] if n >= 3 else []) + ([29, 30, 31] if n == 31 else []):
                            ctx.wait(eng, f"{tag}.ssq{tt}_{nd}")
                            ctx.inc(mm(eng, poa[6][0:1, :], onescol[:, :], sq4b[nd % 4][:, :],
                                       nd == 0, nd == 31), "pe")
                            ctx.mark(f"{tag}.n2mm{tt}_{nd}", "pe")
                if E == "tensor":
                    ctx.mark(f"{tag}.smmdone{tt}", "pe")
                if E == "scalar" and BCUT >= 2 and B2SUB >= 2:
                    ctx.wait(eng, f"{tag}.n2mm{tt}_31")
                    ctx.inc(eng.activation(out=r_row[:, 0:512], in_=poa[6][0:1, :], func=AF.Ln), "se")
                    ctx.mark(f"{tag}.cl{tt}", "se")
                    ctx.wait(eng, f"{tag}.cl{tt}")
                    ctx.inc(eng.activation(out=r_row[:, 0:512], in_=r_row[:, 0:512], func=AF.Exp, scale=-0.5), "se")
                    ctx.mark(f"{tag}.ce{tt}", "se")
                    ctx.wait(eng, f"{tag}.ce{tt}")
                    ctx.inc(eng.mul(out=r_row[:, 0:512], in_=r_row[:, 0:512], mul=64.0), "se")
                    ctx.mark(f"{tag}.c{tt}", "se")
                    ctx.wait(eng, f"{tag}.c{tt}")
                    if B2SUB >= 3:
                        if tt > 0 and BCUT >= 3:
                            ctx.wait(eng, f"{tag}.gmulall{tt - 1}")
                        ctx.inc(eng.dma_start(out=rb_h.ap()[0:1, 0:512], in_=r_row[:, 0:512]), "sq", 16)
                        ctx.mark(f"{tag}.cbo{tt}", "sq")
                        ctx.wait(eng, f"{tag}.cbo{tt}")
                        ctx.inc(eng.dma_start(out=cB[:, :], in_=bass.AP(rb_h, 0, [[0, 128], [1, 512]])), "sq", 16)
                        ctx.mark(f"{tag}.cB{tt}", "sq")
                for n in range(32):
                    if E == "vector" and BCUT >= 3:
                        if n == 0:
                            ctx.wait(eng, f"{tag}.cB{tt}")
                        if n >= 2:
                            ctx.wait(eng, f"{tag}.gel{tt}_{n - 2}")
                        ctx.inc(eng.tensor_tensor(out=gtf[n % 2][:, :], in0=gt[n][:, :], in1=cB[:, :], op=ALU.mult), "ve")
                        ctx.mark(f"{tag}.gmul{tt}_{n}", "ve")
                    if E == "scalar" and BCUT >= 3:
                        ctx.wait(eng, f"{tag}.gmul{tt}_{n}")
                        ctx.inc(eng.activation(out=gt[n][:, :], in_=gtf[n % 2][:, :], func=AF.Gelu), "se")
                        ctx.mark(f"{tag}.gel{tt}_{n}", "se")
                    if BCUT >= 4:
                        j = vbase + n
                        if E == "sync":
                            ctx.wait(eng, f"w{j - 4}.free")
                            ctx.inc(eng.dma_start(out=wld[j % 4][:, :], in_=vnbd.ap()[n]), f"sy{j % 4}", 16)
                            ctx.mark(f"w{j}.ld", f"sy{j % 4}")
                        if E == "scalar":
                            ctx.wait(eng, f"w{j}.ld")
                            if n >= 2:
                                ctx.wait(eng, f"{tag}.avmm{tt}_{n - 2}")
                            else:
                                ctx.wait(eng, f"{tag}.smm{tt}_{30 + n}")
                            ctx.inc(eng.activation(out=wbf[n % 2][:, :], in_=wld[j % 4][:, :], func=AF.Copy), "se")
                            ctx.mark(f"w{j}.cast", "se")
                            ctx.mark(f"w{j}.free", "se")
                        if E == "tensor":
                            ctx.wait(eng, f"w{j}.cast")
                            ctx.wait(eng, f"{tag}.gel{tt}_{n}")
                            for dc in range(8):
                                last = mm(eng, poa[dc][:, :], wbf[n % 2][:, 128 * dc:128 * (dc + 1)], gt[n][:, :],
                                          n == 0, n == 31)
                            ctx.inc(last, "pe")
                            ctx.mark(f"{tag}.avmm{tt}_{n}", "pe")
                if E == "vector" and BCUT >= 3:
                    ctx.mark(f"{tag}.gmulall{tt}", "ve")
                if E == "tensor" and tt == 3 and BCUT >= 4:
                    ctx.mark(f"{tag}.avdone", "pe")
                for dc in range(8 if BCUT >= 4 else 0):
                    if E == "vector":
                        ctx.wait(eng, f"{tag}.avmm{tt}_31")
                        ctx.inc(eng.tensor_tensor(out=xt[dc][:, ttsl], in0=poa[dc][:, :], in1=xt[dc][:, ttsl], op=ALU.add), "ve")
                        ctx.mark(f"{tag}.avev{tt}_{dc}", "ve")
                if E == "vector" and BCUT >= 4:
                    ctx.mark(f"{tag}.avevall{tt}", "ve")
                if BCUT >= 4 and "ffn2" in PH:
                    ln_stats_tc(E, eng, "f1", tt, f"{tag}.avev{tt}_7")
            wj[0] += 256 if BCUT >= 4 else 128
            if E == "vector" and BCUT >= 4:
                ctx.mark(f"{tag}.avevall", "ve")

        def store(E, eng, final_mark):
            if E == "sync":
                ctx.wait(eng, final_mark)
                for c in range(8):
                    ctx.inc(eng.dma_start(out=out_t.ap()[128 * c:128 * (c + 1), :], in_=xt[c][:, :]), "sy", 16)
                if DEBUG:
                    ctx.inc(eng.dma_start(out=dbg1.ap()[0:1, 0:16], in_=smalls[:, :]), "sy", 16)
                    ctx.inc(eng.dma_start(out=dbg1.ap()[0:1, 16:16 + S], in_=m_row), "sy", 16)
                    ctx.inc(eng.dma_start(out=dbg1.ap()[0:1, 16 + S:16 + 2 * S], in_=r_row), "sy", 16)
                    ctx.inc(eng.dma_start(out=dbg1.ap()[0:1, 16 + 2 * S:16 + 3 * S], in_=t_row), "sy", 16)
                    ctx.inc(eng.dma_start(out=dbgs.ap(), in_=scol[:, :]), "sy", 16)
                    ctx.inc(eng.dma_start(out=dbga.ap(), in_=amax_col[:, :]), "sy", 16)
                    ctx.inc(eng.dma_start(out=dbgm.ap(), in_=mB[:, 0:8]), "sy", 16)
                    ctx.inc(eng.dma_start(out=dbgr.ap(), in_=rB[:, 0:8]), "sy", 16)
                ctx.mark("st_done", "sy")
                ctx.wait(eng, "st_done")
            if E == "gpsimd":
                ctx.wait(eng, "st_done")

        def program(E, eng):
            wj[0] = 0
            preload(E, eng)
            final = None
            if "attn_a" in PH:
                attn_a(E, eng)
                if E == "vector":
                    ctx.mark("a1.woevall", "ve")
                if E == "tensor":
                    ctx.mark("a1.woevall_pe", "pe")
                final = "a1.woevall"
            if "ffn1" in PH:
                ffn(E, eng, 0)
                if E == "tensor":
                    ctx.mark("f0.fevall_pe", "pe")
                final = "f0.fevall"
            if "attn_b" in PH:
                attn_b(E, eng)
                final = "b.avevall" if BCUT >= 4 else "f0.fevall"
            if "ffn2" in PH:
                ffn(E, eng, 1)
                final = "f1.fevall"
            store(E, eng, final)

        ctx.emit = False
        for E in ["sync", "tensor", "vector", "scalar", "gpsimd"]:
            program(E, _DummyEngine())
        counted = dict(ctx.cnt)

        ctx.emit = True
        with nc.Block() as block:
            for ename, deco in (("sync", block.sync), ("tensor", block.tensor),
                                ("vector", block.vector), ("scalar", block.scalar),
                                ("gpsimd", block.gpsimd)):
                def run_engine(eng, _ename=ename):
                    ctx.cnt = {}
                    program(_ename, eng)
                deco(run_engine)
        ctx.cnt = counted
    return nc



def _hhblock(wT):
    blk = wT.reshape(8, 128, 8, 128).transpose(2, 1, 0, 3).reshape(8, 128, 1024)
    return np.ascontiguousarray(blk)


def _consts():
    i = np.arange(128)
    causal = (i[:, None] <= i[None, :]).astype(ml_dtypes.bfloat16)
    anti = (i[:, None] >= i[None, :] + 1).astype(ml_dtypes.bfloat16)
    mask = np.concatenate([causal, np.ones((128, 128), ml_dtypes.bfloat16), anti], axis=1)
    ident = np.eye(128, dtype=ml_dtypes.bfloat16)
    ones = np.ones((128, 1), ml_dtypes.bfloat16)
    return mask, ident, ones


def make_inputs(L0_attn_a_x, L0_attn_a_Wq, L0_attn_a_Wk, L0_attn_a_Wv, L0_attn_a_Wo,
                L0_attn_b_key_param_tokens, L0_attn_b_value_param_tokens,
                L0_ffn_weight, L0_ffn_bias):
    x = np.asarray(L0_attn_a_x, dtype=np.float32)
    wq = _hhblock(np.ascontiguousarray(np.asarray(L0_attn_a_Wq, np.float32).T))
    wk = _hhblock(np.ascontiguousarray(np.asarray(L0_attn_a_Wk, np.float32).T))
    wv = _hhblock(np.ascontiguousarray(np.asarray(L0_attn_a_Wv, np.float32).T))
    wo = _hhblock(np.ascontiguousarray(np.asarray(L0_attn_a_Wo, np.float32).T))
    wf = _hhblock(np.ascontiguousarray(np.asarray(L0_ffn_weight, np.float32).T))
    kT = np.ascontiguousarray(np.asarray(L0_attn_b_key_param_tokens, np.float32).T)
    knb = np.ascontiguousarray(kT.reshape(8, 128, 32, 128).transpose(2, 1, 0, 3).reshape(32, 128, 1024))
    vnb = np.ascontiguousarray(np.asarray(L0_attn_b_value_param_tokens, np.float32).reshape(32, 128, 1024))
    bias = np.ascontiguousarray(np.asarray(L0_ffn_bias, np.float32).reshape(8, 128).T)
    mask, ident, ones = _consts()
    identf = np.eye(128, dtype=np.float32)
    onesr = np.ones((1, 128), np.float32)
    in_maps = []
    for b in range(8):
        in_maps.append({
            "x_in": np.ascontiguousarray(x[b].T),
            "wqb": wq, "wkb": wk, "wvb": wv, "wob": wo, "wfb": wf,
            "knbd": knb, "vnbd": vnb, "bias_in": bias,
            "mask_in": mask, "id_in": ident, "ones_in": ones,
            "idf_in": identf, "onesr_in": onesr,
        })
    return in_maps


def kernel(**inputs):
    in_maps = make_inputs(**inputs)
    nc = build_nc(4)
    res = run_bass_kernel_spmd(nc, in_maps, core_ids=list(range(8)))
    out = np.stack([np.asarray(res.results[b]["out"]).T for b in range(8)])
    return np.ascontiguousarray(out)


# version 10 (final)
# speedup vs baseline: 1.2130x; 1.0276x over previous
import contextlib
import numpy as np
import ml_dtypes

import concourse.bass as bass
import concourse.mybir as mybir
from concourse.bass_utils import run_bass_kernel_spmd

F32 = mybir.dt.float32
BF16 = mybir.dt.bfloat16
FP8 = mybir.dt.float8e4
AF = mybir.ActivationFunctionType
ALU = mybir.AluOpType
AX = mybir.AxisListType

S, D = 2048, 1024
LN64 = float(np.log(64.0))
EPS = 1e-5
GSC = 4.0 / (448.0 * 448.0)
DEBUG = False
BCUT = 4
B2SUB = 3


class _DummyInstr:
    def then_inc(self, *a, **k):
        return self


class _DummyEngine:
    def __getattr__(self, name):
        def f(*a, **k):
            return _DummyInstr()
        return f


class Ctx:
    def __init__(self):
        self.emit = False
        self.cnt = {}
        self.cp = {}
        self.sems = {}

    def inc(self, instr, sem, n=1):
        self.cnt[sem] = self.cnt.get(sem, 0) + n
        if self.emit:
            instr.then_inc(self.sems[sem], n)
        return instr

    def mark(self, name, sem):
        cur = (sem, self.cnt.get(sem, 0))
        if not self.emit:
            assert name not in self.cp, f"dup checkpoint {name}"
            self.cp[name] = cur
        else:
            assert self.cp.get(name) == cur, (name, self.cp.get(name), cur)

    def wait(self, eng, name):
        if self.emit and not isinstance(eng, _DummyEngine):
            sem, val = self.cp[name]
            if val > 0:
                eng.wait_ge(self.sems[sem], val)


def build_nc(n_phases=4):
    nc = bass.Bass("TRN2", target_bir_lowering=False, debug=False)

    x_in = nc.dram_tensor("x_in", [D, S], F32, kind="ExternalInput")
    wqb = nc.dram_tensor("wqb", [8, 128, 1024], F32, kind="ExternalInput")
    wkb = nc.dram_tensor("wkb", [8, 128, 1024], F32, kind="ExternalInput")
    wvb = nc.dram_tensor("wvb", [8, 128, 1024], F32, kind="ExternalInput")
    wob = nc.dram_tensor("wob", [8, 128, 1024], F32, kind="ExternalInput")
    wfb = nc.dram_tensor("wfb", [8, 128, 1024], F32, kind="ExternalInput")
    knbd = nc.dram_tensor("knbd", [32, 128, 1024], F32, kind="ExternalInput")
    vnbd = nc.dram_tensor("vnbd", [32, 128, 1024], F32, kind="ExternalInput")
    bias_in = nc.dram_tensor("bias_in", [128, 8], F32, kind="ExternalInput")
    mask_in = nc.dram_tensor("mask_in", [128, 384], BF16, kind="ExternalInput")
    id_in = nc.dram_tensor("id_in", [128, 128], BF16, kind="ExternalInput")
    ones_in = nc.dram_tensor("ones_in", [128, 1], BF16, kind="ExternalInput")
    idf_in = nc.dram_tensor("idf_in", [128, 128], F32, kind="ExternalInput")
    onesr_in = nc.dram_tensor("onesr_in", [1, 128], F32, kind="ExternalInput")
    out_t = nc.dram_tensor("out", [D, S], F32, kind="ExternalOutput")
    if DEBUG:
        dbg1 = nc.dram_tensor("dbg1", [1, 16 + 3 * S], F32, kind="ExternalOutput")
        dbgs = nc.dram_tensor("dbgs", [128, 4], F32, kind="ExternalOutput")
        dbga = nc.dram_tensor("dbga", [128, 1], F32, kind="ExternalOutput")
        dbgm = nc.dram_tensor("dbgm", [128, 8], F32, kind="ExternalOutput")
        dbgr = nc.dram_tensor("dbgr", [128, 8], F32, kind="ExternalOutput")

    rb_h = nc.dram_tensor("rbounce", [1, S], F32)
    cb_h = nc.dram_tensor("cbounce", [128, 1], F32)
    sb_h = nc.dram_tensor("sbounce", [1, 4], F32)
    cc_in = [nc.dram_tensor(f"cc_in{i}", [128, 128], F32) for i in range(2)]
    cc_out = [nc.dram_tensor(f"cc_out{i}", [128, 128], F32) for i in range(2)]

    ctx = Ctx()
    es = contextlib.ExitStack()
    sbuf = lambda name, shape, dt: es.enter_context(nc.sbuf_tensor(name, shape, dt))

    with es:
        xt = [sbuf(f"xt{c}", [128, S], F32) for c in range(8)]
        maskt = sbuf("maskt", [128, 384], BF16)
        ident = sbuf("ident", [128, 128], BF16)
        onescol = sbuf("onescol", [128, 1], BF16)
        identf = sbuf("identf", [128, 128], F32)
        onesr = sbuf("onesr", [1, 128], F32)
        bias_sb = sbuf("bias_sb", [128, 8], F32)
        rows = sbuf("rows", [1, 3 * S], F32)
        m_row = rows[0:1, 0:S]
        r_row = rows[0:1, S:2 * S]
        t_row = rows[0:1, 2 * S:3 * S]
        rows_bf = rows.bitcast(BF16)
        exp_bf = rows_bf[0:1, 2 * S + 1024:2 * S + 1536]
        smalls = sbuf("smalls", [1, 16], F32)
        scol = sbuf("scol", [128, 4], F32)
        amax_col = sbuf("amax_col", [128, 1], F32)
        zc = sbuf("zc", [128, 128], F32)
        aux = sbuf("aux", [128, 2 * S], F32)
        mB = aux[:, 0:S]
        rB = aux[:, S:2 * S]
        cB = aux[:, S:S + 512]
        gtf = [aux[:, S + 512 + i * 512:S + 512 + (i + 1) * 512] for i in range(2)]
        sqb2 = sbuf("sqb2", [128, 1024], BF16)
        sqb = [sqb2[:, 0:512], sqb2[:, 512:1024]]
        sq4b = None
        xbc = sbuf("xbc", [128, 1024], BF16)
        xbcs = [xbc[:, 0:512], xbc[:, 512:1024]]
        sq4b = sqb + xbcs
        xba = sbuf("xba", [128, 4096], BF16)
        xbas = [xba[:, i * 512:(i + 1) * 512] for i in range(8)]
        wld = [sbuf(f"wld{i}", [128, 1024], F32) for i in range(4)]
        wbf = [sbuf(f"wbf{i}", [128, 1024], BF16) for i in range(3)]
        vTt = sbuf("vTt", [128, 512], BF16)
        vaug = [sbuf(f"vaug{t}", [128, 130], BF16) for t in range(16)]
        rcol = sbuf("rcol", [128, 2], F32)

        arenaB = sbuf("arenaB", [128, 12288], BF16)
        qTt = arenaB[:, 0:2048]
        kTt = arenaB[:, 2048:4096]
        ATt = [[arenaB[:, 4096 + (h * 4 + i) * 384: 4096 + (h * 4 + i + 1) * 384]
                for i in range(4)] for h in range(2)]
        aoTs = [arenaB[:, i * 512:(i + 1) * 512] for i in range(8)]
        wobf = [arenaB[:, 4096 + i * 1024:4096 + (i + 1) * 1024] for i in range(8)]
        arenaB8 = arenaB.bitcast(FP8)
        xq8 = [arenaB8[:, i * S:(i + 1) * S] for i in range(8)]
        wq8 = [arenaB8[:, 16384 + i * 1024:16384 + (i + 1) * 1024] for i in range(8)]
        arenaBf = arenaB.bitcast(F32)
        xbt_b = [arenaB[:, i * 512:(i + 1) * 512] for i in range(8)]

        big = sbuf("big", [128, 16384], BF16)
        att_o = [big[:, i * 1024:(i + 1) * 1024] for i in range(16)]
        gt = [big[:, i * 512:(i + 1) * 512] for i in range(32)]
        bigf = big.bitcast(F32)
        xfs = [bigf[:, i * S:(i + 1) * S] for i in range(4)]

        po = [es.enter_context(nc.psum_tensor(f"po{i}", [128, 512], F32)) for i in range(8)]
        poa = [p.ap() for p in po]
        pob = [p.ap().bitcast(BF16) for p in po]

        for name in ["sy", "sy0", "sy1", "sy2", "sy3", "sq", "pe", "ve", "se", "ge", "pl",
                     "c0a", "c0b", "c0c", "c1a", "c1b", "c1c"]:
            ctx.sems[name] = es.enter_context(nc.semaphore(name))

        def mm(eng, out, lhsT, rhs, start, stop):
            return eng.matmul(out, lhsT, rhs, start=start, stop=stop, skip_group_check=True)

        PH = ["attn_a", "ffn1", "attn_b", "ffn2"][:n_phases]
        wj = [0]

        def preload(E, eng):
            if E == "sync":
                for c in range(8):
                    ctx.inc(eng.dma_start(out=xt[c][:, :], in_=x_in.ap()[128 * c:128 * (c + 1), :]), "pl", 16)
                ctx.inc(eng.dma_start(out=maskt[:, :], in_=mask_in.ap()), "pl", 16)
                ctx.inc(eng.dma_start(out=ident[:, :], in_=id_in.ap()), "pl", 16)
                ctx.inc(eng.dma_start(out=onescol[:, :], in_=ones_in.ap()), "pl", 16)
                ctx.inc(eng.dma_start(out=bias_sb[:, :], in_=bias_in.ap()), "pl", 16)
                ctx.inc(eng.dma_start(out=identf[:, :], in_=idf_in.ap()), "pl", 16)
                ctx.inc(eng.dma_start(out=onesr[:, :], in_=onesr_in.ap()), "pl", 16)
                ctx.mark("ld0", "pl")
            if E == "vector":
                ctx.wait(eng, "ld0")
                for t in range(16):
                    eng.memset(vaug[t][:, 64:65], 1.0)
                    eng.memset(vaug[t][:, 129:130], 1.0)
                ctx.inc(eng.memset(zc[:, :], 0.0), "ve")
                ctx.mark("init_ve", "ve")

        def ln_stats_tc(E, eng, tag, tc, ready):
            sl = slice(512 * tc, 512 * (tc + 1))
            sq4 = sqb + [xbc[:, 0:512], xbc[:, 512:1024]]
            if E == "vector":
                ctx.wait(eng, ready)
                if tc > 0:
                    ctx.wait(eng, f"{tag}.smm{(tc - 1) * 8 + 7}")
                for c in range(8):
                    ctx.inc(eng.tensor_copy(out=xbas[c][:, :], in_=xt[c][:, sl]), "ve")
                ctx.mark(f"{tag}.xc{tc}", "ve")
            for c in range(8):
                j = tc * 8 + c
                if E == "scalar":
                    if c == 0:
                        ctx.wait(eng, ready)
                    if j >= 4:
                        ctx.wait(eng, f"{tag}.smm{j - 4}")
                    ctx.inc(eng.activation(out=sq4[c % 4][:, :], in_=xt[c][:, sl], func=AF.Square), "se")
                    ctx.mark(f"{tag}.sq{j}", "se")
                if E == "tensor":
                    if c == 0:
                        ctx.wait(eng, f"{tag}.xc{tc}")
                        if tc > 0:
                            ctx.wait(eng, f"{tag}.sev{tc - 1}")
                    ctx.wait(eng, f"{tag}.sq{j}")
                    mm(eng, poa[4][0:1, :], onescol[:, :], xbas[c][:, :], c == 0, c == 7)
                    ctx.inc(mm(eng, poa[5][0:1, :], onescol[:, :], sq4[c % 4][:, :], c == 0, c == 7), "pe")
                    ctx.mark(f"{tag}.smm{j}", "pe")
            if E == "vector":
                ctx.wait(eng, f"{tag}.smm{tc * 8 + 7}")
                eng.tensor_scalar(out=m_row[:, sl], in0=poa[4][0:1, :], scalar1=1.0 / D, scalar2=None, op0=ALU.mult)
                ctx.inc(eng.tensor_scalar(out=t_row[:, sl], in0=poa[5][0:1, :], scalar1=1.0 / D, scalar2=None, op0=ALU.mult), "ve")
                ctx.mark(f"{tag}.sev{tc}", "ve")

        def ln_stats_fin(E, eng, tag, want_mB, want_rB):
            if E == "vector":
                ctx.wait(eng, f"{tag}.sev3")
                ctx.inc(eng.tensor_tensor(out=r_row[:, :], in0=m_row[:, :], in1=m_row[:, :], op=ALU.mult), "ve")
                ctx.mark(f"{tag}.vmm", "ve")
                ctx.wait(eng, f"{tag}.vmm")
                ctx.inc(eng.scalar_tensor_tensor(out=r_row[:, :], in0=t_row[:, :], scalar=EPS,
                                                 in1=r_row[:, :], op0=ALU.add, op1=ALU.subtract), "ve")
                ctx.mark(f"{tag}.var", "ve")
            if E == "scalar":
                ctx.wait(eng, f"{tag}.var")
                ctx.inc(eng.activation(out=r_row[:, :], in_=r_row[:, :], func=AF.Ln), "se")
                ctx.mark(f"{tag}.rln", "se")
                ctx.wait(eng, f"{tag}.rln")
                ctx.inc(eng.activation(out=r_row[:, :], in_=r_row[:, :], func=AF.Exp, scale=-0.5), "se")
                ctx.mark(f"{tag}.r", "se")
                if want_mB:
                    ctx.wait(eng, f"{tag}.sev3")
                    ctx.inc(eng.dma_start(out=rb_h.ap(), in_=m_row[:, :]), "sq", 16)
                    ctx.mark(f"{tag}.mbo", "sq")
                    ctx.wait(eng, f"{tag}.mbo")
                    ctx.inc(eng.dma_start(out=mB[:, :], in_=bass.AP(rb_h, 0, [[0, 128], [1, S]])), "sq", 16)
                    ctx.mark(f"{tag}.mB", "sq")
                if want_rB:
                    ctx.wait(eng, f"{tag}.r")
                    if want_mB:
                        ctx.wait(eng, f"{tag}.mB")
                    ctx.inc(eng.dma_start(out=rb_h.ap(), in_=r_row[:, :]), "sq", 16)
                    ctx.mark(f"{tag}.rbo", "sq")
                    ctx.wait(eng, f"{tag}.rbo")
                    ctx.inc(eng.dma_start(out=rB[:, :], in_=bass.AP(rb_h, 0, [[0, 128], [1, S]])), "sq", 16)
                    ctx.mark(f"{tag}.rB", "sq")

        def ln_stats(E, eng, tag, want_mbf, want_mB, want_rB, pe_pre=None, se_pre=None):
            for tc in range(4):
                ln_stats_tc(E, eng, tag, tc, se_pre or "ld0")
            ln_stats_fin(E, eng, tag, want_mB, want_rB)

        def attn_a(E, eng):
            tag = "a1"
            ln_stats(E, eng, tag, want_mbf=False, want_mB=True, want_rB=True)
            def wcasts(eng, hh):
                for wi in range(3):
                    j = wj[0] + hh * 3 + wi
                    ctx.wait(eng, f"w{j}.ld")
                    if hh > 0:
                        ctx.wait(eng, f"{tag}.pfold{hh - 1}_3")
                    ctx.inc(eng.tensor_copy(out=wbf[wi][:, :], in_=wld[j % 4][:, :]), "ve")
                    ctx.mark(f"w{j}.cast", "ve")
                    ctx.mark(f"w{j}.free", "ve")

            def xba0(eng, hh):
                if hh == 0:
                    ctx.wait(eng, f"{tag}.mB")
                else:
                    ctx.wait(eng, f"{tag}.pfold{hh - 1}_3")
                for c in range(8):
                    ctx.inc(eng.tensor_tensor(out=xbas[c][:, :], in0=xt[c][:, 0:512], in1=mB[:, 0:512], op=ALU.subtract), "ve")
                ctx.mark(f"{tag}.pc{hh}_0", "ve")

            for hh in range(8):
                for wi, wsrc in enumerate((wqb, wkb, wvb)):
                    j = wj[0] + hh * 3 + wi
                    if E == "sync":
                        if j >= 4:
                            ctx.wait(eng, f"w{j - 4}.free")
                        ctx.inc(eng.dma_start(out=wld[j % 4][:, :], in_=wsrc.ap()[hh]), f"sy{j % 4}", 16)
                        ctx.mark(f"w{j}.ld", f"sy{j % 4}")
                if E == "vector" and hh == 0:
                    wcasts(eng, 0)
                    xba0(eng, 0)

                for tc in range(4):
                    sl = slice(512 * tc, 512 * (tc + 1))
                    if E == "vector" and tc > 0:
                        ctx.wait(eng, f"{tag}.pfold{hh}_{tc - 1}")
                        for c in range(8):
                            ctx.inc(eng.tensor_tensor(out=xbas[c][:, :], in0=xt[c][:, sl], in1=mB[:, sl], op=ALU.subtract), "ve")
                        ctx.mark(f"{tag}.pc{hh}_{tc}", "ve")
                    pb = 3 * (tc % 2)
                    if E == "tensor":
                        ctx.wait(eng, f"{tag}.pc{hh}_{tc}")
                        if tc >= 2:
                            ctx.wait(eng, f"{tag}.pev{hh}_{tc - 2}")
                        elif hh > 0:
                            ctx.wait(eng, f"{tag}.pev{hh - 1}_{tc + 2}")
                            if tc == 1:
                                ctx.wait(eng, f"{tag}.avev{hh - 1}_1_15")
                        if tc == 0:
                            for wi in range(3):
                                ctx.wait(eng, f"w{wj[0] + hh * 3 + wi}.cast")
                        for c in range(8):
                            for wi in range(3):
                                last = mm(eng, poa[pb + wi][:, :], wbf[wi][:, 128 * c:128 * (c + 1)], xbas[c][:, :],
                                          c == 0, c == 7)
                        ctx.inc(last, "pe")
                        ctx.mark(f"{tag}.pfold{hh}_{tc}", "pe")
                    if E == "vector":
                        ctx.wait(eng, f"{tag}.pfold{hh}_{tc}")
                        if hh == 0 and tc == 0:
                            ctx.wait(eng, f"{tag}.rB")
                        if hh > 0 and tc == 0:
                            ctx.wait(eng, f"{tag}.qk{hh - 1}_1_15")
                        if tc > 0:
                            ctx.wait(eng, f"{tag}.vtr{hh}_{tc * 4 - 1}")
                        eng.tensor_tensor(out=qTt[:, sl], in0=poa[pb + 0][:, :], in1=rB[:, sl], op=ALU.mult)
                        eng.tensor_tensor(out=kTt[:, sl], in0=poa[pb + 1][:, :], in1=rB[:, sl], op=ALU.mult)
                        ctx.inc(eng.tensor_tensor(out=vTt[:, :], in0=poa[pb + 2][:, :], in1=rB[:, sl], op=ALU.mult), "ve")
                        ctx.mark(f"{tag}.pev{hh}_{tc}", "ve")
                    for b in range(4):
                        t = tc * 4 + b
                        if E == "tensor":
                            ctx.wait(eng, f"{tag}.pev{hh}_{tc}")
                            if t == 0 and hh > 0:
                                ctx.wait(eng, f"{tag}.avev{hh - 1}_1_15")
                            if t >= 2:
                                ctx.wait(eng, f"{tag}.vev{hh}_{t - 2}")
                            ctx.inc(eng.transpose(pob[6 + (b % 2)][:, 0:128], vTt[:, 128 * b:128 * (b + 1)], ident[:, :]), "pe")
                            ctx.mark(f"{tag}.vtr{hh}_{t}", "pe")
                        if E == "vector":
                            ctx.wait(eng, f"{tag}.vtr{hh}_{t}")
                            if t == 0 and hh > 0:
                                ctx.wait(eng, f"{tag}.avden{hh - 1}_1_15")
                            eng.tensor_copy(out=vaug[t][:, 0:64], in_=pob[6 + (b % 2)][:, 0:64])
                            ctx.inc(eng.tensor_copy(out=vaug[t][:, 65:129], in_=pob[6 + (b % 2)][:, 64:128]), "ve")
                            ctx.mark(f"{tag}.vev{hh}_{t}", "ve")
                if E == "vector" and hh < 7:
                    wcasts(eng, hh + 1)
                    xba0(eng, hh + 1)
                for kb in range(18):
                    qb = kb - 2
                    w = min(384, S - 128 * kb) if kb <= 15 else 0
                    for h in range(2):
                        hp = 64 * h
                        if kb <= 15:
                            if E == "tensor":
                                if kb >= 2:
                                    ctx.wait(eng, f"{tag}.exp{hh}_{h}_{kb - 2}")
                                elif hh > 0:
                                    ctx.wait(eng, f"{tag}.exp{hh - 1}_{h}_{14 + kb}")
                                if kb == 0 and h == 0:
                                    ctx.wait(eng, f"{tag}.pev{hh}_3")
                                    ctx.wait(eng, f"{tag}.vev{hh}_15")
                                ctx.inc(mm(eng, poa[h * 2 + kb % 2][:, 0:w],
                                           kTt[hp:hp + 64, 128 * kb:128 * (kb + 1)],
                                           qTt[hp:hp + 64, 128 * kb:128 * kb + w],
                                           True, True), "pe")
                                ctx.mark(f"{tag}.qk{hh}_{h}_{kb}", "pe")
                            if E == "scalar":
                                ctx.wait(eng, f"{tag}.qk{hh}_{h}_{kb}")
                                if kb >= 4:
                                    ctx.wait(eng, f"{tag}.avden{hh}_{h}_{kb - 2}")
                                elif hh > 0:
                                    ctx.wait(eng, f"{tag}.avden{hh - 1}_{h}_{min(15, kb + 14)}")
                                ctx.inc(eng.activation(out=ATt[h][kb % 4][:, 0:w], in_=poa[h * 2 + kb % 2][:, 0:w],
                                                       func=AF.Exp, scale=0.125), "se")
                                ctx.mark(f"{tag}.exp{hh}_{h}_{kb}", "se")
                            if E == "gpsimd":
                                ctx.wait(eng, f"{tag}.exp{hh}_{h}_{kb}")
                                eng.tensor_mul(out=ATt[h][kb % 4][:, 0:128], in0=ATt[h][kb % 4][:, 0:128], in1=maskt[:, 0:128])
                                if w == 384:
                                    eng.tensor_mul(out=ATt[h][kb % 4][:, 256:384], in0=ATt[h][kb % 4][:, 256:384], in1=maskt[:, 256:384])
                                ctx.inc(eng.engine_nop(), "ge")
                                ctx.mark(f"{tag}.msk{hh}_{h}_{kb}", "ge")
                        if 0 <= qb <= 15:
                            slot = 4 + h if qb % 2 == 0 else 6 + h
                            if E == "tensor":
                                ctx.wait(eng, f"{tag}.msk{hh}_{h}_{qb}")
                                if qb >= 2:
                                    ctx.wait(eng, f"{tag}.avev{hh}_{h}_{qb - 2}")
                                elif hh > 0:
                                    ctx.wait(eng, f"{tag}.avev{hh - 1}_{h}_{qb + 14}")
                                k0 = max(0, qb - 2)
                                for kc in range(k0, qb + 1):
                                    last = mm(eng, poa[slot][:, 0:65],
                                              ATt[h][kc % 4][:, 128 * (qb - kc):128 * (qb - kc + 1)],
                                              vaug[kc][:, 65 * h:65 * (h + 1)],
                                              kc == k0, kc == qb)
                                ctx.inc(last, "pe")
                                ctx.mark(f"{tag}.avden{hh}_{h}_{qb}", "pe")
                            if E == "vector":
                                ctx.wait(eng, f"{tag}.avden{hh}_{h}_{qb}")
                                if qb >= 1:
                                    ctx.wait(eng, f"{tag}.avev{hh}_{h}_{qb - 1}")
                                elif hh > 0:
                                    ctx.wait(eng, f"{tag}.avev{hh - 1}_{h}_15")
                                ctx.inc(eng.reciprocal(out=rcol[:, h:h + 1], in_=poa[slot][:, 64:65]), "ve")
                                ctx.mark(f"{tag}.rc{hh}_{h}_{qb}", "ve")
                            if E == "scalar":
                                ctx.wait(eng, f"{tag}.rc{hh}_{h}_{qb}")
                                hidx = 2 * hh + h
                                ctx.inc(eng.activation(out=att_o[qb][:, 64 * hidx:64 * (hidx + 1)],
                                                       in_=poa[slot][:, 0:64], func=AF.Copy,
                                                       scale=rcol[:, h:h + 1]), "se")
                                ctx.mark(f"{tag}.avev{hh}_{h}_{qb}", "se")
            wj[0] += 24
            for ec in range(8):
                j = wj[0] + ec
                if E == "sync":
                    ctx.wait(eng, f"w{j - 4}.free")
                    ctx.inc(eng.dma_start(out=wld[j % 4][:, :], in_=wob.ap()[ec]), f"sy{j % 4}", 16)
                    ctx.mark(f"w{j}.ld", f"sy{j % 4}")
                if E == "vector":
                    ctx.wait(eng, f"w{j}.ld")
                    if ec == 0:
                        ctx.wait(eng, f"{tag}.avden7_1_15")
                    ctx.inc(eng.tensor_copy(out=wobf[ec][:, :], in_=wld[j % 4][:, :]), "ve")
                    ctx.mark(f"w{j}.cast", "ve")
                    ctx.mark(f"w{j}.free", "ve")
            for tg in range(4):
                gsl = slice(512 * tg, 512 * (tg + 1))
                for bq in range(4):
                    qb = tg * 4 + bq
                    for ec in range(8):
                        t = qb * 8 + ec
                        tl = bq * 8 + ec
                        if E == "tensor":
                            if t == 0:
                                ctx.wait(eng, f"{tag}.avev7_1_15")
                            if t >= 2:
                                ctx.wait(eng, f"{tag}.wtev{t - 2}")
                            if tl < 2 and tg > 0:
                                ctx.wait(eng, f"{tag}.woev{tg - 1}_7")
                            ctx.inc(eng.transpose(pob[2 + t % 2][:, 0:128], att_o[qb][:, 128 * ec:128 * (ec + 1)], ident[:, :]), "pe")
                            ctx.mark(f"{tag}.wtr{t}", "pe")
                        if E == "vector":
                            ctx.wait(eng, f"{tag}.wtr{t}")
                            if t == 0:
                                ctx.wait(eng, f"{tag}.pev7_3")
                            ctx.inc(eng.tensor_copy(out=aoTs[ec][:, 128 * bq:128 * (bq + 1)], in_=pob[2 + t % 2][:, 0:128]), "ve")
                            ctx.mark(f"{tag}.wtev{t}", "ve")
                for ec in range(8):
                    if E == "tensor":
                        ctx.wait(eng, f"{tag}.wtev{(tg * 4 + 3) * 8 + 7}")
                        ctx.wait(eng, f"w{wj[0] + ec}.cast")
                        u = tg * 8 + ec
                        if u >= 2:
                            ctx.wait(eng, f"{tag}.woev{(u - 2) // 8}_{(u - 2) % 8}")
                        for c in range(8):
                            last = mm(eng, poa[ec % 2][:, :], wobf[ec][:, 128 * c:128 * (c + 1)], aoTs[c][:, :],
                                      c == 0, c == 7)
                        ctx.inc(last, "pe")
                        ctx.mark(f"{tag}.womm{tg}_{ec}", "pe")
                    if E == "vector":
                        ctx.wait(eng, f"{tag}.womm{tg}_{ec}")
                        ctx.inc(eng.tensor_tensor(out=xt[ec][:, gsl], in0=poa[ec % 2][:, :], in1=xt[ec][:, gsl], op=ALU.add), "ve")
                        ctx.mark(f"{tag}.woev{tg}_{ec}", "ve")
                if "ffn1" in PH:
                    ln_stats_tc(E, eng, "f0", tg, f"{tag}.woev{tg}_7")
            wj[0] += 8

        def ffn(E, eng, fi):
            tag = f"f{fi}"
            ln_stats_fin(E, eng, tag, want_mB=True, want_rB=True)
            if fi == 0:
                for o in range(8):
                    j = wj[0] + o
                    if E == "sync":
                        ctx.wait(eng, f"w{j - 4}.free")
                        ctx.inc(eng.dma_start(out=wld[j % 4][:, :], in_=wfb.ap()[o]), f"sy{j % 4}", 16)
                        ctx.mark(f"w{j}.ld", f"sy{j % 4}")
                    if E == "vector":
                        ctx.wait(eng, f"w{j}.ld")
                        if o == 0:
                            ctx.inc(eng.tensor_reduce(out=amax_col[:, :], in_=wld[j % 4][:, :], axis=AX.X, op=ALU.max, apply_absolute_value=True), "ve")
                            ctx.mark(f"w{j}.free", "ve")
                            ctx.wait(eng, f"w{j}.free")
                        else:
                            ctx.inc(eng.tensor_reduce(out=scol[:, 3:4], in_=wld[j % 4][:, :], axis=AX.X, op=ALU.max, apply_absolute_value=True), "ve")
                            ctx.mark(f"{tag}.wred{o}", "ve")
                            ctx.wait(eng, f"{tag}.wred{o}")
                            ctx.inc(eng.tensor_tensor(out=amax_col[:, :], in0=amax_col[:, :], in1=scol[:, 3:4], op=ALU.max), "ve")
                            ctx.mark(f"w{j}.free", "ve")
                            ctx.wait(eng, f"w{j}.free")
                if E == "tensor":
                    ctx.wait(eng, f"w{wj[0] + 7}.free")
                    ctx.inc(eng.transpose(poa[4][0:1, 0:128], amax_col[:, :], identf[:, :]), "pe")
                    ctx.mark(f"{tag}.wtr", "pe")
                if E == "vector":
                    ctx.wait(eng, f"{tag}.wtr")
                    ctx.inc(eng.tensor_reduce(out=smalls[0:1, 2:3], in_=poa[4][0:1, 0:128], axis=AX.X, op=ALU.max), "ve")
                    ctx.mark(f"{tag}.gw", "ve")
                    ctx.wait(eng, f"{tag}.gw")
                wj[0] += 8
            def axsub(c):
                return eng.tensor_tensor(out=xfs[c % 4][:, :], in0=xt[c][:, :], in1=mB[:, :], op=ALU.subtract)

            def axmul(c):
                return eng.tensor_tensor(out=xfs[c % 4][:, :], in0=xfs[c % 4][:, :], in1=rB[:, :], op=ALU.mult)

            def axred(c):
                return eng.tensor_reduce(out=zc[:, c:c + 1], in_=xfs[c % 4][:, :], axis=AX.X, op=ALU.max, apply_absolute_value=True)

            def fence(name, sem):
                ctx.mark(name, sem)
                ctx.wait(eng, name)

            if E == "gpsimd":
                ctx.wait(eng, f"{tag}.mB")
                ctx.wait(eng, f"{tag}.rB")
                if fi == 0:
                    ctx.wait(eng, f"{tag}.wtr")
                ctx.inc(axsub(0), "ge")
                ctx.inc(axsub(1), "ge")
                fence(f"{tag}.gsub", "ge")
                ctx.inc(axmul(0), "ge")
                ctx.mark(f"{tag}.axm0", "ge")
                ctx.inc(axmul(1), "ge")
                ctx.mark(f"{tag}.axm1", "ge")
            if E == "vector":
                ctx.wait(eng, f"{tag}.mB")
                ctx.wait(eng, f"{tag}.rB")
                if fi == 0:
                    ctx.wait(eng, f"{tag}.wtr")
                ctx.inc(axsub(2), "ve")
                ctx.inc(axsub(3), "ve")
                fence(f"{tag}.vs1", "ve")
                ctx.inc(axmul(2), "ve")
                ctx.inc(axmul(3), "ve")
                fence(f"{tag}.vm1", "ve")
                ctx.inc(axred(2), "ve")
                ctx.inc(axred(3), "ve")
                ctx.wait(eng, f"{tag}.axm0")
                ctx.inc(axred(0), "ve")
                ctx.wait(eng, f"{tag}.axm1")
                ctx.inc(axred(1), "ve")
                fence(f"{tag}.vr1", "ve")
                ctx.inc(axsub(4), "ve")
                ctx.inc(axsub(5), "ve")
                ctx.inc(axsub(6), "ve")
                ctx.inc(axsub(7), "ve")
                fence(f"{tag}.vs2", "ve")
                ctx.inc(axmul(4), "ve")
                ctx.inc(axmul(5), "ve")
                ctx.inc(axmul(6), "ve")
                ctx.inc(axmul(7), "ve")
                fence(f"{tag}.vm2", "ve")
                ctx.inc(axred(4), "ve")
                ctx.inc(axred(5), "ve")
                ctx.inc(axred(6), "ve")
                ctx.inc(axred(7), "ve")
                fence(f"{tag}.axpre", "ve")
                ctx.inc(eng.tensor_reduce(out=amax_col[:, :], in_=zc[:, 0:8], axis=AX.X, op=ALU.max), "ve")
                ctx.mark(f"{tag}.axall", "ve")
            if E == "tensor":
                ctx.wait(eng, f"{tag}.axall")
                ctx.inc(eng.transpose(poa[4][0:1, 0:128], amax_col[:, :], identf[:, :]), "pe")
                ctx.mark(f"{tag}.axtr", "pe")
            if E == "vector":
                ctx.wait(eng, f"{tag}.axtr")
                ctx.inc(eng.tensor_reduce(out=smalls[0:1, 0:1], in_=poa[4][0:1, 0:128], axis=AX.X, op=ALU.max), "ve")
                ctx.mark(f"{tag}.lred", "ve")
                ctx.wait(eng, f"{tag}.lred")
                ctx.wait(eng, "init_ve")
                ctx.inc(eng.tensor_copy(out=zc[0:1, 0:1], in_=smalls[0:1, 0:1]), "ve")
                ctx.mark(f"{tag}.lmax", "ve")
            ca, cb, cc2 = f"c{fi}a", f"c{fi}b", f"c{fi}c"
            if E == "gpsimd":
                ctx.wait(eng, f"{tag}.lmax")
                ctx.inc(eng.dma_start(out=cc_in[fi].ap(), in_=zc[:, :]), ca, 16)
                ctx.mark(f"{tag}.cci", ca)
                ctx.wait(eng, f"{tag}.cci")
                ctx.inc(eng.collective_compute("AllReduce", ALU.max, replica_groups=[list(range(8))],
                                               ins=[cc_in[fi].ap().opt()], outs=[cc_out[fi].ap().opt()]), cc2, 1)
                ctx.mark(f"{tag}.ccc", cc2)
                ctx.wait(eng, f"{tag}.ccc")
                ctx.inc(eng.dma_start(out=smalls[0:1, 1:2], in_=cc_out[fi].ap()[0:1, 0:1]), cb, 16)
                ctx.mark(f"{tag}.gmo", cb)
                ctx.wait(eng, f"{tag}.gmo")
                ctx.inc(eng.engine_nop(), "ge")
                ctx.mark(f"{tag}.gmax", "ge")
            for c in range(4):
                xfc = xfs[c]
                if E == "vector":
                    ctx.inc(eng.tensor_tensor(out=xfc[:, :], in0=xt[c][:, :], in1=mB[:, :], op=ALU.subtract), "ve")
                    ctx.mark(f"{tag}.xs{c}", "ve")
                    ctx.wait(eng, f"{tag}.xs{c}")
                    ctx.inc(eng.tensor_tensor(out=xfc[:, :], in0=xfc[:, :], in1=rB[:, :], op=ALU.mult), "ve")
                    ctx.mark(f"{tag}.xn{c}", "ve")
            if E == "vector":
                ctx.wait(eng, f"{tag}.gmax")
                if fi == 0:
                    ctx.wait(eng, f"{tag}.gw")
                ctx.inc(eng.reciprocal(out=smalls[0:1, 4:5], in_=smalls[0:1, 1:2]), "ve")
                ctx.inc(eng.tensor_tensor(out=smalls[0:1, 3:4], in0=smalls[0:1, 1:2], in1=smalls[0:1, 2:3], op=ALU.mult), "ve")
                ctx.inc(eng.reciprocal(out=smalls[0:1, 6:7], in_=smalls[0:1, 2:3]), "ve")
                ctx.mark(f"{tag}.sclA", "ve")
                ctx.wait(eng, f"{tag}.sclA")
                ctx.inc(eng.tensor_scalar(out=smalls[0:1, 5:6], in0=smalls[0:1, 4:5], scalar1=224.0, scalar2=None, op0=ALU.mult), "ve")
                ctx.inc(eng.tensor_scalar(out=smalls[0:1, 3:4], in0=smalls[0:1, 3:4], scalar1=GSC, scalar2=None, op0=ALU.mult), "ve")
                ctx.inc(eng.tensor_scalar(out=smalls[0:1, 7:8], in0=smalls[0:1, 6:7], scalar1=224.0, scalar2=None, op0=ALU.mult), "ve")
                ctx.mark(f"{tag}.scales", "ve")
            for si, (sidx, col) in enumerate([(5, 0), (3, 1), (7, 2)]):
                if E == "vector":
                    if si == 0:
                        ctx.wait(eng, f"{tag}.scales")
                    ctx.inc(eng.tensor_scalar(out=t_row[0:1, 128 * si:128 * (si + 1)], in0=onesr[:, :],
                                               scalar1=smalls[0:1, sidx:sidx + 1], scalar2=None, op0=ALU.mult), "ve")
                    ctx.mark(f"{tag}.bro{si}", "ve")
            for si, (sidx, col) in enumerate([(5, 0), (3, 1), (7, 2)]):
                if E == "tensor":
                    ctx.wait(eng, f"{tag}.bro{si}")
                    ctx.inc(eng.transpose(poa[4 + si][:, 0:1], t_row[0:1, 128 * si:128 * (si + 1)], identf[0:1, 0:1]), "pe")
                    ctx.mark(f"{tag}.brt{si}", "pe")
            for si, (sidx, col) in enumerate([(5, 0), (3, 1), (7, 2)]):
                if E == "vector":
                    ctx.wait(eng, f"{tag}.brt{si}")
                    ctx.inc(eng.tensor_copy(out=scol[:, col:col + 1], in_=poa[4 + si][:, 0:1]), "ve")
                    ctx.mark(f"{tag}.brc{si}", "ve")
            if E == "vector":
                ctx.mark(f"{tag}.sc1", "ve")
                ctx.mark(f"{tag}.sc2", "ve")
            for c in range(8):
                xfc = xfs[c % 4]
                if E == "vector" and c >= 4:
                    ctx.wait(eng, f"{tag}.xq{c - 4}")
                    ctx.inc(eng.tensor_tensor(out=xfc[:, :], in0=xt[c][:, :], in1=mB[:, :], op=ALU.subtract), "ve")
                    ctx.mark(f"{tag}.xs{c}", "ve")
                    ctx.wait(eng, f"{tag}.xs{c}")
                    ctx.inc(eng.tensor_tensor(out=xfc[:, :], in0=xfc[:, :], in1=rB[:, :], op=ALU.mult), "ve")
                    ctx.mark(f"{tag}.xn{c}", "ve")
                ceng = "scalar" if c % 2 == 0 else "vector"
                csem = "se" if c % 2 == 0 else "ve"
                if E == ceng:
                    ctx.wait(eng, f"{tag}.xn{c}")
                    if c in (0, 1):
                        ctx.wait(eng, f"{tag}.sc1")
                    if ceng == "scalar":
                        ctx.inc(eng.activation(out=xq8[c][:, :], in_=xfc[:, :], func=AF.Copy, scale=scol[:, 0:1]), csem)
                    else:
                        ctx.inc(eng.tensor_scalar(out=xq8[c][:, :], in0=xfc[:, :], scalar1=scol[:, 0:1], scalar2=None, op0=ALU.mult), csem)
                    ctx.mark(f"{tag}.xq{c}", csem)
            for o in range(8):
                j = wj[0] + o
                if E == "sync":
                    ctx.wait(eng, f"w{j - 4}.free")
                    ctx.inc(eng.dma_start(out=wld[j % 4][:, :], in_=wfb.ap()[o]), f"sy{j % 4}", 16)
                    ctx.mark(f"w{j}.ld", f"sy{j % 4}")
                ceng = "scalar" if o % 2 == 0 else "vector"
                csem = "se" if o % 2 == 0 else "ve"
                if E == ceng:
                    ctx.wait(eng, f"w{j}.ld")
                    if o in (0, 1):
                        ctx.wait(eng, f"{tag}.sc1")
                        if fi == 0:
                            ctx.wait(eng, "a1.woevall")
                    if ceng == "scalar":
                        ctx.inc(eng.activation(out=wq8[o][:, :], in_=wld[j % 4][:, :], func=AF.Copy, scale=scol[:, 2:3]), csem)
                    else:
                        ctx.inc(eng.tensor_scalar(out=wq8[o][:, :], in0=wld[j % 4][:, :], scalar1=scol[:, 2:3], scalar2=None, op0=ALU.mult), csem)
                    ctx.mark(f"w{j}.free", csem)
                    ctx.mark(f"{tag}.wq{o}", csem)
            wj[0] += 8
            for tc in range(4):
                for o in range(8):
                    u = tc * 8 + o
                    sl = slice(512 * tc, 512 * (tc + 1))
                    if E == "tensor":
                        if u == 0:
                            ctx.wait(eng, f"{tag}.wq7")
                            if fi == 1:
                                ctx.wait(eng, "b.avevall")
                            else:
                                ctx.wait(eng, "a1.woevall")
                        if u >= 2:
                            ctx.wait(eng, f"{tag}.fev{u - 2}")
                        for c in range(8):
                            last = mm(eng, poa[o % 2][:, :], wq8[o][:, 128 * c:128 * (c + 1)], xq8[c][:, sl],
                                      c == 0, c == 7)
                        ctx.inc(last, "pe")
                        ctx.mark(f"{tag}.fmm{u}", "pe")
                    if E == "scalar":
                        ctx.wait(eng, f"{tag}.fmm{u}")
                        if u == 0:
                            ctx.wait(eng, f"{tag}.sc2")
                        if u >= 2:
                            ctx.wait(eng, f"{tag}.fev{u - 2}")
                        ctx.inc(eng.activation(out=gtf[o % 2][:, :], in_=poa[o % 2][:, :], func=AF.Copy,
                                               scale=scol[:, 1:2]), "se")
                        ctx.mark(f"{tag}.fep{u}", "se")
                    if E == "vector":
                        ctx.wait(eng, f"{tag}.fep{u}")
                        ctx.inc(eng.scalar_tensor_tensor(out=xt[o][:, sl], in0=gtf[o % 2][:, :],
                                                         scalar=bias_sb[:, o:o + 1], in1=xt[o][:, sl],
                                                         op0=ALU.add, op1=ALU.add), "ve")
                        ctx.mark(f"{tag}.fev{u}", "ve")
                if fi == 0 and "attn_b" in PH:
                    ln_stats_tc(E, eng, "b", tc, f"{tag}.fev{tc * 8 + 7}")
            if E == "vector":
                ctx.mark(f"{tag}.fevall", "ve")

        def attn_b(E, eng):
            tag = "b"
            ln_stats_fin(E, eng, tag, want_mB=True, want_rB=False)
            for tt in range(4):
                ttsl = slice(512 * tt, 512 * (tt + 1))
                for c in range(8):
                    if E == "vector":
                        if c == 0:
                            if tt == 0:
                                ctx.wait(eng, f"{tag}.mB")
                            else:
                                ctx.wait(eng, f"{tag}.smmdone{tt - 1}")
                        ctx.inc(eng.tensor_tensor(out=xbt_b[c][:, :], in0=xt[c][:, ttsl], in1=mB[:, ttsl], op=ALU.subtract), "ve")
                        ctx.mark(f"{tag}.xb{tt}_{c}", "ve")
                stride = 64 if BCUT >= 4 else 32
                kbase = wj[0] + tt * stride
                vbase = kbase + 32
                def kcast(eng, n):
                    j = kbase + n
                    ctx.wait(eng, f"w{j}.ld")
                    if n >= 3:
                        ctx.wait(eng, f"{tag}.smm{tt}_{n - 3}")
                    elif tt > 0:
                        ctx.wait(eng, f"{tag}.avmm{tt - 1}_{29 + n}" if BCUT >= 4 else f"{tag}.smm{tt - 1}_{29 + n}")
                    ctx.inc(eng.tensor_copy(out=wbf[n % 3][:, :], in_=wld[j % 4][:, :]), "ve")
                    ctx.mark(f"w{j}.cast", "ve")
                    ctx.mark(f"w{j}.free", "ve")

                for n in range(32):
                    j = kbase + n
                    if E == "sync":
                        ctx.wait(eng, f"w{j - 4}.free")
                        ctx.inc(eng.dma_start(out=wld[j % 4][:, :], in_=knbd.ap()[n]), f"sy{j % 4}", 16)
                        ctx.mark(f"w{j}.ld", f"sy{j % 4}")
                    if E == "vector":
                        if n == 0:
                            kcast(eng, 0)
                            kcast(eng, 1)
                        if n + 2 < 32:
                            kcast(eng, n + 2)
                    if E == "tensor":
                        if n == 0:
                            for c in range(8):
                                ctx.wait(eng, f"{tag}.xb{tt}_{c}")
                            if tt == 0:
                                ctx.wait(eng, "f0.fevall")
                            elif BCUT >= 4:
                                ctx.wait(eng, f"{tag}.avevall{tt - 1}")
                        ctx.wait(eng, f"w{j}.cast")
                        if n >= 2:
                            ctx.wait(eng, f"{tag}.scp{tt}_{n - 2}")
                            if BCUT >= 2:
                                ctx.wait(eng, f"{tag}.ssq{tt}_{n - 2}")
                        for c in range(8):
                            last = mm(eng, poa[n % 2][:, :], wbf[n % 3][:, 128 * c:128 * (c + 1)], xbt_b[c][:, :],
                                      c == 0, c == 7)
                        ctx.inc(last, "pe")
                        ctx.mark(f"{tag}.smm{tt}_{n}", "pe")
                    if E == "scalar" and BCUT >= 2:
                        ctx.wait(eng, f"{tag}.scp{tt}_{n}")
                        if n >= 4 and B2SUB >= 1:
                            ctx.wait(eng, f"{tag}.n2mm{tt}_{n - 4}")
                        ctx.inc(eng.activation(out=sq4b[n % 4][:, :], in_=gt[n][:, :], func=AF.Square), "se")
                        ctx.mark(f"{tag}.ssq{tt}_{n}", "se")
                    if E == "vector":
                        ctx.wait(eng, f"{tag}.smm{tt}_{n}")
                        ctx.inc(eng.tensor_copy(out=gt[n][:, :], in_=poa[n % 2][:, :]), "ve")
                        ctx.mark(f"{tag}.scp{tt}_{n}", "ve")
                    if E == "tensor" and BCUT >= 2 and B2SUB >= 1:
                        for nd in ([n - 3] if n >= 3 else []) + ([29, 30, 31] if n == 31 else []):
                            ctx.wait(eng, f"{tag}.ssq{tt}_{nd}")
                            ctx.inc(mm(eng, poa[6][0:1, :], onescol[:, :], sq4b[nd % 4][:, :],
                                       nd == 0, nd == 31), "pe")
                            ctx.mark(f"{tag}.n2mm{tt}_{nd}", "pe")
                if E == "tensor":
                    ctx.mark(f"{tag}.smmdone{tt}", "pe")
                if E == "scalar" and BCUT >= 2 and B2SUB >= 2:
                    ctx.wait(eng, f"{tag}.n2mm{tt}_31")
                    ctx.inc(eng.activation(out=r_row[:, 0:512], in_=poa[6][0:1, :], func=AF.Ln), "se")
                    ctx.mark(f"{tag}.cl{tt}", "se")
                    ctx.wait(eng, f"{tag}.cl{tt}")
                    ctx.inc(eng.activation(out=exp_bf[:, :], in_=r_row[:, 0:512], func=AF.Exp, scale=-0.5), "se")
                    ctx.mark(f"{tag}.ce{tt}", "se")
                if E == "tensor" and BCUT >= 2 and B2SUB >= 3:
                    ctx.wait(eng, f"{tag}.ce{tt}")
                    if tt > 0 and BCUT >= 3:
                        ctx.wait(eng, f"{tag}.gmulall{tt - 1}")
                    ctx.inc(mm(eng, poa[5][:, :], maskt[0:1, 128:256], exp_bf[:, :], True, True), "pe")
                    ctx.mark(f"{tag}.cmm{tt}", "pe")
                if E == "vector" and B2SUB >= 3:
                    ctx.wait(eng, f"{tag}.cmm{tt}")
                    ctx.inc(eng.tensor_copy(out=cB[:, :], in_=poa[5][:, :]), "ve")
                    ctx.mark(f"{tag}.cB{tt}", "ve")
                for n in range(32):
                    if E == "vector" and BCUT >= 3:
                        if n == 0:
                            ctx.wait(eng, f"{tag}.cB{tt}")
                        if n >= 2:
                            ctx.wait(eng, f"{tag}.gel{tt}_{n - 2}")
                        ctx.inc(eng.tensor_tensor(out=gtf[n % 2][:, :], in0=gt[n][:, :], in1=cB[:, :], op=ALU.mult), "ve")
                        ctx.mark(f"{tag}.gmul{tt}_{n}", "ve")
                    if E == "scalar" and BCUT >= 3:
                        ctx.wait(eng, f"{tag}.gmul{tt}_{n}")
                        ctx.inc(eng.activation(out=gt[n][:, :], in_=gtf[n % 2][:, :], func=AF.Gelu, scale=64.0), "se")
                        ctx.mark(f"{tag}.gel{tt}_{n}", "se")
                    if BCUT >= 4:
                        j = vbase + n
                        if E == "sync":
                            ctx.wait(eng, f"w{j - 4}.free")
                            ctx.inc(eng.dma_start(out=wld[j % 4][:, :], in_=vnbd.ap()[n]), f"sy{j % 4}", 16)
                            ctx.mark(f"w{j}.ld", f"sy{j % 4}")
                        if E == "scalar":
                            ctx.wait(eng, f"w{j}.ld")
                            if n >= 2:
                                ctx.wait(eng, f"{tag}.avmm{tt}_{n - 2}")
                            else:
                                ctx.wait(eng, f"{tag}.smm{tt}_{30 + n}")
                            ctx.inc(eng.activation(out=wbf[n % 2][:, :], in_=wld[j % 4][:, :], func=AF.Copy), "se")
                            ctx.mark(f"w{j}.cast", "se")
                            ctx.mark(f"w{j}.free", "se")
                        if E == "tensor":
                            ctx.wait(eng, f"w{j}.cast")
                            ctx.wait(eng, f"{tag}.gel{tt}_{n}")
                            for dc in range(8):
                                last = mm(eng, poa[dc][:, :], wbf[n % 2][:, 128 * dc:128 * (dc + 1)], gt[n][:, :],
                                          n == 0, n == 31)
                            ctx.inc(last, "pe")
                            ctx.mark(f"{tag}.avmm{tt}_{n}", "pe")
                if E == "vector" and BCUT >= 3:
                    ctx.mark(f"{tag}.gmulall{tt}", "ve")
                if E == "tensor" and tt == 3 and BCUT >= 4:
                    ctx.mark(f"{tag}.avdone", "pe")
                for dc in range(8 if BCUT >= 4 else 0):
                    if E == "vector":
                        ctx.wait(eng, f"{tag}.avmm{tt}_31")
                        ctx.inc(eng.tensor_tensor(out=xt[dc][:, ttsl], in0=poa[dc][:, :], in1=xt[dc][:, ttsl], op=ALU.add), "ve")
                        ctx.mark(f"{tag}.avev{tt}_{dc}", "ve")
                if E == "vector" and BCUT >= 4:
                    ctx.mark(f"{tag}.avevall{tt}", "ve")
                if BCUT >= 4 and "ffn2" in PH:
                    ln_stats_tc(E, eng, "f1", tt, f"{tag}.avev{tt}_7")
            wj[0] += 256 if BCUT >= 4 else 128
            if E == "vector" and BCUT >= 4:
                ctx.mark(f"{tag}.avevall", "ve")

        def store(E, eng, final_mark):
            if E == "sync":
                ctx.wait(eng, final_mark)
                for c in range(8):
                    ctx.inc(eng.dma_start(out=out_t.ap()[128 * c:128 * (c + 1), :], in_=xt[c][:, :]), "sy", 16)
                if DEBUG:
                    ctx.inc(eng.dma_start(out=dbg1.ap()[0:1, 0:16], in_=smalls[:, :]), "sy", 16)
                    ctx.inc(eng.dma_start(out=dbg1.ap()[0:1, 16:16 + S], in_=m_row), "sy", 16)
                    ctx.inc(eng.dma_start(out=dbg1.ap()[0:1, 16 + S:16 + 2 * S], in_=r_row), "sy", 16)
                    ctx.inc(eng.dma_start(out=dbg1.ap()[0:1, 16 + 2 * S:16 + 3 * S], in_=t_row), "sy", 16)
                    ctx.inc(eng.dma_start(out=dbgs.ap(), in_=scol[:, :]), "sy", 16)
                    ctx.inc(eng.dma_start(out=dbga.ap(), in_=amax_col[:, :]), "sy", 16)
                    ctx.inc(eng.dma_start(out=dbgm.ap(), in_=mB[:, 0:8]), "sy", 16)
                    ctx.inc(eng.dma_start(out=dbgr.ap(), in_=rB[:, 0:8]), "sy", 16)
                ctx.mark("st_done", "sy")
                ctx.wait(eng, "st_done")
            if E == "gpsimd":
                ctx.wait(eng, "st_done")

        def program(E, eng):
            wj[0] = 0
            preload(E, eng)
            final = None
            if "attn_a" in PH:
                attn_a(E, eng)
                if E == "vector":
                    ctx.mark("a1.woevall", "ve")
                if E == "tensor":
                    ctx.mark("a1.woevall_pe", "pe")
                final = "a1.woevall"
            if "ffn1" in PH:
                ffn(E, eng, 0)
                if E == "tensor":
                    ctx.mark("f0.fevall_pe", "pe")
                final = "f0.fevall"
            if "attn_b" in PH:
                attn_b(E, eng)
                final = "b.avevall" if BCUT >= 4 else "f0.fevall"
            if "ffn2" in PH:
                ffn(E, eng, 1)
                final = "f1.fevall"
            store(E, eng, final)

        ctx.emit = False
        for E in ["sync", "tensor", "vector", "scalar", "gpsimd"]:
            program(E, _DummyEngine())
        counted = dict(ctx.cnt)

        ctx.emit = True
        with nc.Block() as block:
            for ename, deco in (("sync", block.sync), ("tensor", block.tensor),
                                ("vector", block.vector), ("scalar", block.scalar),
                                ("gpsimd", block.gpsimd)):
                def run_engine(eng, _ename=ename):
                    ctx.cnt = {}
                    program(_ename, eng)
                deco(run_engine)
        ctx.cnt = counted
    return nc



def _hhblock(wT):
    blk = wT.reshape(8, 128, 8, 128).transpose(2, 1, 0, 3).reshape(8, 128, 1024)
    return np.ascontiguousarray(blk)


def _consts():
    i = np.arange(128)
    causal = (i[:, None] <= i[None, :]).astype(ml_dtypes.bfloat16)
    anti = (i[:, None] >= i[None, :] + 1).astype(ml_dtypes.bfloat16)
    mask = np.concatenate([causal, np.ones((128, 128), ml_dtypes.bfloat16), anti], axis=1)
    ident = np.eye(128, dtype=ml_dtypes.bfloat16)
    ones = np.ones((128, 1), ml_dtypes.bfloat16)
    return mask, ident, ones


def make_inputs(L0_attn_a_x, L0_attn_a_Wq, L0_attn_a_Wk, L0_attn_a_Wv, L0_attn_a_Wo,
                L0_attn_b_key_param_tokens, L0_attn_b_value_param_tokens,
                L0_ffn_weight, L0_ffn_bias):
    x = np.asarray(L0_attn_a_x, dtype=np.float32)
    wq = _hhblock(np.ascontiguousarray(np.asarray(L0_attn_a_Wq, np.float32).T))
    wk = _hhblock(np.ascontiguousarray(np.asarray(L0_attn_a_Wk, np.float32).T))
    wv = _hhblock(np.ascontiguousarray(np.asarray(L0_attn_a_Wv, np.float32).T))
    wo = _hhblock(np.ascontiguousarray(np.asarray(L0_attn_a_Wo, np.float32).T))
    wf = _hhblock(np.ascontiguousarray(np.asarray(L0_ffn_weight, np.float32).T))
    kT = np.ascontiguousarray(np.asarray(L0_attn_b_key_param_tokens, np.float32).T)
    knb = np.ascontiguousarray(kT.reshape(8, 128, 32, 128).transpose(2, 1, 0, 3).reshape(32, 128, 1024))
    vnb = np.ascontiguousarray(np.asarray(L0_attn_b_value_param_tokens, np.float32).reshape(32, 128, 1024))
    bias = np.ascontiguousarray(np.asarray(L0_ffn_bias, np.float32).reshape(8, 128).T)
    mask, ident, ones = _consts()
    identf = np.eye(128, dtype=np.float32)
    onesr = np.ones((1, 128), np.float32)
    in_maps = []
    for b in range(8):
        in_maps.append({
            "x_in": np.ascontiguousarray(x[b].T),
            "wqb": wq, "wkb": wk, "wvb": wv, "wob": wo, "wfb": wf,
            "knbd": knb, "vnbd": vnb, "bias_in": bias,
            "mask_in": mask, "id_in": ident, "ones_in": ones,
            "idf_in": identf, "onesr_in": onesr,
        })
    return in_maps


def kernel(**inputs):
    in_maps = make_inputs(**inputs)
    nc = build_nc(4)
    res = run_bass_kernel_spmd(nc, in_maps, core_ids=list(range(8)))
    out = np.stack([np.asarray(res.results[b]["out"]).T for b in range(8)])
    return np.ascontiguousarray(out)
